# version 07
# baseline (speedup 1.0000x reference)
import numpy as np
import concourse.bass as bass
import concourse.mybir as mybir
from concourse.bass_utils import run_bass_kernel_spmd

F32 = mybir.dt.float32
BF16 = mybir.dt.bfloat16
I16 = mybir.dt.int16
AF = mybir.ActivationFunctionType
ALU = mybir.AluOpType

B, T, D = 2, 4096, 2048
HK, HV, AK, UV = 16, 32, 128, 128
KEY_DIM = HK * AK
VALUE_DIM = HV * UV
CONV_DIM = 2 * KEY_DIM + VALUE_DIM
KCONV = 4
CHUNK = 64
EPS_L2 = 1e-6
EPS_LN = 1e-5
EPS_RMS = 1e-6
NE, NI = 16, 1408
NSH = 2816

T_PRE = 3072
T_OWN = 1024
N_CORES = 8

ENGINES = ("pe", "act", "dve", "sp", "pool")


class Em:

    def __init__(self, nc):
        self.nc = nc
        self.q = {e: [] for e in ENGINES}
        self.cnt = {e: 0 for e in ENGINES}
        self.dma_cnt = {}
        self.last_w = {}
        self.readers = {}
        self.hwm = {e: {} for e in ENGINES}
        self.barrier_req = {}
        self.n_ops = 0

    def barrier(self):
        snap = {("sem_" + e): self.cnt[e] for e in ENGINES if self.cnt[e]}
        snap.update(dict(self.dma_cnt))
        self.barrier_req = snap
        self.last_w = {}
        self.readers = {}

    def op(self, eng, fn, reads=(), writes=(), dma=None):
        need = dict(self.barrier_req)
        for k in reads:
            w = self.last_w.get(k)
            if w:
                need[w[0]] = max(need.get(w[0], 0), w[1])
        for k in writes:
            w = self.last_w.get(k)
            if w:
                need[w[0]] = max(need.get(w[0], 0), w[1])
            for s, v in self.readers.get(k, {}).items():
                need[s] = max(need.get(s, 0), v)
        hw = self.hwm[eng]
        waits = []
        for s, v in sorted(need.items()):
            if hw.get(s, -1) < v:
                waits.append((s, v))
                hw[s] = v
        if dma:
            sem_name = "dma_" + dma
            need.pop(sem_name, None)
            self.dma_cnt[sem_name] = self.dma_cnt.get(sem_name, 0) + 16
            val = self.dma_cnt[sem_name]
        else:
            self.cnt[eng] += 1
            sem_name, val = "sem_" + eng, self.cnt[eng]
        self.q[eng].append((waits, fn, sem_name))
        for k in writes:
            self.last_w[k] = (sem_name, val)
            self.readers[k] = {}
        for k in reads:
            self.readers.setdefault(k, {})
            r = self.readers[k]
            r[sem_name] = max(r.get(sem_name, 0), val)
        self.n_ops += 1

    def build(self, sems):
        def run(engine_obj, name):
            for waits, fn, sem_name in self.q[name]:
                for s, v in waits:
                    engine_obj.wait_ge(sems[s], v)
                last = fn(engine_obj)
                inc = 16 if sem_name.startswith("dma_") else 1
                last.then_inc(sems[sem_name], inc)
            if name == "pool":
                for s2, v2 in sorted(self.dma_cnt.items()):
                    engine_obj.wait_ge(sems[s2], v2)
                for e2 in ENGINES:
                    if self.cnt[e2] and e2 != "pool":
                        engine_obj.wait_ge(sems["sem_" + e2], self.cnt[e2])

        with self.nc.Block() as block:
            @block.tensor
            def _(e):
                run(e, "pe")

            @block.scalar
            def _(e):
                run(e, "act")

            @block.vector
            def _(e):
                run(e, "dve")

            @block.sync
            def _(e):
                run(e, "sp")

            @block.gpsimd
            def _(e):
                run(e, "pool")


def ceil_div(a, b):
    return (a + b - 1) // b


def build_program(t_pre=None, t_own=None, debug_out=()):
    t_pre = T_PRE if t_pre is None else t_pre
    t_own = T_OWN if t_own is None else t_own
    t_all = t_pre + t_own
    assert t_all % 128 == 0 and t_own % 128 == 0 and t_pre % CHUNK == 0
    NT = t_all // 128
    DCH = D // 128

    nc = bass.Bass("TRN2", target_bir_lowering=False)
    em = Em(nc)
    keep = []
    dr = {}

    def din(name, shape, dtype=F32):
        dr[name] = nc.dram_tensor(name, list(shape), dtype, kind="ExternalInput").ap()
        return dr[name]

    def dout_(name, shape, dtype=F32):
        dr[name] = nc.dram_tensor(name, list(shape), dtype, kind="ExternalOutput").ap()
        return dr[name]

    def dscratch(name, shape, dtype=F32):
        dr[name] = nc.dram_tensor(name, list(shape), dtype, kind="Internal").ap()
        return dr[name]

    hidden = din("hidden", (t_all, D))
    w_qkv = din("w_qkv", (D, CONV_DIM))
    w_z = din("w_z", (D, VALUE_DIM))
    w_b = din("w_b", (D, HV))
    w_a = din("w_a", (D, HV))
    conv_w = din("conv_w", (CONV_DIM, KCONV))
    dt_bias = din("dt_bias", (HV,))
    a_log = din("a_log", (HV,))
    norm_w = din("norm_w", (UV,))
    w_out = din("w_out", (VALUE_DIM, D))
    gate_wt = din("gate_wt", (D, NE))
    w1t = din("w1t", (NE, D, NI))
    w2t = din("w2t", (NE, NI, D))
    w3t = din("w3t", (NE, D, NI))
    ws1t = din("ws1t", (D, NSH))
    ws2t = din("ws2t", (NSH, D))
    ws3t = din("ws3t", (D, NSH))

    out = dout_("out", (t_own, D))
    xn_hbm = dscratch("xn_hbm", (NT, 128, D))
    attn_part = dscratch("attn_part", (8, t_own, D))

    dbg = {}
    for name, shape in debug_out:
        dbg[name] = dout_("dbg_" + name, shape)

    sb = {}

    def sbuf(name, shape, dtype=F32):
        h = nc.sbuf_tensor(name, list(shape), dtype)
        keep.append(h)
        t = h.__enter__()
        sb[name] = t
        return t

    ps = []
    for i in range(8):
        h = nc.psum_tensor(f"psum{i}", [128, 512], F32)
        keep.append(h)
        ps.append(h.__enter__())
    ps_rot = [0]

    def next_bank():
        ps_rot[0] = (ps_rot[0] + 1) % 8
        return ps[ps_rot[0]], f"psum{ps_rot[0]}"

    ones_col = sbuf("ones_col", (128, 1))
    ones128 = sbuf("ones128", (128, 128))
    ident128 = sbuf("ident128", (128, 128))

    cscal = {}

    def const_col(val):
        if val not in cscal:
            idx = len(cscal)
            t = sbuf(f"cconst{idx}", (128, 1))
            em.op("pool", lambda e, a=t, v=val: e.memset(a[:], v),
                  writes=[f"cconst{idx}"])
            cscal[val] = (t, f"cconst{idx}")
        t, k = cscal[val]
        return t[:, 0:1], k

    em.op("pool", lambda e: e.memset(ones_col[:], 1.0), writes=["ones_col"])
    em.op("pool", lambda e: e.memset(ones128[:], 1.0), writes=["ones128"])
    em.op("pool", lambda e: e.affine_select(
        ident128[:], ones128[:], [[-1, 128]], ALU.is_equal, 0.0,
        channel_multiplier=1), reads=["ones128"], writes=["ident128"])

    wkv_sb = sbuf("wkv_sb", (128, 16384))
    wq_sb = sbuf("wq_sb", (128, 2048))
    xn_g = sbuf("xn_g", (128, 8 * 512))
    kvf = sbuf("kvf", (128, 6 * 512))

    x_tm = [wkv_sb[0:128, i * D:(i + 1) * D] for i in range(2)]
    ln_scr = wkv_sb[0:128, 2 * D:3 * D]
    ln_s = [sbuf(f"ln_s{i}", (128, 4)) for i in range(2)]
    xn_st = [wkv_sb[0:128, (3 + i) * D:(4 + i) * D] for i in range(2)]

    for it in range(NT):
        bi = it % 2
        xt, kx = x_tm[bi], f"x_tm{bi}"
        st, kst = xn_st[bi], f"xn_st{bi}"
        lt, kl = ln_s[bi], f"ln_s{bi}"
        em.op("sp", lambda e, a=xt, i0=it: e.dma_start(
            out=a[:], in_=hidden[i0 * 128:(i0 + 1) * 128, :]),
            writes=[kx], dma=f"xld{bi}")
        em.op("act", lambda e, a=xt, l=lt: e.activation(
            out=ln_scr[:], in_=a[:], func=AF.Copy, accum_out=l[:, 0:1]),
            reads=[kx], writes=["ln_scr", kl])
        em.op("act", lambda e, a=xt, l=lt: e.activation(
            out=ln_scr[:], in_=a[:], func=AF.Square, accum_out=l[:, 1:2]),
            reads=[kx], writes=["ln_scr", kl])
        em.op("dve", lambda e, l=lt: e.tensor_scalar_mul(
            l[:, 2:3], l[:, 0:1], 1.0 / D), reads=[kl], writes=[kl])
        em.op("dve", lambda e, l=lt: e.scalar_tensor_tensor(
            l[:, 3:4], l[:, 2:3], -1.0, l[:, 2:3], ALU.mult, ALU.mult),
            reads=[kl], writes=[kl])
        em.op("dve", lambda e, l=lt: e.scalar_tensor_tensor(
            l[:, 3:4], l[:, 1:2], 1.0 / D, l[:, 3:4], ALU.mult, ALU.add),
            reads=[kl], writes=[kl])
        eps_ap, eps_k = const_col(EPS_LN)
        em.op("act", lambda e, l=lt, b_=eps_ap: e.activation(
            out=l[:, 3:4], in_=l[:, 3:4], func=AF.Sqrt, bias=b_),
            reads=[kl, eps_k], writes=[kl])
        em.op("dve", lambda e, l=lt: e.reciprocal(l[:, 3:4], l[:, 3:4]),
              reads=[kl], writes=[kl])
        em.op("dve", lambda e, a=xt, l=lt: e.tensor_scalar(
            a[:], a[:], l[:, 2:3], l[:, 3:4], ALU.subtract, ALU.mult),
            reads=[kx, kl], writes=[kx])
        for dc in range(DCH):
            bank, bk = next_bank()
            em.op("pe", lambda e, a=xt, d0=dc, b=bank: e.transpose(
                b[:, 0:128], a[:, d0 * 128:(d0 + 1) * 128], ident128[:]),
                reads=[kx, "ident128"], writes=[bk])
            em.op("dve", lambda e, d0=dc, b=bank, s=st: e.tensor_copy(
                s[:, d0 * 128:(d0 + 1) * 128], b[:, 0:128]),
                reads=[bk], writes=[kst])
        em.op("sp", lambda e, s=st, i0=it: e.dma_start(
            out=xn_hbm[i0], in_=s[:]),
            reads=[kst], writes=["xn_hbm"], dma="xnsp")

    em.barrier()
    TG = 512
    assert t_pre % TG == 0 and t_own % TG == 0
    NTG = t_all // TG
    NCH = t_all // CHUNK
    CPG = TG // CHUNK
    TPG = TG // 128
    NG = 8
    LN2_AK = float(np.log(AK) * 0.5)

    zeros64 = sbuf("zeros64", (64, 64))
    em.op("pool", lambda e: e.memset(zeros64[:], 0.0), writes=["zeros64"])
    wba = wq_sb[0:128, 0:DCH * 64]
    for dc in range(DCH):
        em.op("sp", lambda e, d0=dc: e.dma_start(
            out=wba[:, d0 * 64:d0 * 64 + 32],
            in_=w_b[d0 * 128:(d0 + 1) * 128, :]), writes=["wba"], dma="wba")
        em.op("sp", lambda e, d0=dc: e.dma_start(
            out=wba[:, d0 * 64 + 32:d0 * 64 + 64],
            in_=w_a[d0 * 128:(d0 + 1) * 128, :]), writes=["wba"], dma="wba")
    dtb = sbuf("dtb", (32, 4))
    em.op("sp", lambda e: e.dma_start(out=dtb[0:32, 0:1], in_=dt_bias),
          writes=["dtb"], dma="dtb")
    em.op("sp", lambda e: e.dma_start(out=dtb[0:32, 1:2], in_=a_log),
          writes=["dtb"], dma="dtb")
    em.op("act", lambda e: e.activation(out=dtb[0:32, 2:3], in_=dtb[0:32, 1:2],
          func=AF.Exp), reads=["dtb"], writes=["dtb"])
    em.op("dve", lambda e: e.tensor_scalar_mul(dtb[0:32, 2:3], dtb[0:32, 2:3],
          -1.0), reads=["dtb"], writes=["dtb"])

    conv_sb = sbuf("conv_sb", (128, 64 * 4))
    for cc in range(64):
        em.op("sp", lambda e, c0=cc: e.dma_start(
            out=conv_sb[:, c0 * 4:(c0 + 1) * 4],
            in_=conv_w[c0 * 128:(c0 + 1) * 128, :]), writes=["conv_sb"], dma="conv")

    nw_row = sbuf("nw_row", (1, UV))
    nw_tile = sbuf("nw_tile", (128, UV))
    em.op("sp", lambda e: e.dma_start(out=nw_row[0:1, :], in_=norm_w),
          writes=["nw_row"], dma="nwr")
    bank, bk = next_bank()
    em.op("pe", lambda e, b=bank: e.matmul(b[0:128, 0:UV], ones128[0:1, :],
          nw_row[0:1, :], start=True, stop=True),
          reads=["nw_row", "ones128"], writes=[bk])
    em.op("dve", lambda e, b=bank: e.tensor_copy(nw_tile[:], b[0:128, 0:UV]),
          reads=[bk], writes=["nw_tile"])

    gj_hm = sbuf("gj_hm", (32, t_all))
    p1_hm = sbuf("p1_hm", (32, t_all))
    lnsk4 = sbuf("lnsk4", (4, 512))
    lnsq4 = sbuf("lnsq4", (4, 512))
    egl_hm = sbuf("egl_hm", (32, NCH))
    gj_tm = sbuf("gj_tm", (64, 32 * 8))
    p1_tm = sbuf("p1_tm", (64, 32 * 8))
    p3_tm = sbuf("p3_tm", (64, 32 * 8))
    b_tm = sbuf("b_tm", (64, 32 * 8))
    lnsk_tm = sbuf("lnsk_tm", (64, 4 * 8))
    p3s_hm = sbuf("p3s_hm", (32, 512))
    em.op("pool", lambda e: e.memset(lnsk4[:], 0.0), writes=["lnsk4"])
    em.op("pool", lambda e: e.memset(lnsq4[:], 0.0), writes=["lnsq4"])
    em.op("pool", lambda e: e.memset(lnsk_tm[:], 0.0), writes=["lnsk_tm"])

    fam = kvf[0:32, 0:6 * TG // 1][0:32, 0:3072]
    fam = kvf[0:32, 0:3072]
    one_ap, one_k = const_col(1.0)
    epsl2_ap, epsl2_k = const_col(EPS_L2)
    epsrms_ap, epsrms_k = const_col(EPS_RMS)
    negln_ap, negln_k = const_col(-LN2_AK)

    def load_xn_half(r, tg0, n_tiles=TPG, key="xn_g"):
        for dcl in range(8):
            dc = 8 * r + dcl
            for ti in range(n_tiles):
                tile = tg0 // 128 + ti
                em.op("sp", lambda e, d0=dc, dl=dcl, t0=tile, i0=ti: e.dma_start(
                    out=xn_g[:, dl * TG + i0 * 128:dl * TG + i0 * 128 + 128],
                    in_=xn_hbm[t0, :, d0 * 128:(d0 + 1) * 128]),
                    reads=["xn_hbm"], writes=[key], dma="xng")

    for tgi in range(NTG):
        tg0 = tgi * TG
        pb, kpb = next_bank()
        pa, kpa = next_bank()
        for r in range(2):
            load_xn_half(r, tg0)
            for dcl in range(8):
                dc = 8 * r + dcl
                em.op("pe", lambda e, b=pb, d0=dc, dl=dcl, st=(r == 0 and dcl == 0),
                      sp=(r == 1 and dcl == 7): e.matmul(
                      b[0:32, 0:TG], wba[:, d0 * 64:d0 * 64 + 32],
                      xn_g[:, dl * TG:(dl + 1) * TG], start=st, stop=sp),
                      reads=["wba", "xn_g"], writes=[kpb])
                em.op("pe", lambda e, b=pa, d0=dc, dl=dcl, st=(r == 0 and dcl == 0),
                      sp=(r == 1 and dcl == 7): e.matmul(
                      b[0:32, 0:TG], wba[:, d0 * 64 + 32:d0 * 64 + 64],
                      xn_g[:, dl * TG:(dl + 1) * TG], start=st, stop=sp),
                      reads=["wba", "xn_g"], writes=[kpa])
        b_s, a_s = fam[0:32, 0:TG], fam[0:32, TG:2 * TG]
        bet = fam[0:32, 2 * TG:3 * TG]
        u_s = fam[0:32, 3 * TG:4 * TG]
        l_s = fam[0:32, 4 * TG:5 * TG]
        g_s = fam[0:32, 5 * TG:6 * TG]
        em.op("dve", lambda e, b=pb: e.tensor_copy(b_s, b[0:32, 0:TG]),
              reads=[kpb], writes=["famb"])
        em.op("dve", lambda e, b=pa: e.tensor_copy(a_s, b[0:32, 0:TG]),
              reads=[kpa], writes=["fama"])
        em.op("act", lambda e: e.activation(out=bet, in_=b_s, func=AF.Sigmoid),
              reads=["famb"], writes=["fambet"])
        em.op("act", lambda e: e.activation(out=b_s, in_=bet, func=AF.Ln),
              reads=["fambet"], writes=["famb"])
        em.op("dve", lambda e: e.tensor_single_scalar(u_s, a_s,
              dtb[0:32, 0:1], ALU.add), reads=["fama", "dtb"], writes=["famu"])
        em.op("act", lambda e: e.activation(out=a_s, in_=u_s, func=AF.Abs),
              reads=["famu"], writes=["fama"])
        em.op("act", lambda e: e.activation(out=l_s, in_=a_s, func=AF.Exp,
              scale=-1.0), reads=["fama"], writes=["faml"])
        em.op("act", lambda e, b_=one_ap: e.activation(out=l_s, in_=l_s,
              func=AF.Ln, bias=b_[0:32, :]), reads=["faml", one_k], writes=["faml"])
        em.op("dve", lambda e: e.tensor_scalar_max(u_s, u_s, 0.0),
              reads=["famu"], writes=["famu"])
        em.op("dve", lambda e: e.tensor_add(u_s, u_s, l_s),
              reads=["famu", "faml"], writes=["famu"])
        em.op("dve", lambda e: e.tensor_single_scalar(g_s, u_s, dtb[0:32, 2:3],
              ALU.mult), reads=["famu", "dtb"], writes=["famg"])
        for c in range(CPG):
            em.op("dve", lambda e, c0=c, t0=tg0: e.tensor_tensor_scan(
                gj_hm[:, t0 + c0 * 64:t0 + (c0 + 1) * 64],
                fam[0:32, 5 * TG + c0 * 64:5 * TG + (c0 + 1) * 64],
                zeros64[0:32, :], 0.0, ALU.add, ALU.add),
                reads=["famg", "zeros64"], writes=["gj_hm"])
        em.op("dve", lambda e, t0=tg0: e.tensor_add(p1_hm[:, t0:t0 + TG],
              gj_hm[:, t0:t0 + TG], b_s), reads=["gj_hm", "famb"],
              writes=["p1_hm"])
        em.op("act", lambda e, t0=tg0: e.activation(
            out=egl_hm[:, t0 // 64:t0 // 64 + CPG],
            in_=gj_hm[:, t0 + 63:t0 + TG:64], func=AF.Exp),
            reads=["gj_hm"], writes=["egl_hm"])

    em.barrier()
    iden64 = sbuf("iden64", (64, 64))
    masks = sbuf("masks", (64, 6 * 64))
    negones = sbuf("negones", (128, 128))
    em.op("pool", lambda e: e.memset(negones[:], -1.0), writes=["negones"])
    em.op("pool", lambda e: e.affine_select(iden64[:], ones128[0:64, 0:64],
          [[-1, 64]], ALU.is_equal, 0.0, channel_multiplier=1),
          reads=["ones128"], writes=["iden64"])
    mask_defs = [
        (0, ALU.is_gt, -1, 1, False),
        (1, ALU.is_gt, -1, 1, True),
        (2, ALU.is_gt, 1, -1, False),
        (3, ALU.is_gt, 1, -1, True),
        (4, ALU.is_ge, 1, -1, False),
        (5, ALU.is_ge, 1, -1, True),
    ]
    for slot, cmp, stp, mlt, neg in mask_defs:
        src_t = negones if neg else ones128
        src_k = "negones" if neg else "ones128"
        em.op("pool", lambda e, s0=slot, c0=cmp, sa=src_t, st0=stp, m0=mlt:
            e.affine_select(
            masks[:, s0 * 64:(s0 + 1) * 64], sa[0:64, 0:64], [[st0, 64]], c0,
            0.0, channel_multiplier=m0), reads=[src_k], writes=["masks"])
    m01_B, mneg_B = masks[:, 0:64], masks[:, 64:128]
    m01_Bt, mneg_Bt = masks[:, 128:192], masks[:, 192:256]
    m01_iT, mneg_iT = masks[:, 256:320], masks[:, 320:384]

    sel_sb = sbuf("sel_sb", (32, 4 * 128))
    selg_sb = sbuf("selg_sb", (4, 4 * 128))
    selgn_sb = sbuf("selgn_sb", (4, 4 * 128))
    kmask = sbuf("kmask", (128, 2 * 32))
    ext = sbuf("ext", (128, 8 * (TG + 4)))
    EXTW = TG + 4
    qf = sbuf("qf", (128, 2 * TG))
    z_sb = sbuf("z_sb", (128, 4 * 512))
    k2s = sbuf("k2s", (128, TG))
    wz_buf = [sbuf(f"wz_buf{i}", (128, 512)) for i in range(2)]
    wout_buf = sbuf("wout_buf", (128, 1024))
    S_sb = sbuf("S_sb", (128, 4 * 128))
    eglrep = sbuf("eglrep", (128, max(4 * NCH, 96)))
    dtile = sbuf("dtile", (64, 64))
    Bp = [sbuf(f"Bp{i}", (64, 64)) for i in range(2)]
    Btp = [sbuf(f"Btp{i}", (64, 64)) for i in range(2)]
    Rt = sbuf("Rt", (64, 64))
    intraT = sbuf("intraT", (64, 64))
    kts = sbuf("kts", (64, 128))
    u1 = sbuf("u1", (64, 128))
    Pt = sbuf("Pt", (64, 128))
    vnew = sbuf("vnew", (64, 128))
    qs = sbuf("qs", (128, 64))
    qsc = sbuf("qsc", (128, 64))
    e1b = sbuf("e1b", (64, 4))
    e5b = sbuf("e5b", (64, 4))
    xg = sbuf("xg", (128, 128))
    rmsc = sbuf("rmsc", (128, 2))
    normed_sb = sbuf("normed_sb", (128, 512))
    normT = sbuf("normT", (128, 512))
    part_sb = sbuf("part_sb", (128, 512))
    zsl = sbuf("zsl", (128, 128))

    def emit_group(g):
        sfx = f"_g{g}"
        kv_cols = ([KEY_DIM + (2 * g + j) * 128 for j in range(2)] +
                   [2 * KEY_DIM + (4 * g + h) * 128 for h in range(4)])
        q_cols = [(2 * g + j) * 128 for j in range(2)]
        for dc in range(DCH):
            for ct, col in enumerate(kv_cols):
                em.op("sp", lambda e, d0=dc, c0=col, t0=ct: e.dma_start(
                    out=wkv_sb[:, (d0 * 6 + t0) * 128:(d0 * 6 + t0 + 1) * 128],
                    in_=w_qkv[d0 * 128:(d0 + 1) * 128, c0:c0 + 128]),
                    writes=["wkv_sb"], dma="wkv")
        for hl in range(4):
            em.op("pool", lambda e, h0=hl: e.affine_select(
                sel_sb[:, h0 * 128:(h0 + 1) * 128], ones128[0:32, :],
                [[0, 128]], ALU.is_equal, 0.0, base=-(4 * g + h0),
                channel_multiplier=1), reads=["ones128"], writes=["sel_sb"])
            em.op("pool", lambda e, h0=hl: e.affine_select(
                selgn_sb[0:4, h0 * 128:(h0 + 1) * 128], negones[0:4, :],
                [[0, 128]], ALU.is_equal, 0.0, base=-h0,
                channel_multiplier=1), reads=["negones"], writes=["selgn_sb"])
        for hl in range(4):
            em.op("pool", lambda e, h0=hl: e.affine_select(
                selg_sb[0:4, h0 * 128:(h0 + 1) * 128], ones128[0:4, :],
                [[0, 128]], ALU.is_equal, 0.0, base=-h0,
                channel_multiplier=1), reads=["ones128"], writes=["selg_sb"])
        for j in range(2):
            c0 = 2 * j
            em.op("pool", lambda e, cc=c0, j0=j: e.affine_select(
                kmask[:, j0 * 32:(j0 + 1) * 32], ones128[:, 0:32], [[1, 32]],
                ALU.is_ge, 0.0, base=-cc, channel_multiplier=0),
                reads=["ones128"], writes=["kmask"])
            em.op("pool", lambda e, cc=c0, j0=j: e.affine_select(
                kmask[:, j0 * 32:(j0 + 1) * 32], kmask[:, j0 * 32:(j0 + 1) * 32],
                [[-1, 32]], ALU.is_ge, 0.0, base=cc + 1, channel_multiplier=0),
                reads=["kmask"], writes=["kmask"])
        for hl in range(4):
            bank, bk = next_bank()
            em.op("pe", lambda e, h0=hl, b=bank: e.matmul(
                b[0:128, 0:NCH], sel_sb[:, h0 * 128:(h0 + 1) * 128],
                egl_hm[0:32, :], start=True, stop=True),
                reads=["sel_sb", "egl_hm"], writes=[bk])
            em.op("dve", lambda e, h0=hl, b=bank: e.tensor_copy(
                eglrep[:, h0 * NCH:(h0 + 1) * NCH], b[0:128, 0:NCH]),
                reads=[bk], writes=["eglrep"])
        em.op("pool", lambda e: e.memset(S_sb[:], 0.0), writes=["S_sb"])
        em.op("pool", lambda e: e.memset(ext[:], 0.0), writes=["ext"])

        NPRE_TG = t_pre // TG

        def conv_one(ct_slot, dst_ap, dst_k, ch_glob, psum_ap, psum_k):
            e0 = ext[:, ct_slot * EXTW:(ct_slot + 1) * EXTW]
            em.op("act", lambda e: e.activation(out=e0[:, 3:3 + TG],
                  in_=psum_ap, func=AF.Copy), reads=[psum_k], writes=["ext"])
            em.op("act", lambda e: e.activation(out=dst_ap, in_=e0[:, 0:TG],
                  func=AF.Copy, scale=conv_sb[:, ch_glob * 4:ch_glob * 4 + 1]),
                  reads=["ext", "conv_sb"], writes=[dst_k])
            for j in range(1, KCONV):
                em.op("dve", lambda e, j0=j: e.scalar_tensor_tensor(
                    dst_ap, e0[:, j0:j0 + TG],
                    conv_sb[:, ch_glob * 4 + j0:ch_glob * 4 + j0 + 1], dst_ap,
                    ALU.mult, ALU.add), reads=["ext", "conv_sb", dst_k],
                    writes=[dst_k])
            em.op("act", lambda e: e.activation(out=dst_ap, in_=dst_ap,
                  func=AF.Silu), reads=[dst_k], writes=[dst_k])
            em.op("dve", lambda e: e.tensor_copy(e0[:, 0:3], e0[:, TG:TG + 3]),
                  reads=["ext"], writes=["ext"])

        for tgi in range(NTG):
            tg0 = tgi * TG
            own_tg = tg0 >= t_pre
            warm_tg = (t_pre > 0 and tgi == NPRE_TG - 1)
            banks = []
            nct = 6 + (2 if own_tg else 0)
            for _ in range(nct + (2 if warm_tg else 0)):
                banks.append(next_bank())
            for r in range(2):
                load_xn_half(r, tg0)
                if own_tg or warm_tg:
                    for dcl in range(8):
                        for tq, col in enumerate(q_cols):
                            em.op("sp", lambda e, d0=8 * r + dcl, dl=dcl,
                                  c0=col, t0=tq: e.dma_start(
                                  out=wq_sb[:, (dl * 2 + t0) * 128:(dl * 2 + t0 + 1) * 128],
                                  in_=w_qkv[d0 * 128:(d0 + 1) * 128, c0:c0 + 128]),
                                  writes=["wq_sb"], dma="wq")
                for dcl in range(8):
                    dc = 8 * r + dcl
                    st = (r == 0 and dcl == 0)
                    sp = (r == 1 and dcl == 7)
                    for ct in range(6):
                        b0, bk0 = banks[ct]
                        em.op("pe", lambda e, d0=dc, dl=dcl, t0=ct, b=b0,
                              s1=st, s2=sp: e.matmul(
                              b[:, 0:TG], wkv_sb[:, (d0 * 6 + t0) * 128:(d0 * 6 + t0 + 1) * 128],
                              xn_g[:, dl * TG:(dl + 1) * TG], start=s1, stop=s2),
                              reads=["wkv_sb", "xn_g"], writes=[bk0])
                    if own_tg:
                        for tq in range(2):
                            b0, bk0 = banks[6 + tq]
                            em.op("pe", lambda e, dl=dcl, t0=tq, b=b0,
                                  s1=st, s2=sp: e.matmul(
                                  b[:, 0:TG], wq_sb[:, (dl * 2 + t0) * 128:(dl * 2 + t0 + 1) * 128],
                                  xn_g[:, dl * TG:(dl + 1) * TG], start=s1, stop=s2),
                                  reads=["wq_sb", "xn_g"], writes=[bk0])
                    if warm_tg:
                        for tq in range(2):
                            b0, bk0 = banks[6 + tq]
                            em.op("pe", lambda e, dl=dcl, t0=tq, b=b0,
                                  s1=st, s2=sp: e.matmul(
                                  b[:, 0:128], wq_sb[:, (dl * 2 + t0) * 128:(dl * 2 + t0 + 1) * 128],
                                  xn_g[:, dl * TG + 3 * 128:dl * TG + 4 * 128],
                                  start=s1, stop=s2),
                                  reads=["wq_sb", "xn_g"], writes=[bk0])
            for ct in range(6):
                ch_glob = ((HK + 2 * g + ct) if ct < 2
                           else (2 * HK + 4 * g + ct - 2))
                b0, bk0 = banks[ct]
                conv_one(ct, kvf[:, ct * TG:(ct + 1) * TG], "kvf", ch_glob,
                         b0[:, 0:TG], bk0)
            if own_tg:
                for tq in range(2):
                    b0, bk0 = banks[6 + tq]
                    conv_one(6 + tq, qf[:, tq * TG:(tq + 1) * TG], "qf",
                             2 * g + tq, b0[:, 0:TG], bk0)
            if warm_tg:
                for tq in range(2):
                    b0, bk0 = banks[6 + tq]
                    slot = 6 + tq
                    e0 = ext[:, slot * EXTW:(slot + 1) * EXTW]
                    em.op("act", lambda e, b=b0, e_=e0: e.activation(
                        out=e_[:, 3:3 + 128], in_=b[:, 0:128], func=AF.Copy),
                        reads=[bk0], writes=["ext"])
                    em.op("dve", lambda e, e_=e0: e.tensor_copy(
                        e_[:, 0:3], e_[:, 128:131]), reads=["ext"], writes=["ext"])
            pks, kpks = next_bank()
            for j in range(2):
                em.op("act", lambda e, j0=j: e.activation(
                    out=k2s[:], in_=kvf[:, j0 * TG:(j0 + 1) * TG],
                    func=AF.Square), reads=["kvf"], writes=["k2s"])
                em.op("pe", lambda e, j0=j, b=pks: e.matmul(
                    b[0:32, 0:TG], kmask[:, j0 * 32:(j0 + 1) * 32],
                    k2s[:], start=(j0 == 0), stop=(j0 == 1)),
                    reads=["kmask", "k2s"], writes=[kpks])
            em.op("act", lambda e, b=pks: e.activation(
                out=lnsk4[0:4, 0:TG],
                in_=b[0:4, 0:TG], func=AF.Ln,
                bias=epsl2_ap[0:4, :]),
                reads=[kpks, epsl2_k], writes=["lnsk4"])
            em.op("dve", lambda e: e.tensor_scalar_mul(
                lnsk4[0:4, 0:TG],
                lnsk4[0:4, 0:TG], -0.5),
                reads=["lnsk4"], writes=["lnsk4"])
            if own_tg:
                pqs, kpqs = next_bank()
                for j in range(2):
                    em.op("act", lambda e, j0=j: e.activation(
                        out=k2s[:], in_=qf[:, j0 * TG:(j0 + 1) * TG],
                        func=AF.Square), reads=["qf"], writes=["k2s"])
                    em.op("pe", lambda e, j0=j, b=pqs: e.matmul(
                        b[0:32, 0:TG], kmask[:, j0 * 32:(j0 + 1) * 32],
                        k2s[:], start=(j0 == 0), stop=(j0 == 1)),
                        reads=["kmask", "k2s"], writes=[kpqs])
                em.op("act", lambda e, b=pqs: e.activation(
                    out=lnsq4[0:4, 0:TG],
                    in_=b[0:4, 0:TG], func=AF.Ln,
                    bias=epsl2_ap[0:4, :]),
                    reads=[kpqs, epsl2_k], writes=["lnsq4"])
                em.op("dve", lambda e: e.tensor_scalar_mul(
                    lnsq4[0:4, 0:TG],
                    lnsq4[0:4, 0:TG], -0.5),
                    reads=["lnsq4"], writes=["lnsq4"])
            if own_tg:
                zbanks = [next_bank() for _ in range(4)]
                for r in range(2):
                    load_xn_half(r, tg0, key="xn_g")
                    for dcl in range(8):
                        dc = 8 * r + dcl
                        wb_i = dc % 2
                        em.op("sp", lambda e, d0=dc, w0=wb_i: e.dma_start(
                            out=wz_buf[w0][:],
                            in_=w_z[d0 * 128:(d0 + 1) * 128,
                                    4 * g * 128:(4 * g + 4) * 128]),
                            writes=[f"wz{wb_i}"], dma=f"wz{wb_i}")
                        for ti in range(4):
                            zb, zbk = zbanks[ti]
                            em.op("pe", lambda e, d0=dc, dl=dcl, i0=ti, b=zb,
                                  w0=wb_i, s1=(r == 0 and dcl == 0),
                                  s2=(r == 1 and dcl == 7): e.matmul(
                                  b[:, 0:512],
                                  xn_g[:, dl * TG + i0 * 128:dl * TG + (i0 + 1) * 128],
                                  wz_buf[w0][:], start=s1, stop=s2),
                                  reads=["xn_g", f"wz{wb_i}"], writes=[zbk])
                for ti in range(4):
                    zb, zbk = zbanks[ti]
                    em.op("dve", lambda e, i0=ti, b=zb: e.tensor_copy(
                        z_sb[:, i0 * 512:(i0 + 1) * 512], b[:, 0:512]),
                        reads=[zbk], writes=["z_sb"])
            for c in range(CPG):
                em.op("dve", lambda e, c0=c, t0=tg0: e.tensor_scalar(
                    p3s_hm[:, c0 * 64:(c0 + 1) * 64],
                    gj_hm[:, t0 + c0 * 64:t0 + (c0 + 1) * 64], -1.0,
                    gj_hm[:, t0 + c0 * 64 + 63:t0 + c0 * 64 + 64],
                    ALU.mult, ALU.add), reads=["gj_hm"], writes=["p3s_hm"])
                for src_ap, src_k, dst, dst_k in (
                    (gj_hm[:, tg0 + c * 64:tg0 + (c + 1) * 64], "gj_hm", gj_tm, "gj_tm"),
                    (p1_hm[:, tg0 + c * 64:tg0 + (c + 1) * 64], "p1_hm", p1_tm, "p1_tm"),
                    (p3s_hm[:, c * 64:(c + 1) * 64], "p3s_hm", p3_tm, "p3_tm"),
                ):
                    bank, bk = next_bank()
                    em.op("pe", lambda e, a=src_ap, b=bank: e.transpose(
                        b[0:64, 0:32], a, ident128[0:32, 0:32]),
                        reads=[src_k, "ident128"], writes=[bk])
                    em.op("dve", lambda e, b=bank, d=dst, c0=c, dk=dst_k: e.tensor_copy(
                        d[0:64, c0 * 32:(c0 + 1) * 32], b[0:64, 0:32]),
                        reads=[bk], writes=[dst_k])
                bank, bk = next_bank()
                em.op("pe", lambda e, c0=c, b=bank: e.transpose(
                    b[0:64, 0:4], lnsk4[0:4, c0 * 64:(c0 + 1) * 64],
                    ident128[0:4, 0:4]), reads=["lnsk4", "ident128"], writes=[bk])
                em.op("dve", lambda e, b=bank, c0=c: e.tensor_copy(
                    lnsk_tm[0:64, c0 * 4:(c0 + 1) * 4], b[0:64, 0:4]),
                    reads=[bk], writes=["lnsk_tm"])
                em.op("dve", lambda e, c0=c: e.tensor_sub(
                    b_tm[0:64, c0 * 32:(c0 + 1) * 32],
                    p1_tm[0:64, c0 * 32:(c0 + 1) * 32],
                    gj_tm[0:64, c0 * 32:(c0 + 1) * 32]),
                    reads=["p1_tm", "gj_tm"], writes=["b_tm"])
                em.op("act", lambda e, c0=c: e.activation(
                    out=b_tm[0:64, c0 * 32:(c0 + 1) * 32],
                    in_=b_tm[0:64, c0 * 32:(c0 + 1) * 32], func=AF.Exp),
                    reads=["b_tm"], writes=["b_tm"])
            for c in range(CPG):
                cg = tg0 // 64 + c
                own_ch = cg * 64 >= t_pre
                ti = c // 2
                co2 = c % 2
                em.op("dve", lambda e, c0=c: e.tensor_add(e1b[0:64, 0:4],
                      p1_tm[0:64, c0 * 32 + 4 * g:c0 * 32 + 4 * g + 4],
                      lnsk_tm[0:64, c0 * 4:(c0 + 1) * 4]),
                      reads=["p1_tm", "lnsk_tm"], writes=["e1b"])
                em.op("act", lambda e: e.activation(out=e1b[0:64, 0:4],
                      in_=e1b[0:64, 0:4],
                      func=AF.Exp), reads=["e1b"], writes=["e1b"])
                em.op("dve", lambda e, c0=c: e.tensor_add(e5b[0:64, 0:4],
                      p3_tm[0:64, c0 * 32 + 4 * g:c0 * 32 + 4 * g + 4],
                      lnsk_tm[0:64, c0 * 4:(c0 + 1) * 4]),
                      reads=["p3_tm", "lnsk_tm"], writes=["e5b"])
                em.op("act", lambda e: e.activation(out=e5b[0:64, 0:4],
                      in_=e5b[0:64, 0:4],
                      func=AF.Exp), reads=["e5b"], writes=["e5b"])
                for hl in range(4):
                    emit_chunk(g, hl, cg, c, ti, co2, own_ch, tg0)
                if own_ch and co2 == 1:
                    flush_tile(g, tg0, ti)

    def emit_chunk(g, hl, cg, c, ti, co2, own_ch, tg0):
        h = 4 * g + hl
        j = hl // 2
        ksl = kvf[:, j * TG + c * 64:j * TG + (c + 1) * 64]
        vsl = kvf[:, (2 + hl) * TG + c * 64:(2 + hl) * TG + (c + 1) * 64]
        S_h = S_sb[:, hl * 128:(hl + 1) * 128]
        sk = f"S{hl}"
        gslice = gj_hm[:, cg * 64:(cg + 1) * 64]
        p1slice = p1_hm[:, cg * 64:(cg + 1) * 64]
        lkslice = lnsk4[0:4, c * 64:(c + 1) * 64]
        gjcol = gj_tm[0:64, c * 32 + h:c * 32 + h + 1]
        p1col = p1_tm[0:64, c * 32 + h:c * 32 + h + 1]
        lkcol = lnsk_tm[0:64, c * 4 + hl:c * 4 + hl + 1]
        bcol = b_tm[0:64, c * 32 + h:c * 32 + h + 1]
        e1col = e1b[0:64, hl:hl + 1]
        e5col = e5b[0:64, hl:hl + 1]
        eglcol = eglrep[:, hl * NCH + cg:hl * NCH + cg + 1]
        selp = sel_sb[:, hl * 128:hl * 128 + 64]
        selg = selg_sb[0:4, hl * 128:hl * 128 + 64]
        selgn = selgn_sb[0:4, hl * 128:hl * 128 + 64]

        pg, kpg = next_bank()
        em.op("pe", lambda e, b=pg: e.matmul(b[0:64, 0:64], ksl, ksl,
              start=True, stop=True), reads=["kvf"], writes=[kpg])
        pB, kpB = next_bank()
        em.op("pe", lambda e, b=pB: (
            e.matmul(b[0:64, 0:64], selp, gslice, start=True, stop=False),
            e.matmul(b[0:64, 0:64], selgn, lkslice, start=False, stop=True))[-1],
            reads=["sel_sb", "selgn_sb", "gj_hm", "lnsk4"], writes=[kpB])
        em.op("dve", lambda e, b=pB: e.tensor_scalar(dtile[:], b[0:64, 0:64],
              p1col, lkcol, ALU.subtract, ALU.subtract),
              reads=[kpB, "p1_tm", "lnsk_tm"], writes=["dtile"])
        em.op("dve", lambda e: e.tensor_mul(dtile[:], dtile[:], mneg_B),
              reads=["dtile", "masks"], writes=["dtile"])
        em.op("act", lambda e: e.activation(out=dtile[:], in_=dtile[:],
              func=AF.Exp), reads=["dtile"], writes=["dtile"])
        em.op("dve", lambda e: e.tensor_mul(dtile[:], dtile[:], mneg_B),
              reads=["dtile", "masks"], writes=["dtile"])
        em.op("dve", lambda e, b=pg: e.tensor_mul(Bp[0][:], b[0:64, 0:64],
              dtile[:]), reads=[kpg, "dtile"], writes=["Bp0"])
        pBt, kpBt = next_bank()
        em.op("pe", lambda e, b=pBt: (
            e.matmul(b[0:64, 0:64], selp, p1slice, start=True, stop=False),
            e.matmul(b[0:64, 0:64], selg, lkslice, start=False, stop=True))[-1],
            reads=["sel_sb", "selg_sb", "p1_hm", "lnsk4"], writes=[kpBt])
        em.op("dve", lambda e, b=pBt: e.tensor_scalar(dtile[:], b[0:64, 0:64],
              gjcol, lkcol, ALU.subtract, ALU.add),
              reads=[kpBt, "gj_tm", "lnsk_tm"], writes=["dtile"])
        em.op("dve", lambda e: e.tensor_mul(dtile[:], dtile[:], m01_Bt),
              reads=["dtile", "masks"], writes=["dtile"])
        em.op("act", lambda e: e.activation(out=dtile[:], in_=dtile[:],
              func=AF.Exp), reads=["dtile"], writes=["dtile"])
        em.op("dve", lambda e: e.tensor_mul(dtile[:], dtile[:], mneg_Bt),
              reads=["dtile", "masks"], writes=["dtile"])
        em.op("dve", lambda e, b=pg: e.tensor_mul(Btp[0][:], b[0:64, 0:64],
              dtile[:]), reads=[kpg, "dtile"], writes=["Btp0"])
        em.op("dve", lambda e: e.tensor_add(Rt[:], Btp[0][:], iden64[:]),
              reads=["Btp0", "iden64"], writes=["Rt"])
        cur = 0
        for step in range(5):
            nxt = 1 - cur
            pB2, kpB2 = next_bank()
            em.op("pe", lambda e, b=pB2, c0=cur: e.matmul(b[0:64, 0:64],
                  Btp[c0][:], Bp[c0][:], start=True, stop=True),
                  reads=[f"Btp{cur}", f"Bp{cur}"], writes=[kpB2])
            pBt2, kpBt2 = next_bank()
            em.op("pe", lambda e, b=pBt2, c0=cur: e.matmul(b[0:64, 0:64],
                  Bp[c0][:], Btp[c0][:], start=True, stop=True),
                  reads=[f"Bp{cur}", f"Btp{cur}"], writes=[kpBt2])
            em.op("dve", lambda e, b=pB2, c0=nxt: e.tensor_copy(Bp[c0][:],
                  b[0:64, 0:64]), reads=[kpB2], writes=[f"Bp{nxt}"])
            em.op("dve", lambda e, b=pBt2, c0=nxt: e.tensor_copy(Btp[c0][:],
                  b[0:64, 0:64]), reads=[kpBt2], writes=[f"Btp{nxt}"])
            pRu, kpRu = next_bank()
            em.op("pe", lambda e, b=pRu, c0=nxt: e.matmul(b[0:64, 0:64],
                  Bp[c0][:], Rt[:], start=True, stop=True),
                  reads=[f"Bp{nxt}", "Rt"], writes=[kpRu])
            em.op("dve", lambda e, b=pRu: e.tensor_add(Rt[:], Rt[:],
                  b[0:64, 0:64]), reads=[kpRu, "Rt"], writes=["Rt"])
            cur = nxt
        pkt, kpkt = next_bank()
        em.op("pe", lambda e, b=pkt: e.transpose(b[0:64, 0:128], ksl,
              ident128[:]), reads=["kvf", "ident128"], writes=[kpkt])
        em.op("dve", lambda e, b=pkt: e.tensor_scalar(kts[:], b[0:64, 0:128],
              e5col, None, ALU.mult), reads=[kpkt, "e5b"], writes=["kts"])
        pP, kpP = next_bank()
        em.op("pe", lambda e, b=pP: e.matmul(b[0:64, 0:128], ksl, S_h,
              start=True, stop=True), reads=["kvf", sk], writes=[kpP])
        em.op("dve", lambda e, b=pP: e.tensor_scalar(u1[:], b[0:64, 0:128],
              e1col, None, ALU.mult), reads=[kpP, "e1b"], writes=["u1"])
        pvt, kpvt = next_bank()
        em.op("pe", lambda e, b=pvt: e.transpose(b[0:64, 0:128], vsl,
              ident128[:]), reads=["kvf", "ident128"], writes=[kpvt])
        em.op("dve", lambda e, b=pvt: e.scalar_tensor_tensor(Pt[:],
              b[0:64, 0:128], bcol, u1[:], ALU.mult, ALU.subtract),
              reads=[kpvt, "b_tm", "u1"], writes=["Pt"])
        pvn, kpvn = next_bank()
        em.op("pe", lambda e, b=pvn: e.matmul(b[0:64, 0:128], Rt[:], Pt[:],
              start=True, stop=True), reads=["Rt", "Pt"], writes=[kpvn])
        em.op("dve", lambda e, b=pvn: e.tensor_copy(vnew[:], b[0:64, 0:128]),
              reads=[kpvn], writes=["vnew"])
        if own_ch:
            emit_own(g, hl, cg, c, ti, co2, tg0, ksl, S_h, sk)
        pSu, kpSu = next_bank()
        em.op("pe", lambda e, b=pSu: e.matmul(b[0:128, 0:128], kts[:],
              vnew[:], start=True, stop=True), reads=["kts", "vnew"],
              writes=[kpSu])
        em.op("dve", lambda e, b=pSu: e.scalar_tensor_tensor(S_h, S_h,
              eglcol, b[0:128, 0:128], ALU.mult, ALU.add),
              reads=[sk, "eglrep", kpSu], writes=[sk])

    def emit_own(g, hl, cg, c, ti, co2, tg0, ksl, S_h, sk):
        h = 4 * g + hl
        jq = hl // 2
        qsl = qf[:, jq * TG + c * 64:jq * TG + (c + 1) * 64]
        gslice = gj_hm[:, cg * 64:(cg + 1) * 64]
        lqslice = lnsq4[0:4, c * 64:(c + 1) * 64]
        gjcol = gj_tm[0:64, c * 32 + h:c * 32 + h + 1]
        lkcol = lnsk_tm[0:64, c * 4 + hl:c * 4 + hl + 1]
        selp = sel_sb[:, hl * 128:hl * 128 + 64]
        selp128 = sel_sb[:, hl * 128:(hl + 1) * 128]
        selg = selg_sb[0:4, hl * 128:hl * 128 + 64]
        selg128 = selg_sb[0:4, hl * 128:(hl + 1) * 128]
        pq, kpq = next_bank()
        em.op("pe", lambda e, b=pq: e.matmul(b[0:64, 0:64], ksl, qsl,
              start=True, stop=True), reads=["kvf", "qf"], writes=[kpq])
        pDi, kpDi = next_bank()
        em.op("pe", lambda e, b=pDi: (
            e.matmul(b[0:64, 0:64], selp, gslice, start=True, stop=False),
            e.matmul(b[0:64, 0:64], selg, lqslice, start=False, stop=True))[-1],
            reads=["sel_sb", "selg_sb", "gj_hm", "lnsq4"], writes=[kpDi])
        em.op("dve", lambda e, b=pDi: e.tensor_scalar(dtile[:], b[0:64, 0:64],
              gjcol, lkcol, ALU.subtract, ALU.add),
              reads=[kpDi, "gj_tm", "lnsk_tm"], writes=["dtile"])
        em.op("dve", lambda e: e.tensor_mul(dtile[:], dtile[:], m01_iT),
              reads=["dtile", "masks"], writes=["dtile"])
        em.op("act", lambda e, b_=negln_ap: e.activation(out=dtile[:],
              in_=dtile[:], func=AF.Exp, bias=b_[0:64, :]),
              reads=["dtile", negln_k], writes=["dtile"])
        em.op("dve", lambda e: e.tensor_mul(dtile[:], dtile[:], m01_iT),
              reads=["dtile", "masks"], writes=["dtile"])
        em.op("dve", lambda e, b=pq: e.tensor_mul(intraT[:], b[0:64, 0:64],
              dtile[:]), reads=[kpq, "dtile"], writes=["intraT"])
        pQ, kpQ = next_bank()
        em.op("pe", lambda e, b=pQ: (
            e.matmul(b[0:128, 0:64], selp128, gslice, start=True, stop=False),
            e.matmul(b[0:128, 0:64], selg128, lqslice, start=False, stop=True))[-1],
            reads=["sel_sb", "selg_sb", "gj_hm", "lnsq4"], writes=[kpQ])
        em.op("act", lambda e, b=pQ, b_=negln_ap: e.activation(out=qsc[:],
              in_=b[0:128, 0:64], func=AF.Exp, bias=b_),
              reads=[kpQ, negln_k], writes=["qsc"])
        em.op("dve", lambda e: e.tensor_mul(qs[:], qsl, qsc[:]),
              reads=["qf", "qsc"], writes=["qs"])
        pO, kpO = next_bank()
        em.op("pe", lambda e, b=pO, c2=co2: (
            e.matmul(b[64 * c2:64 * c2 + 64, 0:128], qs[:], S_h,
                     start=True, stop=False),
            e.matmul(b[64 * c2:64 * c2 + 64, 0:128], intraT[:], vnew[:],
                     start=False, stop=True))[-1],
            reads=["qs", sk, "intraT", "vnew"], writes=[kpO])
        em.op("act", lambda e, c2=co2, i0=ti, h0=hl: e.activation(
            out=zsl[64 * c2:64 * c2 + 64, 0:128],
            in_=z_sb[64 * c2:64 * c2 + 64,
                     i0 * 512 + h0 * 128:i0 * 512 + (h0 + 1) * 128],
            func=AF.Silu), reads=["z_sb"], writes=["zsl"])
        em.op("dve", lambda e, b=pO, c2=co2: e.tensor_mul(
            xg[64 * c2:64 * c2 + 64, :], b[64 * c2:64 * c2 + 64, 0:128],
            zsl[64 * c2:64 * c2 + 64, 0:128]),
            reads=[kpO, "zsl"], writes=["xg"])
        em.op("act", lambda e, c2=co2: e.activation(
            out=zsl[64 * c2:64 * c2 + 64, 0:128],
            in_=xg[64 * c2:64 * c2 + 64, :], func=AF.Square,
            accum_out=rmsc[64 * c2:64 * c2 + 64, 0:1]),
            reads=["xg"], writes=["zsl", "rmsc"])
        em.op("act", lambda e, c2=co2, b_=epsrms_ap: e.activation(
            out=rmsc[64 * c2:64 * c2 + 64, 1:2],
            in_=rmsc[64 * c2:64 * c2 + 64, 0:1], func=AF.Sqrt,
            scale=1.0 / UV, bias=b_[64 * c2:64 * c2 + 64, :]),
            reads=["rmsc", epsrms_k], writes=["rmsc"])
        em.op("dve", lambda e, c2=co2: e.reciprocal(
            rmsc[64 * c2:64 * c2 + 64, 1:2], rmsc[64 * c2:64 * c2 + 64, 1:2]),
            reads=["rmsc"], writes=["rmsc"])
        em.op("dve", lambda e, c2=co2: e.tensor_scalar(
            xg[64 * c2:64 * c2 + 64, :], xg[64 * c2:64 * c2 + 64, :],
            rmsc[64 * c2:64 * c2 + 64, 1:2], None, ALU.mult),
            reads=["xg", "rmsc"], writes=["xg"])
        em.op("dve", lambda e, c2=co2, h0=hl: e.tensor_mul(
            normed_sb[64 * c2:64 * c2 + 64, h0 * 128:(h0 + 1) * 128],
            xg[64 * c2:64 * c2 + 64, :], nw_tile[64 * c2:64 * c2 + 64, :]),
            reads=["xg", "nw_tile"], writes=["normed_sb"])

    def flush_tile(g, tg0, ti):
        trow0 = tg0 + ti * 128 - t_pre
        for hl in range(4):
            bank, bk = next_bank()
            em.op("pe", lambda e, h0=hl, b=bank: e.transpose(
                b[0:128, 0:128], normed_sb[:, h0 * 128:(h0 + 1) * 128],
                ident128[:]), reads=["normed_sb", "ident128"], writes=[bk])
            em.op("dve", lambda e, h0=hl, b=bank: e.tensor_copy(
                normT[:, h0 * 128:(h0 + 1) * 128], b[0:128, 0:128]),
                reads=[bk], writes=["normT"])
        obanks = [next_bank() for _ in range(4)]
        for hl in range(4):
            for dh in range(2):
                em.op("sp", lambda e, h0=hl, q0=dh: e.dma_start(
                    out=wout_buf[:],
                    in_=w_out[(4 * g + h0) * 128:(4 * g + h0 + 1) * 128,
                              q0 * 1024:(q0 + 1) * 1024]),
                    writes=["wout_buf"], dma="wol")
                for d2 in range(2):
                    dq = dh * 2 + d2
                    ob, obk = obanks[dq]
                    em.op("pe", lambda e, h0=hl, q0=d2, b=ob: e.matmul(
                        b[:, 0:512], normT[:, h0 * 128:(h0 + 1) * 128],
                        wout_buf[:, q0 * 512:(q0 + 1) * 512],
                        start=(h0 == 0), stop=(h0 == 3)),
                        reads=["normT", "wout_buf"], writes=[obk])
        for dq in range(4):
            ob, obk = obanks[dq]
            em.op("dve", lambda e, b=ob: e.tensor_copy(part_sb[:], b[:, 0:512]),
                  reads=[obk], writes=["part_sb"])
            em.op("pool", lambda e, q0=dq, t0=trow0: e.dma_start(
                out=attn_part[g, t0:t0 + 128, q0 * 512:(q0 + 1) * 512],
                in_=part_sb[:]), reads=["part_sb"], writes=["attn_part"],
                dma="partw")

    for g in range(NG):
        emit_group(g)
        em.barrier()

    em.barrier()
    NTO = t_own // 128
    TP0 = t_pre // 128

    fin_b = normT[0:128, 0:512]
    gwt_sb = conv_sb[0:128, 0:DCH * 16]
    for dc in range(DCH):
        em.op("sp", lambda e, d0=dc: e.dma_start(
            out=gwt_sb[:, d0 * 16:(d0 + 1) * 16],
            in_=gate_wt[d0 * 128:(d0 + 1) * 128, :]), writes=["gwt_sb"],
            dma="gwt")
    comb_sb = part_sb[0:128, 256:384]
    combT_h = [S_sb[0:16, 0:512], normed_sb[0:16, 0:512]]
    rt_s = eglrep[0:128, max(4 * NCH, 96) - 96:max(4 * NCH, 96) - 16]
    rt_b = wz_buf[0][0:128, 0:256]
    rt_c = wz_buf[1][0:128, 0:256]
    iota_g = eglrep[0:128, max(4 * NCH, 96) - 16:max(4 * NCH, 96)]
    iota_e = part_sb[0:128, 0:256]
    em.op("pool", lambda e: e.affine_select(
        iota_g[:], ones128[:, 0:16], [[1, 4], [-1, 4]], ALU.is_gt, 0.0,
        channel_multiplier=0), reads=["ones128"], writes=["iota_g"])
    em.op("pool", lambda e: e.memset(iota_e[:], 1.0), writes=["iota_e"])
    em.op("pool", lambda e: e.affine_select(
        iota_e[:], iota_e[:], [[1, 16], [-1, 16]],
        ALU.is_gt, 0.0, channel_multiplier=0),
        reads=["iota_e"], writes=["iota_e"])
    xn_b = z_sb[0:128, 0:D]
    assert NTO <= 8

    AX = mybir.AxisListType.X
    for it in range(NTO):
        em.op("sp", lambda e, i0=it: e.dma_start(
            out=xn_b[:], in_=xn_hbm[TP0 + i0]),
            reads=["xn_hbm"], writes=["xn_b"], dma="xnb")
        psc, kpsc = next_bank()
        for dc in range(DCH):
            em.op("pe", lambda e, d0=dc, b=psc: e.matmul(
                b[:, 0:16], xn_b[:, d0 * 128:(d0 + 1) * 128],
                gwt_sb[:, d0 * 16:(d0 + 1) * 16],
                start=(d0 == 0), stop=(d0 == DCH - 1)),
                reads=["xn_b", "gwt_sb"], writes=[kpsc])
        sc = rt_s[0:128, 0:16]
        gmx = rt_s[0:128, 16:20]
        cnt4 = rt_s[0:128, 20:24]
        msc = rt_s[0:128, 32:48]
        cnt16 = rt_s[0:128, 48:64]
        den = rt_s[0:128, 64:65]
        em.op("act", lambda e, b=psc: e.activation(out=sc, in_=b[:, 0:16],
              func=AF.Sigmoid), reads=[kpsc], writes=["rt_s"])
        em.op("dve", lambda e: e.reduce_max(gmx, sc.rearrange(
              "p (a b) -> p a b", b=4), axis=AX), reads=["rt_s"], writes=["rt_s"])
        em.op("dve", lambda e: e.tensor_tensor(
              rt_b[:, 0:16].rearrange("p (i j) -> p i j", j=4),
              gmx.unsqueeze(1).broadcast_to((128, 4, 4)),
              gmx.unsqueeze(2).broadcast_to((128, 4, 4)), ALU.is_gt),
              reads=["rt_s"], writes=["rt_b"])
        em.op("dve", lambda e: e.tensor_tensor(
              rt_b[:, 16:32].rearrange("p (i j) -> p i j", j=4),
              gmx.unsqueeze(1).broadcast_to((128, 4, 4)),
              gmx.unsqueeze(2).broadcast_to((128, 4, 4)), ALU.is_equal),
              reads=["rt_s"], writes=["rt_b"])
        em.op("dve", lambda e: e.tensor_mul(rt_b[:, 16:32], rt_b[:, 16:32],
              iota_g[:]), reads=["rt_b", "iota_g"], writes=["rt_b"])
        em.op("dve", lambda e: e.tensor_add(rt_b[:, 0:16], rt_b[:, 0:16],
              rt_b[:, 16:32]), reads=["rt_b"], writes=["rt_b"])
        em.op("dve", lambda e: e.reduce_sum(cnt4, rt_b[:, 0:16].rearrange(
              "p (i j) -> p i j", j=4), axis=AX), reads=["rt_b"], writes=["rt_s"])
        em.op("dve", lambda e: e.tensor_single_scalar(cnt4, cnt4, 1.5,
              ALU.is_lt), reads=["rt_s"], writes=["rt_s"])
        em.op("dve", lambda e: e.tensor_tensor(
              msc.rearrange("p (a b) -> p a b", b=4),
              sc.rearrange("p (a b) -> p a b", b=4),
              cnt4.unsqueeze(2).broadcast_to((128, 4, 4)), ALU.mult),
              reads=["rt_s"], writes=["rt_s"])
        em.op("dve", lambda e: e.tensor_tensor(
              rt_b[:, 0:256].rearrange("p (i j) -> p i j", j=16),
              msc.unsqueeze(1).broadcast_to((128, 16, 16)),
              msc.unsqueeze(2).broadcast_to((128, 16, 16)), ALU.is_gt),
              reads=["rt_s"], writes=["rt_b"])
        eqb = rt_c[0:128, 0:256]
        em.op("dve", lambda e, s=eqb: e.tensor_tensor(
              s.rearrange("p (i j) -> p i j", j=16),
              msc.unsqueeze(1).broadcast_to((128, 16, 16)),
              msc.unsqueeze(2).broadcast_to((128, 16, 16)), ALU.is_equal),
              reads=["rt_s"], writes=["rt_c"])
        em.op("dve", lambda e, s=eqb: e.tensor_mul(s, s, iota_e[:]),
              reads=["rt_c", "iota_e"], writes=["rt_c"])
        em.op("dve", lambda e, s=eqb: e.tensor_add(rt_b[:, 0:256],
              rt_b[:, 0:256], s), reads=["rt_b", "rt_c"], writes=["rt_b"])
        em.op("dve", lambda e: e.reduce_sum(cnt16, rt_b[:, 0:256].rearrange(
              "p (i j) -> p i j", j=16), axis=AX), reads=["rt_b"], writes=["rt_s"])
        em.op("dve", lambda e: e.tensor_single_scalar(cnt16, cnt16, 3.5,
              ALU.is_lt), reads=["rt_s"], writes=["rt_s"])
        em.op("dve", lambda e, i0=it: e.tensor_mul(
              comb_sb[:, i0 * 16:(i0 + 1) * 16], sc, cnt16),
              reads=["rt_s"], writes=["comb_sb"])
        em.op("dve", lambda e, i0=it: e.reduce_sum(den,
              comb_sb[:, i0 * 16:(i0 + 1) * 16], axis=AX),
              reads=["comb_sb"], writes=["rt_s"])
        em.op("dve", lambda e: e.reciprocal(den, den),
              reads=["rt_s"], writes=["rt_s"])
        em.op("dve", lambda e, i0=it: e.tensor_scalar(
              comb_sb[:, i0 * 16:(i0 + 1) * 16],
              comb_sb[:, i0 * 16:(i0 + 1) * 16], den, None, ALU.mult),
              reads=["comb_sb", "rt_s"], writes=["comb_sb"])
        bank, bk = next_bank()
        em.op("pe", lambda e, i0=it, b=bank: e.transpose(
            b[0:16, 0:128], comb_sb[:, i0 * 16:(i0 + 1) * 16],
            ident128[:]), reads=["comb_sb", "ident128"], writes=[bk])
        em.op("dve", lambda e, i0=it, b=bank: e.tensor_copy(
            combT_h[i0 // 4][0:16, (i0 % 4) * 128:(i0 % 4 + 1) * 128],
            b[0:16, 0:128]),
            reads=[bk], writes=["combT"])

    em.barrier()
    xn_own = wkv_sb[0:128, 0:8192]
    yslot = wkv_sb[0:128, 8192:16384]
    w13_b = [wq_sb[0:128, i * 512:(i + 1) * 512] for i in range(4)]
    wrep = k2s[0:128, 0:512]
    sel16 = S_sb[0:16, 512:512] if False else None
    sel16 = xg[0:16, 0:128]
    gexp_slots = []
    for nm, tnsr, ncols in (("xng", xn_g, 4096), ("kvf", kvf, 3072)):
        for s0 in range(0, ncols - 511, 512):
            if len(gexp_slots) < 11:
                gexp_slots.append((tnsr[0:128, s0:s0 + 512], f"gx_{nm}_{s0}"))
    gsh_slots = []
    for nm, tnsr, ncols in (("qf", qf, 1024), ("zb", z_sb, 2048),
                            ("ext", ext, 4128), ("xng", xn_g, 4096),
                            ("kvf", kvf, 3072)):
        for s0 in range(0, ncols - 511, 512):
            if len(gsh_slots) < NSH // 128:
                gsh_slots.append((tnsr[0:128, s0:s0 + 512], f"gsh_{nm}_{s0}"))
    NSH_CH = NSH // 128

    def y_add(tq, dq, pbank, pkey, first):
        dst = yslot[0:128, tq * 2048 + dq * 512:tq * 2048 + (dq + 1) * 512]
        if first:
            em.op("dve", lambda e, b=pbank: e.tensor_copy(
                dst, b[:, 0:512]), reads=[pkey], writes=["yslot"])
        else:
            em.op("dve", lambda e, b=pbank: e.tensor_add(
                dst, dst, b[:, 0:512]), reads=[pkey, "yslot"], writes=["yslot"])

    for half in range(t_own // 512):
        em.barrier()
        th0 = half * 512
        for dc in range(DCH):
            for ti in range(4):
                em.op("sp", lambda e, d0=dc, i0=ti, h0=half: e.dma_start(
                    out=xn_own[0:128, d0 * 512 + i0 * 128:d0 * 512 + (i0 + 1) * 128],
                    in_=xn_hbm[TP0 + h0 * 4 + i0, :, d0 * 128:(d0 + 1) * 128]),
                    reads=["xn_hbm"], writes=["xn_own"], dma="xnown")
        for ex in range(NE):
            em.op("pool", lambda e, e0=ex: e.affine_select(
                sel16[0:16, :], ones128[0:16, :], [[0, 128]], ALU.is_equal,
                0.0, base=-e0, channel_multiplier=1),
                reads=["ones128"], writes=["sel16"])
            pwr, kpwr = next_bank()
            em.op("pe", lambda e, b=pwr, h0=half: e.matmul(b[0:128, 0:512],
                  sel16[0:16, :], combT_h[h0][0:16, 0:512],
                  start=True, stop=True),
                  reads=["sel16", "combT"], writes=[kpwr])
            em.op("dve", lambda e, b=pwr: e.tensor_copy(wrep[:], b[0:128, 0:512]),
                  reads=[kpwr], writes=["wrep"])
            for ic in range(11):
                p1, kp1 = next_bank()
                p3, kp3 = next_bank()
                for dc in range(DCH):
                    wb = w13_b[dc % 2]
                    wb3 = w13_b[2 + dc % 2]
                    em.op("sp", lambda e, e0=ex, d0=dc, i0=ic, w=wb, wi=dc % 2:
                          e.dma_start(out=w[0:128, 0:128],
                          in_=w1t[e0, d0 * 128:(d0 + 1) * 128,
                                  i0 * 128:(i0 + 1) * 128]),
                          writes=[f"w13_{dc % 2}"], dma=f"w13l{dc % 2}")
                    em.op("sp", lambda e, e0=ex, d0=dc, i0=ic, w=wb3:
                          e.dma_start(out=w[0:128, 0:128],
                          in_=w3t[e0, d0 * 128:(d0 + 1) * 128,
                                  i0 * 128:(i0 + 1) * 128]),
                          writes=[f"w33_{dc % 2}"], dma=f"w33l{dc % 2}")
                    em.op("pe", lambda e, d0=dc, b=p1, w=wb: e.matmul(
                          b[:, 0:512], w[0:128, 0:128],
                          xn_own[0:128, d0 * 512:(d0 + 1) * 512],
                          start=(d0 == 0), stop=(d0 == DCH - 1)),
                          reads=[f"w13_{dc % 2}", "xn_own"], writes=[kp1])
                    em.op("pe", lambda e, d0=dc, b=p3, w=wb3: e.matmul(
                          b[:, 0:512], w[0:128, 0:128],
                          xn_own[0:128, d0 * 512:(d0 + 1) * 512],
                          start=(d0 == 0), stop=(d0 == DCH - 1)),
                          reads=[f"w33_{dc % 2}", "xn_own"], writes=[kp3])
                gsl, gslk = gexp_slots[ic]
                em.op("act", lambda e, b=p1, s=gsl: e.activation(out=s,
                      in_=b[:, 0:512], func=AF.Silu), reads=[kp1],
                      writes=[gslk])
                em.op("dve", lambda e, b=p3, s=gsl: e.tensor_mul(s, s,
                      b[:, 0:512]), reads=[kp3, gslk], writes=[gslk])
                em.op("dve", lambda e, s=gsl: e.tensor_mul(s, s, wrep[:]),
                      reads=["wrep", gslk], writes=[gslk])
            for dq in range(4):
                ybanks = [next_bank() for _ in range(4)]
                for ic in range(11):
                    wb = w13_b[ic % 2]
                    em.op("sp", lambda e, e0=ex, i0=ic, q0=dq, w=wb:
                          e.dma_start(out=w[0:128, 0:512],
                          in_=w2t[e0, i0 * 128:(i0 + 1) * 128,
                                  q0 * 512:(q0 + 1) * 512]),
                          writes=[f"w13_{ic % 2}"], dma=f"w13l{ic % 2}")
                    gsl, gslk = gexp_slots[ic]
                    for tq in range(4):
                        py, kpy = ybanks[tq]
                        em.op("pe", lambda e, t0=tq, b=py, w=wb, s=gsl,
                              i0=ic: e.matmul(
                              b[:, 0:512],
                              s[0:128, t0 * 128:(t0 + 1) * 128],
                              w[0:128, 0:512],
                              start=(i0 == 0), stop=(i0 == 10)),
                              reads=[f"w13_{ic % 2}", gslk], writes=[kpy])
                for tq in range(4):
                    py, kpy = ybanks[tq]
                    y_add(tq, dq, py, kpy, ex == 0)
        em.barrier()
        for ic in range(NSH_CH):
            p1, kp1 = next_bank()
            p3, kp3 = next_bank()
            for dc in range(DCH):
                wb = w13_b[dc % 2]
                wb3 = w13_b[2 + dc % 2]
                em.op("sp", lambda e, d0=dc, i0=ic, w=wb:
                      e.dma_start(out=w[0:128, 0:128],
                      in_=ws1t[d0 * 128:(d0 + 1) * 128, i0 * 128:(i0 + 1) * 128]),
                      writes=[f"w13_{dc % 2}"], dma=f"w13l{dc % 2}")
                em.op("sp", lambda e, d0=dc, i0=ic, w=wb3:
                      e.dma_start(out=w[0:128, 0:128],
                      in_=ws3t[d0 * 128:(d0 + 1) * 128, i0 * 128:(i0 + 1) * 128]),
                      writes=[f"w33_{dc % 2}"], dma=f"w33l{dc % 2}")
                em.op("pe", lambda e, d0=dc, b=p1, w=wb: e.matmul(
                      b[:, 0:512], w[0:128, 0:128],
                      xn_own[0:128, d0 * 512:(d0 + 1) * 512],
                      start=(d0 == 0), stop=(d0 == DCH - 1)),
                      reads=[f"w13_{dc % 2}", "xn_own"], writes=[kp1])
                em.op("pe", lambda e, d0=dc, b=p3, w=wb3: e.matmul(
                      b[:, 0:512], w[0:128, 0:128],
                      xn_own[0:128, d0 * 512:(d0 + 1) * 512],
                      start=(d0 == 0), stop=(d0 == DCH - 1)),
                      reads=[f"w33_{dc % 2}", "xn_own"], writes=[kp3])
            gs, gk = gsh_slots[ic]
            em.op("act", lambda e, b=p1, s=gs: e.activation(out=s,
                  in_=b[:, 0:512], func=AF.Silu), reads=[kp1], writes=[gk])
            em.op("dve", lambda e, b=p3, s=gs: e.tensor_mul(s, s, b[:, 0:512]),
                  reads=[kp3, gk], writes=[gk])
        for dq in range(4):
            ybanks = [next_bank() for _ in range(4)]
            for ic in range(NSH_CH):
                wb = w13_b[ic % 2]
                gs, gk = gsh_slots[ic]
                em.op("sp", lambda e, i0=ic, q0=dq, w=wb:
                      e.dma_start(out=w[0:128, 0:512],
                      in_=ws2t[i0 * 128:(i0 + 1) * 128, q0 * 512:(q0 + 1) * 512]),
                      writes=[f"w13_{ic % 2}"], dma=f"w13l{ic % 2}")
                for tq in range(4):
                    py, kpy = ybanks[tq]
                    em.op("pe", lambda e, i0=ic, t0=tq, b=py, w=wb, s=gs:
                          e.matmul(b[:, 0:512],
                          s[0:128, t0 * 128:(t0 + 1) * 128], w[0:128, 0:512],
                          start=(i0 == 0), stop=(i0 == NSH_CH - 1)),
                          reads=[f"w13_{ic % 2}", gk], writes=[kpy])
            for tq in range(4):
                py, kpy = ybanks[tq]
                y_add(tq, dq, py, kpy, False)
        em.barrier()
        fin_a = xn_own[0:128, 0:D]
        for ti in range(4):
            it = half * 4 + ti
            em.op("sp", lambda e, i0=it: e.dma_start(
                out=fin_a[:], in_=hidden[t_pre + i0 * 128:t_pre + (i0 + 1) * 128, :]),
                writes=["fin_a"], dma="finx")
            for g in range(NG):
                for dq in range(4):
                    em.op("sp", lambda e, i0=it, g0=g, q0=dq: e.dma_start(
                        out=fin_b[:],
                        in_=attn_part[g0, i0 * 128:(i0 + 1) * 128,
                                      q0 * 512:(q0 + 1) * 512]),
                        reads=["attn_part"], writes=["fin_b"], dma="finp")
                    em.op("dve", lambda e, q0=dq: e.tensor_add(
                        fin_a[:, q0 * 512:(q0 + 1) * 512],
                        fin_a[:, q0 * 512:(q0 + 1) * 512], fin_b[:]),
                        reads=["fin_a", "fin_b"], writes=["fin_a"])
            em.op("dve", lambda e, t0=ti: e.tensor_add(fin_a[:], fin_a[:],
                  yslot[0:128, t0 * 2048:(t0 + 1) * 2048]),
                  reads=["fin_a", "yslot"], writes=["fin_a"])
            em.op("pool", lambda e, i0=it: e.dma_start(
                out=out[i0 * 128:(i0 + 1) * 128, :], in_=fin_a[:]),
                reads=["fin_a"], dma="outw")

    if debug_out:
        dbg_b = sbuf("dbg_b", (128, 512))
        for name, shape in debug_out:
            if name == "xn_hbm":
                for tt in range(NT):
                    for half in range(4):
                        em.op("pool", lambda e, t0=tt, h0=half: e.dma_start(
                            out=dbg_b[:, 0:512],
                            in_=xn_hbm[t0, :, h0 * 512:(h0 + 1) * 512]),
                            reads=["xn_hbm"], writes=["dbg_b"], dma="dbg")
                        em.op("pool", lambda e, t0=tt, h0=half: e.dma_start(
                            out=dbg["xn_hbm"][t0, :, h0 * 512:(h0 + 1) * 512],
                            in_=dbg_b[:, 0:512]),
                            reads=["dbg_b"], dma="dbg")

    sems = {}
    for e in ENGINES:
        h = nc.semaphore("sem_" + e)
        keep.append(h)
        sems["sem_" + e] = h.__enter__()
    for s in em.dma_cnt:
        h = nc.semaphore(s)
        keep.append(h)
        sems[s] = h.__enter__()

    em.build(sems)
    return nc


def prep_inputs(kw, t_pre=None, t_own=None):
    t_pre = T_PRE if t_pre is None else t_pre
    t_own = T_OWN if t_own is None else t_own
    hid = np.asarray(kw["L0_attn_hidden"], dtype=np.float32)
    shared = {
        "w_qkv": np.ascontiguousarray(kw["L0_attn_W_qkv"], dtype=np.float32),
        "w_z": np.ascontiguousarray(kw["L0_attn_W_z"], dtype=np.float32),
        "w_b": np.ascontiguousarray(kw["L0_attn_W_b"], dtype=np.float32),
        "w_a": np.ascontiguousarray(kw["L0_attn_W_a"], dtype=np.float32),
        "conv_w": np.ascontiguousarray(kw["L0_attn_conv_weight"], dtype=np.float32),
        "dt_bias": np.ascontiguousarray(kw["L0_attn_dt_bias"], dtype=np.float32),
        "a_log": np.ascontiguousarray(kw["L0_attn_A_log"], dtype=np.float32),
        "norm_w": np.ascontiguousarray(kw["L0_attn_norm_w"], dtype=np.float32),
        "w_out": np.ascontiguousarray(kw["L0_attn_W_out"], dtype=np.float32),
        "gate_wt": np.ascontiguousarray(np.asarray(kw["L0_ffn_gate_w"]).T, dtype=np.float32),
        "w1t": np.ascontiguousarray(np.transpose(np.asarray(kw["L0_ffn_W1"]), (0, 2, 1)), dtype=np.float32),
        "w2t": np.ascontiguousarray(np.transpose(np.asarray(kw["L0_ffn_W2"]), (0, 2, 1)), dtype=np.float32),
        "w3t": np.ascontiguousarray(np.transpose(np.asarray(kw["L0_ffn_W3"]), (0, 2, 1)), dtype=np.float32),
        "ws1t": np.ascontiguousarray(np.asarray(kw["L0_ffn_Ws1"]).T, dtype=np.float32),
        "ws2t": np.ascontiguousarray(np.asarray(kw["L0_ffn_Ws2"]).T, dtype=np.float32),
        "ws3t": np.ascontiguousarray(np.asarray(kw["L0_ffn_Ws3"]).T, dtype=np.float32),
    }
    in_maps = []
    n_q = T // t_own
    for c in range(N_CORES):
        b, q = divmod(c, n_q)
        start = q * t_own
        pre_real = hid[b, max(0, start - t_pre):start, :]
        pad = np.zeros((t_pre - pre_real.shape[0], D), dtype=np.float32)
        own = hid[b, start:start + t_own, :]
        m = dict(shared)
        m["hidden"] = np.ascontiguousarray(
            np.concatenate([pad, pre_real, own], axis=0))
        in_maps.append(m)
    return in_maps


def kernel(**kw):
    in_maps = prep_inputs(kw)
    nc = build_program()
    res = run_bass_kernel_spmd(nc, in_maps, core_ids=list(range(N_CORES)))
    parts = [np.asarray(r["out"]) for r in res.results]
    full = np.concatenate(parts, axis=0).reshape(B, T, D)
    return np.ascontiguousarray(full, dtype=np.float32)


# version 11
# speedup vs baseline: 1.0411x; 1.0411x over previous
import numpy as np
import concourse.bass as bass
import concourse.mybir as mybir
from concourse.bass_utils import run_bass_kernel_spmd

F32 = mybir.dt.float32
BF16 = mybir.dt.bfloat16
I16 = mybir.dt.int16
AF = mybir.ActivationFunctionType
ALU = mybir.AluOpType

B, T, D = 2, 4096, 2048
HK, HV, AK, UV = 16, 32, 128, 128
KEY_DIM = HK * AK
VALUE_DIM = HV * UV
CONV_DIM = 2 * KEY_DIM + VALUE_DIM
KCONV = 4
CHUNK = 64
EPS_L2 = 1e-6
EPS_LN = 1e-5
EPS_RMS = 1e-6
NE, NI = 16, 1408
NSH = 2816

T_PRE = 3072
T_OWN = 1024
N_CORES = 8

ENGINES = ("pe", "act", "dve", "sp", "pool")


class Em:

    def __init__(self, nc):
        self.nc = nc
        self.q = {e: [] for e in ENGINES}
        self.cnt = {e: 0 for e in ENGINES}
        self.dma_cnt = {}
        self.last_w = {}
        self.readers = {}
        self.hwm = {e: {} for e in ENGINES}
        self.barrier_req = {}
        self.n_ops = 0

    def barrier(self):
        snap = {("sem_" + e): self.cnt[e] for e in ENGINES if self.cnt[e]}
        snap.update(dict(self.dma_cnt))
        self.barrier_req = snap
        self.last_w = {}
        self.readers = {}

    def op(self, eng, fn, reads=(), writes=(), dma=None):
        need = dict(self.barrier_req)
        for k in reads:
            w = self.last_w.get(k)
            if w:
                need[w[0]] = max(need.get(w[0], 0), w[1])
        for k in writes:
            w = self.last_w.get(k)
            if w:
                need[w[0]] = max(need.get(w[0], 0), w[1])
            for s, v in self.readers.get(k, {}).items():
                need[s] = max(need.get(s, 0), v)
        hw = self.hwm[eng]
        waits = []
        for s, v in sorted(need.items()):
            if hw.get(s, -1) < v:
                waits.append((s, v))
                hw[s] = v
        if dma:
            sem_name = "dma_" + dma
            need.pop(sem_name, None)
            self.dma_cnt[sem_name] = self.dma_cnt.get(sem_name, 0) + 16
            val = self.dma_cnt[sem_name]
        else:
            self.cnt[eng] += 1
            sem_name, val = "sem_" + eng, self.cnt[eng]
        self.q[eng].append((waits, fn, sem_name))
        for k in writes:
            self.last_w[k] = (sem_name, val)
            self.readers[k] = {}
        for k in reads:
            self.readers.setdefault(k, {})
            r = self.readers[k]
            r[sem_name] = max(r.get(sem_name, 0), val)
        self.n_ops += 1

    def build(self, sems):
        def run(engine_obj, name):
            for waits, fn, sem_name in self.q[name]:
                for s, v in waits:
                    engine_obj.wait_ge(sems[s], v)
                last = fn(engine_obj)
                inc = 16 if sem_name.startswith("dma_") else 1
                last.then_inc(sems[sem_name], inc)
            if name == "pool":
                for s2, v2 in sorted(self.dma_cnt.items()):
                    engine_obj.wait_ge(sems[s2], v2)
                for e2 in ENGINES:
                    if self.cnt[e2] and e2 != "pool":
                        engine_obj.wait_ge(sems["sem_" + e2], self.cnt[e2])

        with self.nc.Block() as block:
            @block.tensor
            def _(e):
                run(e, "pe")

            @block.scalar
            def _(e):
                run(e, "act")

            @block.vector
            def _(e):
                run(e, "dve")

            @block.sync
            def _(e):
                run(e, "sp")

            @block.gpsimd
            def _(e):
                run(e, "pool")


def ceil_div(a, b):
    return (a + b - 1) // b


def build_program(t_pre=None, t_own=None, debug_out=()):
    t_pre = T_PRE if t_pre is None else t_pre
    t_own = T_OWN if t_own is None else t_own
    t_all = t_pre + t_own
    assert t_all % 128 == 0 and t_own % 128 == 0 and t_pre % CHUNK == 0
    NT = t_all // 128
    DCH = D // 128

    nc = bass.Bass("TRN2", target_bir_lowering=False)
    em = Em(nc)
    keep = []
    dr = {}

    def din(name, shape, dtype=F32):
        dr[name] = nc.dram_tensor(name, list(shape), dtype, kind="ExternalInput").ap()
        return dr[name]

    def dout_(name, shape, dtype=F32):
        dr[name] = nc.dram_tensor(name, list(shape), dtype, kind="ExternalOutput").ap()
        return dr[name]

    def dscratch(name, shape, dtype=F32):
        dr[name] = nc.dram_tensor(name, list(shape), dtype, kind="Internal").ap()
        return dr[name]

    hidden = din("hidden", (t_all, D))
    w_qkv = din("w_qkv", (D, CONV_DIM))
    w_z = din("w_z", (D, VALUE_DIM))
    w_b = din("w_b", (D, HV))
    w_a = din("w_a", (D, HV))
    conv_w = din("conv_w", (CONV_DIM, KCONV))
    dt_bias = din("dt_bias", (HV,))
    a_log = din("a_log", (HV,))
    norm_w = din("norm_w", (UV,))
    w_out = din("w_out", (VALUE_DIM, D))
    gate_wt = din("gate_wt", (D, NE))
    w1t = din("w1t", (NE, D, NI))
    w2t = din("w2t", (NE, NI, D))
    w3t = din("w3t", (NE, D, NI))
    ws1t = din("ws1t", (D, NSH))
    ws2t = din("ws2t", (NSH, D))
    ws3t = din("ws3t", (D, NSH))

    out = dout_("out", (t_own, D))
    xn_hbm = dscratch("xn_hbm", (NT, 128, D))
    attn_part = dscratch("attn_part", (8, t_own, D))

    dbg = {}
    for name, shape in debug_out:
        dbg[name] = dout_("dbg_" + name, shape)

    sb = {}

    def sbuf(name, shape, dtype=F32):
        h = nc.sbuf_tensor(name, list(shape), dtype)
        keep.append(h)
        t = h.__enter__()
        sb[name] = t
        return t

    ps = []
    for i in range(8):
        h = nc.psum_tensor(f"psum{i}", [128, 512], F32)
        keep.append(h)
        ps.append(h.__enter__())
    ps_rot = [0]

    def next_bank():
        ps_rot[0] = (ps_rot[0] + 1) % 8
        return ps[ps_rot[0]], f"psum{ps_rot[0]}"

    ones_col = sbuf("ones_col", (128, 1))
    ones128 = sbuf("ones128", (128, 128))
    ident128 = sbuf("ident128", (128, 128))

    cscal = {}

    def const_col(val):
        if val not in cscal:
            idx = len(cscal)
            t = sbuf(f"cconst{idx}", (128, 1))
            em.op("pool", lambda e, a=t, v=val: e.memset(a[:], v),
                  writes=[f"cconst{idx}"])
            cscal[val] = (t, f"cconst{idx}")
        t, k = cscal[val]
        return t[:, 0:1], k

    em.op("pool", lambda e: e.memset(ones_col[:], 1.0), writes=["ones_col"])
    em.op("pool", lambda e: e.memset(ones128[:], 1.0), writes=["ones128"])
    em.op("pool", lambda e: e.affine_select(
        ident128[:], ones128[:], [[-1, 128]], ALU.is_equal, 0.0,
        channel_multiplier=1), reads=["ones128"], writes=["ident128"])

    wkv_sb = sbuf("wkv_sb", (128, 16384))
    wq_sb = sbuf("wq_sb", (128, 2048))
    xn_g = sbuf("xn_g", (128, 8 * 512))
    kvf = sbuf("kvf", (128, 6 * 512))

    x_tm = [wkv_sb[0:128, i * D:(i + 1) * D] for i in range(2)]
    ln_scr = wkv_sb[0:128, 2 * D:3 * D]
    ln_s = [sbuf(f"ln_s{i}", (128, 4)) for i in range(2)]
    xn_st = [wkv_sb[0:128, (3 + i) * D:(4 + i) * D] for i in range(2)]

    for it in range(NT):
        bi = it % 2
        xt, kx = x_tm[bi], f"x_tm{bi}"
        st, kst = xn_st[bi], f"xn_st{bi}"
        lt, kl = ln_s[bi], f"ln_s{bi}"
        em.op("sp", lambda e, a=xt, i0=it: e.dma_start(
            out=a[:], in_=hidden[i0 * 128:(i0 + 1) * 128, :]),
            writes=[kx], dma=f"xld{bi}")
        em.op("act", lambda e, a=xt, l=lt: e.activation(
            out=ln_scr[:], in_=a[:], func=AF.Copy, accum_out=l[:, 0:1]),
            reads=[kx], writes=["ln_scr", kl])
        em.op("act", lambda e, a=xt, l=lt: e.activation(
            out=ln_scr[:], in_=a[:], func=AF.Square, accum_out=l[:, 1:2]),
            reads=[kx], writes=["ln_scr", kl])
        em.op("dve", lambda e, l=lt: e.tensor_scalar_mul(
            l[:, 2:3], l[:, 0:1], 1.0 / D), reads=[kl], writes=[kl])
        em.op("dve", lambda e, l=lt: e.scalar_tensor_tensor(
            l[:, 3:4], l[:, 2:3], -1.0, l[:, 2:3], ALU.mult, ALU.mult),
            reads=[kl], writes=[kl])
        em.op("dve", lambda e, l=lt: e.scalar_tensor_tensor(
            l[:, 3:4], l[:, 1:2], 1.0 / D, l[:, 3:4], ALU.mult, ALU.add),
            reads=[kl], writes=[kl])
        eps_ap, eps_k = const_col(EPS_LN)
        em.op("act", lambda e, l=lt, b_=eps_ap: e.activation(
            out=l[:, 3:4], in_=l[:, 3:4], func=AF.Sqrt, bias=b_),
            reads=[kl, eps_k], writes=[kl])
        em.op("dve", lambda e, l=lt: e.reciprocal(l[:, 3:4], l[:, 3:4]),
              reads=[kl], writes=[kl])
        em.op("dve", lambda e, a=xt, l=lt: e.tensor_scalar(
            a[:], a[:], l[:, 2:3], l[:, 3:4], ALU.subtract, ALU.mult),
            reads=[kx, kl], writes=[kx])
        for dc in range(DCH):
            bank, bk = next_bank()
            em.op("pe", lambda e, a=xt, d0=dc, b=bank: e.transpose(
                b[:, 0:128], a[:, d0 * 128:(d0 + 1) * 128], ident128[:]),
                reads=[kx, "ident128"], writes=[bk])
            em.op("dve", lambda e, d0=dc, b=bank, s=st: e.tensor_copy(
                s[:, d0 * 128:(d0 + 1) * 128], b[:, 0:128]),
                reads=[bk], writes=[kst])
        em.op("sp", lambda e, s=st, i0=it: e.dma_start(
            out=xn_hbm[i0], in_=s[:]),
            reads=[kst], writes=["xn_hbm"], dma="xnsp")

    em.barrier()
    TG = 512
    assert t_pre % TG == 0 and t_own % TG == 0
    NTG = t_all // TG
    NCH = t_all // CHUNK
    CPG = TG // CHUNK
    TPG = TG // 128
    NG = 8
    LN2_AK = float(np.log(AK) * 0.5)

    zeros64 = sbuf("zeros64", (64, 64))
    em.op("pool", lambda e: e.memset(zeros64[:], 0.0), writes=["zeros64"])
    wba = wq_sb[0:128, 0:DCH * 64]
    for dc in range(DCH):
        em.op("sp", lambda e, d0=dc: e.dma_start(
            out=wba[:, d0 * 64:d0 * 64 + 32],
            in_=w_b[d0 * 128:(d0 + 1) * 128, :]), writes=["wba"], dma="wba")
        em.op("sp", lambda e, d0=dc: e.dma_start(
            out=wba[:, d0 * 64 + 32:d0 * 64 + 64],
            in_=w_a[d0 * 128:(d0 + 1) * 128, :]), writes=["wba"], dma="wba")
    dtb = sbuf("dtb", (32, 4))
    em.op("sp", lambda e: e.dma_start(out=dtb[0:32, 0:1], in_=dt_bias),
          writes=["dtb"], dma="dtb")
    em.op("sp", lambda e: e.dma_start(out=dtb[0:32, 1:2], in_=a_log),
          writes=["dtb"], dma="dtb")
    em.op("act", lambda e: e.activation(out=dtb[0:32, 2:3], in_=dtb[0:32, 1:2],
          func=AF.Exp), reads=["dtb"], writes=["dtb"])
    em.op("dve", lambda e: e.tensor_scalar_mul(dtb[0:32, 2:3], dtb[0:32, 2:3],
          -1.0), reads=["dtb"], writes=["dtb"])

    conv_sb = sbuf("conv_sb", (128, 64 * 4))
    for cc in range(64):
        em.op("sp", lambda e, c0=cc: e.dma_start(
            out=conv_sb[:, c0 * 4:(c0 + 1) * 4],
            in_=conv_w[c0 * 128:(c0 + 1) * 128, :]), writes=["conv_sb"], dma="conv")

    nw_row = sbuf("nw_row", (1, UV))
    nw_tile = sbuf("nw_tile", (128, UV))
    em.op("sp", lambda e: e.dma_start(out=nw_row[0:1, :], in_=norm_w),
          writes=["nw_row"], dma="nwr")
    bank, bk = next_bank()
    em.op("pe", lambda e, b=bank: e.matmul(b[0:128, 0:UV], ones128[0:1, :],
          nw_row[0:1, :], start=True, stop=True),
          reads=["nw_row", "ones128"], writes=[bk])
    em.op("dve", lambda e, b=bank: e.tensor_copy(nw_tile[:], b[0:128, 0:UV]),
          reads=[bk], writes=["nw_tile"])

    gj_hm = sbuf("gj_hm", (32, t_all))
    p1_hm = sbuf("p1_hm", (32, t_all))
    lnsk4 = sbuf("lnsk4", (4, 512))
    lnsq4 = sbuf("lnsq4", (4, 512))
    egl_hm = sbuf("egl_hm", (32, NCH))
    gj_tm = sbuf("gj_tm", (64, 32 * 8))
    p1_tm = sbuf("p1_tm", (64, 32 * 8))
    p3_tm = sbuf("p3_tm", (64, 32 * 8))
    b_tm = sbuf("b_tm", (64, 32 * 8))
    lnsk_tm = sbuf("lnsk_tm", (64, 4 * 8))
    p3s_hm = sbuf("p3s_hm", (32, 512))
    em.op("pool", lambda e: e.memset(lnsk4[:], 0.0), writes=["lnsk4"])
    em.op("pool", lambda e: e.memset(lnsq4[:], 0.0), writes=["lnsq4"])
    em.op("pool", lambda e: e.memset(lnsk_tm[:], 0.0), writes=["lnsk_tm"])

    fam = kvf[0:32, 0:6 * TG // 1][0:32, 0:3072]
    fam = kvf[0:32, 0:3072]
    one_ap, one_k = const_col(1.0)
    epsl2_ap, epsl2_k = const_col(EPS_L2)
    epsrms_ap, epsrms_k = const_col(EPS_RMS)
    negln_ap, negln_k = const_col(-LN2_AK)

    def load_xn_half(r, tg0, n_tiles=TPG, key="xn_g"):
        for dcl in range(8):
            dc = 8 * r + dcl
            for ti in range(n_tiles):
                tile = tg0 // 128 + ti
                em.op(("sp", "act")[dcl % 2],
                    lambda e, d0=dc, dl=dcl, t0=tile, i0=ti: e.dma_start(
                    out=xn_g[:, dl * TG + i0 * 128:dl * TG + i0 * 128 + 128],
                    in_=xn_hbm[t0, :, d0 * 128:(d0 + 1) * 128]),
                    reads=["xn_hbm"], writes=[key + str(dcl % 2)], dma="xng" + str(dcl % 2))

    for tgi in range(NTG):
        tg0 = tgi * TG
        pb, kpb = next_bank()
        pa, kpa = next_bank()
        for r in range(2):
            load_xn_half(r, tg0)
            for dcl in range(8):
                dc = 8 * r + dcl
                em.op("pe", lambda e, b=pb, d0=dc, dl=dcl, st=(r == 0 and dcl == 0),
                      sp=(r == 1 and dcl == 7): e.matmul(
                      b[0:32, 0:TG], wba[:, d0 * 64:d0 * 64 + 32],
                      xn_g[:, dl * TG:(dl + 1) * TG], start=st, stop=sp),
                      reads=["wba", "xn_g0", "xn_g1"], writes=[kpb])
                em.op("pe", lambda e, b=pa, d0=dc, dl=dcl, st=(r == 0 and dcl == 0),
                      sp=(r == 1 and dcl == 7): e.matmul(
                      b[0:32, 0:TG], wba[:, d0 * 64 + 32:d0 * 64 + 64],
                      xn_g[:, dl * TG:(dl + 1) * TG], start=st, stop=sp),
                      reads=["wba", "xn_g0", "xn_g1"], writes=[kpa])
        b_s, a_s = fam[0:32, 0:TG], fam[0:32, TG:2 * TG]
        bet = fam[0:32, 2 * TG:3 * TG]
        u_s = fam[0:32, 3 * TG:4 * TG]
        l_s = fam[0:32, 4 * TG:5 * TG]
        g_s = fam[0:32, 5 * TG:6 * TG]
        em.op("dve", lambda e, b=pb: e.tensor_copy(b_s, b[0:32, 0:TG]),
              reads=[kpb], writes=["famb"])
        em.op("dve", lambda e, b=pa: e.tensor_copy(a_s, b[0:32, 0:TG]),
              reads=[kpa], writes=["fama"])
        em.op("act", lambda e: e.activation(out=bet, in_=b_s, func=AF.Sigmoid),
              reads=["famb"], writes=["fambet"])
        em.op("act", lambda e: e.activation(out=b_s, in_=bet, func=AF.Ln),
              reads=["fambet"], writes=["famb"])
        em.op("dve", lambda e: e.tensor_single_scalar(u_s, a_s,
              dtb[0:32, 0:1], ALU.add), reads=["fama", "dtb"], writes=["famu"])
        em.op("act", lambda e: e.activation(out=a_s, in_=u_s, func=AF.Abs),
              reads=["famu"], writes=["fama"])
        em.op("act", lambda e: e.activation(out=l_s, in_=a_s, func=AF.Exp,
              scale=-1.0), reads=["fama"], writes=["faml"])
        em.op("act", lambda e, b_=one_ap: e.activation(out=l_s, in_=l_s,
              func=AF.Ln, bias=b_[0:32, :]), reads=["faml", one_k], writes=["faml"])
        em.op("dve", lambda e: e.tensor_scalar_max(u_s, u_s, 0.0),
              reads=["famu"], writes=["famu"])
        em.op("dve", lambda e: e.tensor_add(u_s, u_s, l_s),
              reads=["famu", "faml"], writes=["famu"])
        em.op("dve", lambda e: e.tensor_single_scalar(g_s, u_s, dtb[0:32, 2:3],
              ALU.mult), reads=["famu", "dtb"], writes=["famg"])
        for c in range(CPG):
            em.op("dve", lambda e, c0=c, t0=tg0: e.tensor_tensor_scan(
                gj_hm[:, t0 + c0 * 64:t0 + (c0 + 1) * 64],
                fam[0:32, 5 * TG + c0 * 64:5 * TG + (c0 + 1) * 64],
                zeros64[0:32, :], 0.0, ALU.add, ALU.add),
                reads=["famg", "zeros64"], writes=["gj_hm"])
        em.op("dve", lambda e, t0=tg0: e.tensor_add(p1_hm[:, t0:t0 + TG],
              gj_hm[:, t0:t0 + TG], b_s), reads=["gj_hm", "famb"],
              writes=["p1_hm"])
        em.op("act", lambda e, t0=tg0: e.activation(
            out=egl_hm[:, t0 // 64:t0 // 64 + CPG],
            in_=gj_hm[:, t0 + 63:t0 + TG:64], func=AF.Exp),
            reads=["gj_hm"], writes=["egl_hm"])

    em.barrier()
    iden64 = sbuf("iden64", (64, 64))
    masks = sbuf("masks", (64, 6 * 64))
    negones = sbuf("negones", (128, 128))
    em.op("pool", lambda e: e.memset(negones[:], -1.0), writes=["negones"])
    em.op("pool", lambda e: e.affine_select(iden64[:], ones128[0:64, 0:64],
          [[-1, 64]], ALU.is_equal, 0.0, channel_multiplier=1),
          reads=["ones128"], writes=["iden64"])
    mask_defs = [
        (0, ALU.is_gt, -1, 1, False),
        (1, ALU.is_gt, -1, 1, True),
        (2, ALU.is_gt, 1, -1, False),
        (3, ALU.is_gt, 1, -1, True),
        (4, ALU.is_ge, 1, -1, False),
        (5, ALU.is_ge, 1, -1, True),
    ]
    for slot, cmp, stp, mlt, neg in mask_defs:
        src_t = negones if neg else ones128
        src_k = "negones" if neg else "ones128"
        em.op("pool", lambda e, s0=slot, c0=cmp, sa=src_t, st0=stp, m0=mlt:
            e.affine_select(
            masks[:, s0 * 64:(s0 + 1) * 64], sa[0:64, 0:64], [[st0, 64]], c0,
            0.0, channel_multiplier=m0), reads=[src_k], writes=["masks"])
    m01_B, mneg_B = masks[:, 0:64], masks[:, 64:128]
    m01_Bt, mneg_Bt = masks[:, 128:192], masks[:, 192:256]
    m01_iT, mneg_iT = masks[:, 256:320], masks[:, 320:384]

    sel_sb = sbuf("sel_sb", (32, 4 * 128))
    selg_sb = sbuf("selg_sb", (4, 4 * 128))
    selgn_sb = sbuf("selgn_sb", (4, 4 * 128))
    kmask = sbuf("kmask", (128, 2 * 32))
    ext = sbuf("ext", (128, 8 * (TG + 4)))
    EXTW = TG + 4
    qf = sbuf("qf", (128, 2 * TG))
    z_sb = sbuf("z_sb", (128, 4 * 512))
    wz_buf = [sbuf(f"wz_buf{i}", (128, 512)) for i in range(2)]
    wout_buf = sbuf("wout_buf", (128, 512))
    S_sb = sbuf("S_sb", (128, 4 * 128))
    eglrep = sbuf("eglrep", (128, max(4 * NCH, 96)))
    dtile_s = [sbuf(f"dtileS{s}", (64, 64)) for s in range(2)]
    Bp_s = [[sbuf(f"BpS{s}_{i}", (64, 64)) for i in range(2)] for s in range(2)]
    Btp_s = [[sbuf(f"BtpS{s}_{i}", (64, 64)) for i in range(2)] for s in range(2)]
    Rt_s = [sbuf(f"RtS{s}", (64, 64)) for s in range(2)]
    intraT_s = [sbuf(f"intraTS{s}", (64, 64)) for s in range(2)]
    kts_s = [sbuf(f"ktsS{s}", (64, 128)) for s in range(2)]
    u1_s = [sbuf(f"u1S{s}", (64, 128)) for s in range(2)]
    Pt_s = [sbuf(f"PtS{s}", (64, 128)) for s in range(2)]
    vnew_s = [sbuf(f"vnewS{s}", (64, 128)) for s in range(2)]
    qs_s = [sbuf(f"qsS{s}", (128, 64)) for s in range(2)]
    qsc_s = [sbuf(f"qscS{s}", (128, 64)) for s in range(2)]
    e1b = sbuf("e1b", (64, 4))
    e5b = sbuf("e5b", (64, 4))
    xg = sbuf("xg", (128, 128))
    rmsc = sbuf("rmsc", (128, 2))
    normed_sb = sbuf("normed_sb", (128, 512))
    part_sb = sbuf("part_sb", (128, 512))
    k2s = part_sb[0:128, 0:512]
    normT = sbuf("normT", (128, 512))
    zsl = sbuf("zsl", (128, 128))

    def emit_group(g):
        sfx = f"_g{g}"
        kv_cols = ([KEY_DIM + (2 * g + j) * 128 for j in range(2)] +
                   [2 * KEY_DIM + (4 * g + h) * 128 for h in range(4)])
        q_cols = [(2 * g + j) * 128 for j in range(2)]
        for dc in range(DCH):
            for ct, col in enumerate(kv_cols):
                qi = (dc * 6 + ct) % 4
                em.op(("sp", "act", "pool", "sp")[qi],
                    lambda e, d0=dc, c0=col, t0=ct: e.dma_start(
                    out=wkv_sb[:, (d0 * 6 + t0) * 128:(d0 * 6 + t0 + 1) * 128],
                    in_=w_qkv[d0 * 128:(d0 + 1) * 128, c0:c0 + 128]),
                    writes=[f"wkv_sb{qi}"], dma=f"wkv{qi}")
        for hl in range(4):
            em.op("pool", lambda e, h0=hl: e.affine_select(
                sel_sb[:, h0 * 128:(h0 + 1) * 128], ones128[0:32, :],
                [[0, 128]], ALU.is_equal, 0.0, base=-(4 * g + h0),
                channel_multiplier=1), reads=["ones128"], writes=["sel_sb"])
            em.op("pool", lambda e, h0=hl: e.affine_select(
                selgn_sb[0:4, h0 * 128:(h0 + 1) * 128], negones[0:4, :],
                [[0, 128]], ALU.is_equal, 0.0, base=-h0,
                channel_multiplier=1), reads=["negones"], writes=["selgn_sb"])
        for hl in range(4):
            em.op("pool", lambda e, h0=hl: e.affine_select(
                selg_sb[0:4, h0 * 128:(h0 + 1) * 128], ones128[0:4, :],
                [[0, 128]], ALU.is_equal, 0.0, base=-h0,
                channel_multiplier=1), reads=["ones128"], writes=["selg_sb"])
        for j in range(2):
            c0 = 2 * j
            em.op("pool", lambda e, cc=c0, j0=j: e.affine_select(
                kmask[:, j0 * 32:(j0 + 1) * 32], ones128[:, 0:32], [[1, 32]],
                ALU.is_ge, 0.0, base=-cc, channel_multiplier=0),
                reads=["ones128"], writes=["kmask"])
            em.op("pool", lambda e, cc=c0, j0=j: e.affine_select(
                kmask[:, j0 * 32:(j0 + 1) * 32], kmask[:, j0 * 32:(j0 + 1) * 32],
                [[-1, 32]], ALU.is_ge, 0.0, base=cc + 1, channel_multiplier=0),
                reads=["kmask"], writes=["kmask"])
        for hl in range(4):
            bank, bk = next_bank()
            em.op("pe", lambda e, h0=hl, b=bank: e.matmul(
                b[0:128, 0:NCH], sel_sb[:, h0 * 128:(h0 + 1) * 128],
                egl_hm[0:32, :], start=True, stop=True),
                reads=["sel_sb", "egl_hm"], writes=[bk])
            em.op("dve", lambda e, h0=hl, b=bank: e.tensor_copy(
                eglrep[:, h0 * NCH:(h0 + 1) * NCH], b[0:128, 0:NCH]),
                reads=[bk], writes=["eglrep"])
        em.op("pool", lambda e: e.memset(S_sb[:], 0.0), writes=["S_sb"])
        em.op("pool", lambda e: e.memset(ext[:], 0.0), writes=["ext"])

        NPRE_TG = t_pre // TG

        def conv_one(ct_slot, dst_ap, dst_k, ch_glob, psum_ap, psum_k):
            e0 = ext[:, ct_slot * EXTW:(ct_slot + 1) * EXTW]
            em.op("act", lambda e: e.activation(out=e0[:, 3:3 + TG],
                  in_=psum_ap, func=AF.Copy), reads=[psum_k], writes=["ext"])
            em.op("act", lambda e: e.activation(out=dst_ap, in_=e0[:, 0:TG],
                  func=AF.Copy, scale=conv_sb[:, ch_glob * 4:ch_glob * 4 + 1]),
                  reads=["ext", "conv_sb"], writes=[dst_k])
            for j in range(1, KCONV):
                em.op("dve", lambda e, j0=j: e.scalar_tensor_tensor(
                    dst_ap, e0[:, j0:j0 + TG],
                    conv_sb[:, ch_glob * 4 + j0:ch_glob * 4 + j0 + 1], dst_ap,
                    ALU.mult, ALU.add), reads=["ext", "conv_sb", dst_k],
                    writes=[dst_k])
            em.op("act", lambda e: e.activation(out=dst_ap, in_=dst_ap,
                  func=AF.Silu), reads=[dst_k], writes=[dst_k])
            em.op("dve", lambda e: e.tensor_copy(e0[:, 0:3], e0[:, TG:TG + 3]),
                  reads=["ext"], writes=["ext"])

        for tgi in range(NTG):
            tg0 = tgi * TG
            own_tg = tg0 >= t_pre
            warm_tg = (t_pre > 0 and tgi == NPRE_TG - 1)
            banks = []
            nct = 6 + (2 if own_tg else 0)
            for _ in range(nct + (2 if warm_tg else 0)):
                banks.append(next_bank())
            for r in range(2):
                load_xn_half(r, tg0)
                if own_tg or warm_tg:
                    for dcl in range(8):
                        for tq, col in enumerate(q_cols):
                            em.op("sp", lambda e, d0=8 * r + dcl, dl=dcl,
                                  c0=col, t0=tq: e.dma_start(
                                  out=wq_sb[:, (dl * 2 + t0) * 128:(dl * 2 + t0 + 1) * 128],
                                  in_=w_qkv[d0 * 128:(d0 + 1) * 128, c0:c0 + 128]),
                                  writes=["wq_sb"], dma="wq")
                for dcl in range(8):
                    dc = 8 * r + dcl
                    st = (r == 0 and dcl == 0)
                    sp = (r == 1 and dcl == 7)
                    for ct in range(6):
                        b0, bk0 = banks[ct]
                        em.op("pe", lambda e, d0=dc, dl=dcl, t0=ct, b=b0,
                              s1=st, s2=sp: e.matmul(
                              b[:, 0:TG], wkv_sb[:, (d0 * 6 + t0) * 128:(d0 * 6 + t0 + 1) * 128],
                              xn_g[:, dl * TG:(dl + 1) * TG], start=s1, stop=s2),
                              reads=["wkv_sb0", "wkv_sb1", "wkv_sb2", "wkv_sb3", "xn_g0", "xn_g1"], writes=[bk0])
                    if own_tg:
                        for tq in range(2):
                            b0, bk0 = banks[6 + tq]
                            em.op("pe", lambda e, dl=dcl, t0=tq, b=b0,
                                  s1=st, s2=sp: e.matmul(
                                  b[:, 0:TG], wq_sb[:, (dl * 2 + t0) * 128:(dl * 2 + t0 + 1) * 128],
                                  xn_g[:, dl * TG:(dl + 1) * TG], start=s1, stop=s2),
                                  reads=["wq_sb", "xn_g0", "xn_g1"], writes=[bk0])
                    if warm_tg:
                        for tq in range(2):
                            b0, bk0 = banks[6 + tq]
                            em.op("pe", lambda e, dl=dcl, t0=tq, b=b0,
                                  s1=st, s2=sp: e.matmul(
                                  b[:, 0:128], wq_sb[:, (dl * 2 + t0) * 128:(dl * 2 + t0 + 1) * 128],
                                  xn_g[:, dl * TG + 3 * 128:dl * TG + 4 * 128],
                                  start=s1, stop=s2),
                                  reads=["wq_sb", "xn_g0", "xn_g1"], writes=[bk0])
            for ct in range(6):
                ch_glob = ((HK + 2 * g + ct) if ct < 2
                           else (2 * HK + 4 * g + ct - 2))
                b0, bk0 = banks[ct]
                conv_one(ct, kvf[:, ct * TG:(ct + 1) * TG], "kvf", ch_glob,
                         b0[:, 0:TG], bk0)
            if own_tg:
                for tq in range(2):
                    b0, bk0 = banks[6 + tq]
                    conv_one(6 + tq, qf[:, tq * TG:(tq + 1) * TG], "qf",
                             2 * g + tq, b0[:, 0:TG], bk0)
            if warm_tg:
                for tq in range(2):
                    b0, bk0 = banks[6 + tq]
                    slot = 6 + tq
                    e0 = ext[:, slot * EXTW:(slot + 1) * EXTW]
                    em.op("act", lambda e, b=b0, e_=e0: e.activation(
                        out=e_[:, 3:3 + 128], in_=b[:, 0:128], func=AF.Copy),
                        reads=[bk0], writes=["ext"])
                    em.op("dve", lambda e, e_=e0: e.tensor_copy(
                        e_[:, 0:3], e_[:, 128:131]), reads=["ext"], writes=["ext"])
            pks, kpks = next_bank()
            for j in range(2):
                em.op("act", lambda e, j0=j: e.activation(
                    out=k2s[:], in_=kvf[:, j0 * TG:(j0 + 1) * TG],
                    func=AF.Square), reads=["kvf"], writes=["part_sb"])
                em.op("pe", lambda e, j0=j, b=pks: e.matmul(
                    b[0:32, 0:TG], kmask[:, j0 * 32:(j0 + 1) * 32],
                    k2s[:], start=(j0 == 0), stop=(j0 == 1)),
                    reads=["kmask", "part_sb"], writes=[kpks])
            em.op("act", lambda e, b=pks: e.activation(
                out=lnsk4[0:4, 0:TG],
                in_=b[0:4, 0:TG], func=AF.Ln,
                bias=epsl2_ap[0:4, :]),
                reads=[kpks, epsl2_k], writes=["lnsk4"])
            em.op("dve", lambda e: e.tensor_scalar_mul(
                lnsk4[0:4, 0:TG],
                lnsk4[0:4, 0:TG], -0.5),
                reads=["lnsk4"], writes=["lnsk4"])
            if own_tg:
                pqs, kpqs = next_bank()
                for j in range(2):
                    em.op("act", lambda e, j0=j: e.activation(
                        out=k2s[:], in_=qf[:, j0 * TG:(j0 + 1) * TG],
                        func=AF.Square), reads=["qf"], writes=["part_sb"])
                    em.op("pe", lambda e, j0=j, b=pqs: e.matmul(
                        b[0:32, 0:TG], kmask[:, j0 * 32:(j0 + 1) * 32],
                        k2s[:], start=(j0 == 0), stop=(j0 == 1)),
                        reads=["kmask", "part_sb"], writes=[kpqs])
                em.op("act", lambda e, b=pqs: e.activation(
                    out=lnsq4[0:4, 0:TG],
                    in_=b[0:4, 0:TG], func=AF.Ln,
                    bias=epsl2_ap[0:4, :]),
                    reads=[kpqs, epsl2_k], writes=["lnsq4"])
                em.op("dve", lambda e: e.tensor_scalar_mul(
                    lnsq4[0:4, 0:TG],
                    lnsq4[0:4, 0:TG], -0.5),
                    reads=["lnsq4"], writes=["lnsq4"])
            if own_tg:
                zbanks = [next_bank() for _ in range(4)]
                for r in range(2):
                    load_xn_half(r, tg0, key="xn_g")
                    for dcl in range(8):
                        dc = 8 * r + dcl
                        wb_i = dc % 2
                        em.op("sp", lambda e, d0=dc, w0=wb_i: e.dma_start(
                            out=wz_buf[w0][:],
                            in_=w_z[d0 * 128:(d0 + 1) * 128,
                                    4 * g * 128:(4 * g + 4) * 128]),
                            writes=[f"wz{wb_i}"], dma=f"wz{wb_i}")
                        for ti in range(4):
                            zb, zbk = zbanks[ti]
                            em.op("pe", lambda e, d0=dc, dl=dcl, i0=ti, b=zb,
                                  w0=wb_i, s1=(r == 0 and dcl == 0),
                                  s2=(r == 1 and dcl == 7): e.matmul(
                                  b[:, 0:512],
                                  xn_g[:, dl * TG + i0 * 128:dl * TG + (i0 + 1) * 128],
                                  wz_buf[w0][:], start=s1, stop=s2),
                                  reads=["xn_g0", "xn_g1", f"wz{wb_i}"], writes=[zbk])
                for ti in range(4):
                    zb, zbk = zbanks[ti]
                    em.op("dve", lambda e, i0=ti, b=zb: e.tensor_copy(
                        z_sb[:, i0 * 512:(i0 + 1) * 512], b[:, 0:512]),
                        reads=[zbk], writes=["z_sb"])
            for c in range(CPG):
                em.op("dve", lambda e, c0=c, t0=tg0: e.tensor_scalar(
                    p3s_hm[:, c0 * 64:(c0 + 1) * 64],
                    gj_hm[:, t0 + c0 * 64:t0 + (c0 + 1) * 64], -1.0,
                    gj_hm[:, t0 + c0 * 64 + 63:t0 + c0 * 64 + 64],
                    ALU.mult, ALU.add), reads=["gj_hm"], writes=["p3s_hm"])
                for src_ap, src_k, dst, dst_k in (
                    (gj_hm[:, tg0 + c * 64:tg0 + (c + 1) * 64], "gj_hm", gj_tm, "gj_tm"),
                    (p1_hm[:, tg0 + c * 64:tg0 + (c + 1) * 64], "p1_hm", p1_tm, "p1_tm"),
                    (p3s_hm[:, c * 64:(c + 1) * 64], "p3s_hm", p3_tm, "p3_tm"),
                ):
                    bank, bk = next_bank()
                    em.op("pe", lambda e, a=src_ap, b=bank: e.transpose(
                        b[0:64, 0:32], a, ident128[0:32, 0:32]),
                        reads=[src_k, "ident128"], writes=[bk])
                    em.op("dve", lambda e, b=bank, d=dst, c0=c, dk=dst_k: e.tensor_copy(
                        d[0:64, c0 * 32:(c0 + 1) * 32], b[0:64, 0:32]),
                        reads=[bk], writes=[dst_k])
                bank, bk = next_bank()
                em.op("pe", lambda e, c0=c, b=bank: e.transpose(
                    b[0:64, 0:4], lnsk4[0:4, c0 * 64:(c0 + 1) * 64],
                    ident128[0:4, 0:4]), reads=["lnsk4", "ident128"], writes=[bk])
                em.op("dve", lambda e, b=bank, c0=c: e.tensor_copy(
                    lnsk_tm[0:64, c0 * 4:(c0 + 1) * 4], b[0:64, 0:4]),
                    reads=[bk], writes=["lnsk_tm"])
                em.op("dve", lambda e, c0=c: e.tensor_sub(
                    b_tm[0:64, c0 * 32:(c0 + 1) * 32],
                    p1_tm[0:64, c0 * 32:(c0 + 1) * 32],
                    gj_tm[0:64, c0 * 32:(c0 + 1) * 32]),
                    reads=["p1_tm", "gj_tm"], writes=["b_tm"])
                em.op("act", lambda e, c0=c: e.activation(
                    out=b_tm[0:64, c0 * 32:(c0 + 1) * 32],
                    in_=b_tm[0:64, c0 * 32:(c0 + 1) * 32], func=AF.Exp),
                    reads=["b_tm"], writes=["b_tm"])
            for c in range(CPG):
                cg = tg0 // 64 + c
                own_ch = cg * 64 >= t_pre
                ti = c // 2
                co2 = c % 2
                em.op("dve", lambda e, c0=c: e.tensor_add(e1b[0:64, 0:4],
                      p1_tm[0:64, c0 * 32 + 4 * g:c0 * 32 + 4 * g + 4],
                      lnsk_tm[0:64, c0 * 4:(c0 + 1) * 4]),
                      reads=["p1_tm", "lnsk_tm"], writes=["e1b"])
                em.op("act", lambda e: e.activation(out=e1b[0:64, 0:4],
                      in_=e1b[0:64, 0:4],
                      func=AF.Exp), reads=["e1b"], writes=["e1b"])
                em.op("dve", lambda e, c0=c: e.tensor_add(e5b[0:64, 0:4],
                      p3_tm[0:64, c0 * 32 + 4 * g:c0 * 32 + 4 * g + 4],
                      lnsk_tm[0:64, c0 * 4:(c0 + 1) * 4]),
                      reads=["p3_tm", "lnsk_tm"], writes=["e5b"])
                em.op("act", lambda e: e.activation(out=e5b[0:64, 0:4],
                      in_=e5b[0:64, 0:4],
                      func=AF.Exp), reads=["e5b"], writes=["e5b"])
                for hl in range(4):
                    emit_chunk(g, hl, cg, c, ti, co2, own_ch, tg0)
                if own_ch and co2 == 1:
                    flush_tile(g, tg0, ti)

    def emit_chunk(g, hl, cg, c, ti, co2, own_ch, tg0):
        h = 4 * g + hl
        j = hl // 2
        ksl = kvf[:, j * TG + c * 64:j * TG + (c + 1) * 64]
        vsl = kvf[:, (2 + hl) * TG + c * 64:(2 + hl) * TG + (c + 1) * 64]
        S_h = S_sb[:, hl * 128:(hl + 1) * 128]
        sk = f"S{hl}"
        gslice = gj_hm[:, cg * 64:(cg + 1) * 64]
        p1slice = p1_hm[:, cg * 64:(cg + 1) * 64]
        lkslice = lnsk4[0:4, c * 64:(c + 1) * 64]
        gjcol = gj_tm[0:64, c * 32 + h:c * 32 + h + 1]
        p1col = p1_tm[0:64, c * 32 + h:c * 32 + h + 1]
        lkcol = lnsk_tm[0:64, c * 4 + hl:c * 4 + hl + 1]
        bcol = b_tm[0:64, c * 32 + h:c * 32 + h + 1]
        e1col = e1b[0:64, hl:hl + 1]
        e5col = e5b[0:64, hl:hl + 1]
        eglcol = eglrep[:, hl * NCH + cg:hl * NCH + cg + 1]
        selp = sel_sb[:, hl * 128:hl * 128 + 64]
        selg = selg_sb[0:4, hl * 128:hl * 128 + 64]
        selgn = selgn_sb[0:4, hl * 128:hl * 128 + 64]
        sx = hl % 2
        dtile = dtile_s[sx]
        Bp, Btp, Rt = Bp_s[sx], Btp_s[sx], Rt_s[sx]
        kts, u1, Pt, vnew = kts_s[sx], u1_s[sx], Pt_s[sx], vnew_s[sx]
        kD, kB0, kB1, kBt0, kBt1, kR = (f"dtile{sx}", f"Bp{sx}_0", f"Bp{sx}_1",
                                        f"Btp{sx}_0", f"Btp{sx}_1", f"Rt{sx}")
        kkts, ku1, kPt, kvn = f"kts{sx}", f"u1{sx}", f"Pt{sx}", f"vnew{sx}"

        pg, kpg = next_bank()
        em.op("pe", lambda e, b=pg: e.matmul(b[0:64, 0:64], ksl, ksl,
              start=True, stop=True), reads=["kvf"], writes=[kpg])
        pB, kpB = next_bank()
        em.op("pe", lambda e, b=pB: (
            e.matmul(b[0:64, 0:64], selp, gslice, start=True, stop=False),
            e.matmul(b[0:64, 0:64], selgn, lkslice, start=False, stop=True))[-1],
            reads=["sel_sb", "selgn_sb", "gj_hm", "lnsk4"], writes=[kpB])
        em.op("dve", lambda e, b=pB: e.tensor_scalar(dtile[:], b[0:64, 0:64],
              p1col, lkcol, ALU.subtract, ALU.subtract),
              reads=[kpB, "p1_tm", "lnsk_tm"], writes=[kD])
        em.op("dve", lambda e: e.tensor_mul(dtile[:], dtile[:], mneg_B),
              reads=[kD, "masks"], writes=[kD])
        em.op("act", lambda e: e.activation(out=dtile[:], in_=dtile[:],
              func=AF.Exp), reads=[kD], writes=[kD])
        em.op("dve", lambda e: e.tensor_mul(dtile[:], dtile[:], mneg_B),
              reads=[kD, "masks"], writes=[kD])
        em.op("dve", lambda e, b=pg: e.tensor_mul(Bp[0][:], b[0:64, 0:64],
              dtile[:]), reads=[kpg, kD], writes=[kB0])
        pBt, kpBt = next_bank()
        em.op("pe", lambda e, b=pBt: (
            e.matmul(b[0:64, 0:64], selp, p1slice, start=True, stop=False),
            e.matmul(b[0:64, 0:64], selg, lkslice, start=False, stop=True))[-1],
            reads=["sel_sb", "selg_sb", "p1_hm", "lnsk4"], writes=[kpBt])
        em.op("dve", lambda e, b=pBt: e.tensor_scalar(dtile[:], b[0:64, 0:64],
              gjcol, lkcol, ALU.subtract, ALU.add),
              reads=[kpBt, "gj_tm", "lnsk_tm"], writes=[kD])
        em.op("dve", lambda e: e.tensor_mul(dtile[:], dtile[:], m01_Bt),
              reads=[kD, "masks"], writes=[kD])
        em.op("act", lambda e: e.activation(out=dtile[:], in_=dtile[:],
              func=AF.Exp), reads=[kD], writes=[kD])
        em.op("dve", lambda e: e.tensor_mul(dtile[:], dtile[:], mneg_Bt),
              reads=[kD, "masks"], writes=[kD])
        em.op("dve", lambda e, b=pg: e.tensor_mul(Btp[0][:], b[0:64, 0:64],
              dtile[:]), reads=[kpg, kD], writes=[kBt0])
        em.op("dve", lambda e: e.tensor_add(Rt[:], Btp[0][:], iden64[:]),
              reads=[kBt0, "iden64"], writes=[kR])
        cur = 0
        for step in range(5):
            nxt = 1 - cur
            pB2, kpB2 = next_bank()
            em.op("pe", lambda e, b=pB2, c0=cur: e.matmul(b[0:64, 0:64],
                  Btp[c0][:], Bp[c0][:], start=True, stop=True),
                  reads=[(kBt0 if cur == 0 else kBt1), (kB0 if cur == 0 else kB1)], writes=[kpB2])
            pBt2, kpBt2 = next_bank()
            em.op("pe", lambda e, b=pBt2, c0=cur: e.matmul(b[0:64, 0:64],
                  Bp[c0][:], Btp[c0][:], start=True, stop=True),
                  reads=[(kB0 if cur == 0 else kB1), (kBt0 if cur == 0 else kBt1)], writes=[kpBt2])
            em.op("dve", lambda e, b=pB2, c0=nxt: e.tensor_copy(Bp[c0][:],
                  b[0:64, 0:64]), reads=[kpB2], writes=[(kB0 if nxt == 0 else kB1)])
            em.op("dve", lambda e, b=pBt2, c0=nxt: e.tensor_copy(Btp[c0][:],
                  b[0:64, 0:64]), reads=[kpBt2], writes=[(kBt0 if nxt == 0 else kBt1)])
            pRu, kpRu = next_bank()
            em.op("pe", lambda e, b=pRu, c0=nxt: e.matmul(b[0:64, 0:64],
                  Bp[c0][:], Rt[:], start=True, stop=True),
                  reads=[(kB0 if nxt == 0 else kB1), kR], writes=[kpRu])
            em.op("dve", lambda e, b=pRu: e.tensor_add(Rt[:], Rt[:],
                  b[0:64, 0:64]), reads=[kpRu, kR], writes=[kR])
            cur = nxt
        pkt, kpkt = next_bank()
        em.op("pe", lambda e, b=pkt: e.transpose(b[0:64, 0:128], ksl,
              ident128[:]), reads=["kvf", "ident128"], writes=[kpkt])
        em.op("dve", lambda e, b=pkt: e.tensor_scalar(kts[:], b[0:64, 0:128],
              e5col, None, ALU.mult), reads=[kpkt, "e5b"], writes=[kkts])
        pP, kpP = next_bank()
        em.op("pe", lambda e, b=pP: e.matmul(b[0:64, 0:128], ksl, S_h,
              start=True, stop=True), reads=["kvf", sk], writes=[kpP])
        em.op("dve", lambda e, b=pP: e.tensor_scalar(u1[:], b[0:64, 0:128],
              e1col, None, ALU.mult), reads=[kpP, "e1b"], writes=[ku1])
        pvt, kpvt = next_bank()
        em.op("pe", lambda e, b=pvt: e.transpose(b[0:64, 0:128], vsl,
              ident128[:]), reads=["kvf", "ident128"], writes=[kpvt])
        em.op("dve", lambda e, b=pvt: e.scalar_tensor_tensor(Pt[:],
              b[0:64, 0:128], bcol, u1[:], ALU.mult, ALU.subtract),
              reads=[kpvt, "b_tm", ku1], writes=[kPt])
        pvn, kpvn = next_bank()
        em.op("pe", lambda e, b=pvn: e.matmul(b[0:64, 0:128], Rt[:], Pt[:],
              start=True, stop=True), reads=[kR, kPt], writes=[kpvn])
        em.op("dve", lambda e, b=pvn: e.tensor_copy(vnew[:], b[0:64, 0:128]),
              reads=[kpvn], writes=[kvn])
        if own_ch:
            emit_own(g, hl, cg, c, ti, co2, tg0, ksl, S_h, sk)
        pSu, kpSu = next_bank()
        em.op("pe", lambda e, b=pSu: e.matmul(b[0:128, 0:128], kts[:],
              vnew[:], start=True, stop=True), reads=[kkts, kvn],
              writes=[kpSu])
        em.op("dve", lambda e, b=pSu: e.scalar_tensor_tensor(S_h, S_h,
              eglcol, b[0:128, 0:128], ALU.mult, ALU.add),
              reads=[sk, "eglrep", kpSu], writes=[sk])

    def emit_own(g, hl, cg, c, ti, co2, tg0, ksl, S_h, sk):
        h = 4 * g + hl
        jq = hl // 2
        qsl = qf[:, jq * TG + c * 64:jq * TG + (c + 1) * 64]
        gslice = gj_hm[:, cg * 64:(cg + 1) * 64]
        lqslice = lnsq4[0:4, c * 64:(c + 1) * 64]
        gjcol = gj_tm[0:64, c * 32 + h:c * 32 + h + 1]
        lkcol = lnsk_tm[0:64, c * 4 + hl:c * 4 + hl + 1]
        selp = sel_sb[:, hl * 128:hl * 128 + 64]
        selp128 = sel_sb[:, hl * 128:(hl + 1) * 128]
        selg = selg_sb[0:4, hl * 128:hl * 128 + 64]
        selg128 = selg_sb[0:4, hl * 128:(hl + 1) * 128]
        sx = hl % 2
        dtile = dtile_s[sx]
        intraT, vnew = intraT_s[sx], vnew_s[sx]
        qs, qsc = qs_s[sx], qsc_s[sx]
        kD, kIT, kvn = f"dtile{sx}", f"intraT{sx}", f"vnew{sx}"
        kqs, kqsc = f"qs{sx}", f"qsc{sx}"
        pq, kpq = next_bank()
        em.op("pe", lambda e, b=pq: e.matmul(b[0:64, 0:64], ksl, qsl,
              start=True, stop=True), reads=["kvf", "qf"], writes=[kpq])
        pDi, kpDi = next_bank()
        em.op("pe", lambda e, b=pDi: (
            e.matmul(b[0:64, 0:64], selp, gslice, start=True, stop=False),
            e.matmul(b[0:64, 0:64], selg, lqslice, start=False, stop=True))[-1],
            reads=["sel_sb", "selg_sb", "gj_hm", "lnsq4"], writes=[kpDi])
        em.op("dve", lambda e, b=pDi: e.tensor_scalar(dtile[:], b[0:64, 0:64],
              gjcol, lkcol, ALU.subtract, ALU.add),
              reads=[kpDi, "gj_tm", "lnsk_tm"], writes=[kD])
        em.op("dve", lambda e: e.tensor_mul(dtile[:], dtile[:], m01_iT),
              reads=[kD, "masks"], writes=[kD])
        em.op("act", lambda e, b_=negln_ap: e.activation(out=dtile[:],
              in_=dtile[:], func=AF.Exp, bias=b_[0:64, :]),
              reads=[kD, negln_k], writes=[kD])
        em.op("dve", lambda e: e.tensor_mul(dtile[:], dtile[:], m01_iT),
              reads=[kD, "masks"], writes=[kD])
        em.op("dve", lambda e, b=pq: e.tensor_mul(intraT[:], b[0:64, 0:64],
              dtile[:]), reads=[kpq, kD], writes=[kIT])
        pQ, kpQ = next_bank()
        em.op("pe", lambda e, b=pQ: (
            e.matmul(b[0:128, 0:64], selp128, gslice, start=True, stop=False),
            e.matmul(b[0:128, 0:64], selg128, lqslice, start=False, stop=True))[-1],
            reads=["sel_sb", "selg_sb", "gj_hm", "lnsq4"], writes=[kpQ])
        em.op("act", lambda e, b=pQ, b_=negln_ap: e.activation(out=qsc[:],
              in_=b[0:128, 0:64], func=AF.Exp, bias=b_),
              reads=[kpQ, negln_k], writes=[kqsc])
        em.op("dve", lambda e: e.tensor_mul(qs[:], qsl, qsc[:]),
              reads=["qf", kqsc], writes=[kqs])
        pO, kpO = next_bank()
        em.op("pe", lambda e, b=pO, c2=co2: (
            e.matmul(b[64 * c2:64 * c2 + 64, 0:128], qs[:], S_h,
                     start=True, stop=False),
            e.matmul(b[64 * c2:64 * c2 + 64, 0:128], intraT[:], vnew[:],
                     start=False, stop=True))[-1],
            reads=[kqs, sk, kIT, kvn], writes=[kpO])
        em.op("act", lambda e, c2=co2, i0=ti, h0=hl: e.activation(
            out=zsl[64 * c2:64 * c2 + 64, 0:128],
            in_=z_sb[64 * c2:64 * c2 + 64,
                     i0 * 512 + h0 * 128:i0 * 512 + (h0 + 1) * 128],
            func=AF.Silu), reads=["z_sb"], writes=["zsl"])
        em.op("dve", lambda e, b=pO, c2=co2: e.tensor_mul(
            xg[64 * c2:64 * c2 + 64, :], b[64 * c2:64 * c2 + 64, 0:128],
            zsl[64 * c2:64 * c2 + 64, 0:128]),
            reads=[kpO, "zsl"], writes=["xg"])
        em.op("act", lambda e, c2=co2: e.activation(
            out=zsl[64 * c2:64 * c2 + 64, 0:128],
            in_=xg[64 * c2:64 * c2 + 64, :], func=AF.Square,
            accum_out=rmsc[64 * c2:64 * c2 + 64, 0:1]),
            reads=["xg"], writes=["zsl", "rmsc"])
        em.op("act", lambda e, c2=co2, b_=epsrms_ap: e.activation(
            out=rmsc[64 * c2:64 * c2 + 64, 1:2],
            in_=rmsc[64 * c2:64 * c2 + 64, 0:1], func=AF.Sqrt,
            scale=1.0 / UV, bias=b_[64 * c2:64 * c2 + 64, :]),
            reads=["rmsc", epsrms_k], writes=["rmsc"])
        em.op("dve", lambda e, c2=co2: e.reciprocal(
            rmsc[64 * c2:64 * c2 + 64, 1:2], rmsc[64 * c2:64 * c2 + 64, 1:2]),
            reads=["rmsc"], writes=["rmsc"])
        em.op("dve", lambda e, c2=co2: e.tensor_scalar(
            xg[64 * c2:64 * c2 + 64, :], xg[64 * c2:64 * c2 + 64, :],
            rmsc[64 * c2:64 * c2 + 64, 1:2], None, ALU.mult),
            reads=["xg", "rmsc"], writes=["xg"])
        em.op("dve", lambda e, c2=co2, h0=hl: e.tensor_mul(
            normed_sb[64 * c2:64 * c2 + 64, h0 * 128:(h0 + 1) * 128],
            xg[64 * c2:64 * c2 + 64, :], nw_tile[64 * c2:64 * c2 + 64, :]),
            reads=["xg", "nw_tile"], writes=["normed_sb"])

    def flush_tile(g, tg0, ti):
        trow0 = tg0 + ti * 128 - t_pre
        for hl in range(4):
            bank, bk = next_bank()
            em.op("pe", lambda e, h0=hl, b=bank: e.transpose(
                b[0:128, 0:128], normed_sb[:, h0 * 128:(h0 + 1) * 128],
                ident128[:]), reads=["normed_sb", "ident128"], writes=[bk])
            em.op("dve", lambda e, h0=hl, b=bank: e.tensor_copy(
                normT[:, h0 * 128:(h0 + 1) * 128], b[0:128, 0:128]),
                reads=[bk], writes=["normT"])
        obanks = [next_bank() for _ in range(4)]
        for hl in range(4):
            for dq in range(4):
                em.op("sp", lambda e, h0=hl, q0=dq: e.dma_start(
                    out=wout_buf[:],
                    in_=w_out[(4 * g + h0) * 128:(4 * g + h0 + 1) * 128,
                              q0 * 512:(q0 + 1) * 512]),
                    writes=["wout_buf"], dma="wol")
                ob, obk = obanks[dq]
                em.op("pe", lambda e, h0=hl, b=ob: e.matmul(
                    b[:, 0:512], normT[:, h0 * 128:(h0 + 1) * 128],
                    wout_buf[:, 0:512],
                    start=(h0 == 0), stop=(h0 == 3)),
                    reads=["normT", "wout_buf"], writes=[obk])
        for dq in range(4):
            ob, obk = obanks[dq]
            em.op("dve", lambda e, b=ob: e.tensor_copy(part_sb[:], b[:, 0:512]),
                  reads=[obk], writes=["part_sb"])
            em.op("pool", lambda e, q0=dq, t0=trow0: e.dma_start(
                out=attn_part[g, t0:t0 + 128, q0 * 512:(q0 + 1) * 512],
                in_=part_sb[:]), reads=["part_sb"], writes=["attn_part"],
                dma="partw")

    for g in range(NG):
        emit_group(g)
        em.barrier()

    em.barrier()
    NTO = t_own // 128
    TP0 = t_pre // 128

    fin_b = wout_buf[0:128, 0:512]
    gwt_sb = conv_sb[0:128, 0:DCH * 16]
    for dc in range(DCH):
        em.op("sp", lambda e, d0=dc: e.dma_start(
            out=gwt_sb[:, d0 * 16:(d0 + 1) * 16],
            in_=gate_wt[d0 * 128:(d0 + 1) * 128, :]), writes=["gwt_sb"],
            dma="gwt")
    comb_sb = normed_sb[0:128, 0:128]
    combT_h = [S_sb[0:16, 0:512], normT[0:16, 0:512]]
    rt_s = eglrep[0:128, max(4 * NCH, 96) - 96:max(4 * NCH, 96) - 16]
    rt_b = wz_buf[0][0:128, 0:256]
    rt_c = wz_buf[1][0:128, 0:256]
    iota_g = eglrep[0:128, max(4 * NCH, 96) - 16:max(4 * NCH, 96)]
    iota_e = normed_sb[0:128, 128:384]
    em.op("pool", lambda e: e.affine_select(
        iota_g[:], ones128[:, 0:16], [[1, 4], [-1, 4]], ALU.is_gt, 0.0,
        channel_multiplier=0), reads=["ones128"], writes=["iota_g"])
    em.op("pool", lambda e: e.memset(iota_e[:], 1.0), writes=["iota_e"])
    em.op("pool", lambda e: e.affine_select(
        iota_e[:], iota_e[:], [[1, 16], [-1, 16]],
        ALU.is_gt, 0.0, channel_multiplier=0),
        reads=["iota_e"], writes=["iota_e"])
    xn_b = z_sb[0:128, 0:D]
    assert NTO <= 8

    AX = mybir.AxisListType.X
    for it in range(NTO):
        em.op("sp", lambda e, i0=it: e.dma_start(
            out=xn_b[:], in_=xn_hbm[TP0 + i0]),
            reads=["xn_hbm"], writes=["xn_b"], dma="xnb")
        psc, kpsc = next_bank()
        for dc in range(DCH):
            em.op("pe", lambda e, d0=dc, b=psc: e.matmul(
                b[:, 0:16], xn_b[:, d0 * 128:(d0 + 1) * 128],
                gwt_sb[:, d0 * 16:(d0 + 1) * 16],
                start=(d0 == 0), stop=(d0 == DCH - 1)),
                reads=["xn_b", "gwt_sb"], writes=[kpsc])
        sc = rt_s[0:128, 0:16]
        gmx = rt_s[0:128, 16:20]
        cnt4 = rt_s[0:128, 20:24]
        msc = rt_s[0:128, 32:48]
        cnt16 = rt_s[0:128, 48:64]
        den = rt_s[0:128, 64:65]
        em.op("act", lambda e, b=psc: e.activation(out=sc, in_=b[:, 0:16],
              func=AF.Sigmoid), reads=[kpsc], writes=["rt_s"])
        em.op("dve", lambda e: e.reduce_max(gmx, sc.rearrange(
              "p (a b) -> p a b", b=4), axis=AX), reads=["rt_s"], writes=["rt_s"])
        em.op("dve", lambda e: e.tensor_tensor(
              rt_b[:, 0:16].rearrange("p (i j) -> p i j", j=4),
              gmx.unsqueeze(1).broadcast_to((128, 4, 4)),
              gmx.unsqueeze(2).broadcast_to((128, 4, 4)), ALU.is_gt),
              reads=["rt_s"], writes=["rt_b"])
        em.op("dve", lambda e: e.tensor_tensor(
              rt_b[:, 16:32].rearrange("p (i j) -> p i j", j=4),
              gmx.unsqueeze(1).broadcast_to((128, 4, 4)),
              gmx.unsqueeze(2).broadcast_to((128, 4, 4)), ALU.is_equal),
              reads=["rt_s"], writes=["rt_b"])
        em.op("dve", lambda e: e.tensor_mul(rt_b[:, 16:32], rt_b[:, 16:32],
              iota_g[:]), reads=["rt_b", "iota_g"], writes=["rt_b"])
        em.op("dve", lambda e: e.tensor_add(rt_b[:, 0:16], rt_b[:, 0:16],
              rt_b[:, 16:32]), reads=["rt_b"], writes=["rt_b"])
        em.op("dve", lambda e: e.reduce_sum(cnt4, rt_b[:, 0:16].rearrange(
              "p (i j) -> p i j", j=4), axis=AX), reads=["rt_b"], writes=["rt_s"])
        em.op("dve", lambda e: e.tensor_single_scalar(cnt4, cnt4, 1.5,
              ALU.is_lt), reads=["rt_s"], writes=["rt_s"])
        em.op("dve", lambda e: e.tensor_tensor(
              msc.rearrange("p (a b) -> p a b", b=4),
              sc.rearrange("p (a b) -> p a b", b=4),
              cnt4.unsqueeze(2).broadcast_to((128, 4, 4)), ALU.mult),
              reads=["rt_s"], writes=["rt_s"])
        em.op("dve", lambda e: e.tensor_tensor(
              rt_b[:, 0:256].rearrange("p (i j) -> p i j", j=16),
              msc.unsqueeze(1).broadcast_to((128, 16, 16)),
              msc.unsqueeze(2).broadcast_to((128, 16, 16)), ALU.is_gt),
              reads=["rt_s"], writes=["rt_b"])
        eqb = rt_c[0:128, 0:256]
        em.op("dve", lambda e, s=eqb: e.tensor_tensor(
              s.rearrange("p (i j) -> p i j", j=16),
              msc.unsqueeze(1).broadcast_to((128, 16, 16)),
              msc.unsqueeze(2).broadcast_to((128, 16, 16)), ALU.is_equal),
              reads=["rt_s"], writes=["rt_c"])
        em.op("dve", lambda e, s=eqb: e.tensor_mul(s, s, iota_e[:]),
              reads=["rt_c", "iota_e"], writes=["rt_c"])
        em.op("dve", lambda e, s=eqb: e.tensor_add(rt_b[:, 0:256],
              rt_b[:, 0:256], s), reads=["rt_b", "rt_c"], writes=["rt_b"])
        em.op("dve", lambda e: e.reduce_sum(cnt16, rt_b[:, 0:256].rearrange(
              "p (i j) -> p i j", j=16), axis=AX), reads=["rt_b"], writes=["rt_s"])
        em.op("dve", lambda e: e.tensor_single_scalar(cnt16, cnt16, 3.5,
              ALU.is_lt), reads=["rt_s"], writes=["rt_s"])
        em.op("dve", lambda e, i0=it: e.tensor_mul(
              comb_sb[:, i0 * 16:(i0 + 1) * 16], sc, cnt16),
              reads=["rt_s"], writes=["comb_sb"])
        em.op("dve", lambda e, i0=it: e.reduce_sum(den,
              comb_sb[:, i0 * 16:(i0 + 1) * 16], axis=AX),
              reads=["comb_sb"], writes=["rt_s"])
        em.op("dve", lambda e: e.reciprocal(den, den),
              reads=["rt_s"], writes=["rt_s"])
        em.op("dve", lambda e, i0=it: e.tensor_scalar(
              comb_sb[:, i0 * 16:(i0 + 1) * 16],
              comb_sb[:, i0 * 16:(i0 + 1) * 16], den, None, ALU.mult),
              reads=["comb_sb", "rt_s"], writes=["comb_sb"])
        bank, bk = next_bank()
        em.op("pe", lambda e, i0=it, b=bank: e.transpose(
            b[0:16, 0:128], comb_sb[:, i0 * 16:(i0 + 1) * 16],
            ident128[:]), reads=["comb_sb", "ident128"], writes=[bk])
        em.op("dve", lambda e, i0=it, b=bank: e.tensor_copy(
            combT_h[i0 // 4][0:16, (i0 % 4) * 128:(i0 % 4 + 1) * 128],
            b[0:16, 0:128]),
            reads=[bk], writes=["combT"])

    em.barrier()
    xn_own = wkv_sb[0:128, 0:8192]
    yslot = wkv_sb[0:128, 8192:16384]
    w13_b = [wq_sb[0:128, i * 512:(i + 1) * 512] for i in range(4)]
    wrep = part_sb[0:128, 0:512]
    sel16 = S_sb[0:16, 512:512] if False else None
    sel16 = xg[0:16, 0:128]
    gexp_slots = []
    for nm, tnsr, ncols in (("xng", xn_g, 4096), ("kvf", kvf, 3072)):
        for s0 in range(0, ncols - 511, 512):
            if len(gexp_slots) < 11:
                gexp_slots.append((tnsr[0:128, s0:s0 + 512], f"gx_{nm}_{s0}"))
    gsh_slots = []
    for nm, tnsr, ncols in (("qf", qf, 1024), ("zb", z_sb, 2048),
                            ("ext", ext, 4128), ("xng", xn_g, 4096),
                            ("kvf", kvf, 3072)):
        for s0 in range(0, ncols - 511, 512):
            if len(gsh_slots) < NSH // 128:
                gsh_slots.append((tnsr[0:128, s0:s0 + 512], f"gsh_{nm}_{s0}"))
    NSH_CH = NSH // 128

    def y_add(tq, dq, pbank, pkey, first):
        dst = yslot[0:128, tq * 2048 + dq * 512:tq * 2048 + (dq + 1) * 512]
        if first:
            em.op("dve", lambda e, b=pbank: e.tensor_copy(
                dst, b[:, 0:512]), reads=[pkey], writes=["yslot"])
        else:
            em.op("dve", lambda e, b=pbank: e.tensor_add(
                dst, dst, b[:, 0:512]), reads=[pkey, "yslot"], writes=["yslot"])

    for half in range(t_own // 512):
        em.barrier()
        th0 = half * 512
        for dc in range(DCH):
            for ti in range(4):
                em.op(("sp", "act")[dc % 2],
                    lambda e, d0=dc, i0=ti, h0=half: e.dma_start(
                    out=xn_own[0:128, d0 * 512 + i0 * 128:d0 * 512 + (i0 + 1) * 128],
                    in_=xn_hbm[TP0 + h0 * 4 + i0, :, d0 * 128:(d0 + 1) * 128]),
                    reads=["xn_hbm"], writes=["xn_own" + str(dc % 2)],
                    dma="xnown" + str(dc % 2))
        for ex in range(NE):
            em.op("pool", lambda e, e0=ex: e.affine_select(
                sel16[0:16, :], ones128[0:16, :], [[0, 128]], ALU.is_equal,
                0.0, base=-e0, channel_multiplier=1),
                reads=["ones128"], writes=["sel16"])
            pwr, kpwr = next_bank()
            em.op("pe", lambda e, b=pwr, h0=half: e.matmul(b[0:128, 0:512],
                  sel16[0:16, :], combT_h[h0][0:16, 0:512],
                  start=True, stop=True),
                  reads=["sel16", "combT"], writes=[kpwr])
            em.op("dve", lambda e, b=pwr: e.tensor_copy(wrep[:], b[0:128, 0:512]),
                  reads=[kpwr], writes=["wrep"])
            for ic in range(11):
                p1, kp1 = next_bank()
                p3, kp3 = next_bank()
                for dc in range(DCH):
                    wb = w13_b[dc % 2]
                    wb3 = w13_b[2 + dc % 2]
                    em.op(("sp", "act")[dc % 2],
                          lambda e, e0=ex, d0=dc, i0=ic, w=wb:
                          e.dma_start(out=w[0:128, 0:128],
                          in_=w1t[e0, d0 * 128:(d0 + 1) * 128,
                                  i0 * 128:(i0 + 1) * 128]),
                          writes=[f"w13_{dc % 2}"], dma=f"w13l{dc % 2}")
                    em.op(("pool", "sp")[dc % 2],
                          lambda e, e0=ex, d0=dc, i0=ic, w=wb3:
                          e.dma_start(out=w[0:128, 0:128],
                          in_=w3t[e0, d0 * 128:(d0 + 1) * 128,
                                  i0 * 128:(i0 + 1) * 128]),
                          writes=[f"w33_{dc % 2}"], dma=f"w33l{dc % 2}")
                    em.op("pe", lambda e, d0=dc, b=p1, w=wb: e.matmul(
                          b[:, 0:512], w[0:128, 0:128],
                          xn_own[0:128, d0 * 512:(d0 + 1) * 512],
                          start=(d0 == 0), stop=(d0 == DCH - 1)),
                          reads=[f"w13_{dc % 2}", "xn_own0", "xn_own1"], writes=[kp1])
                    em.op("pe", lambda e, d0=dc, b=p3, w=wb3: e.matmul(
                          b[:, 0:512], w[0:128, 0:128],
                          xn_own[0:128, d0 * 512:(d0 + 1) * 512],
                          start=(d0 == 0), stop=(d0 == DCH - 1)),
                          reads=[f"w33_{dc % 2}", "xn_own0", "xn_own1"], writes=[kp3])
                gsl, gslk = gexp_slots[ic]
                em.op("act", lambda e, b=p1, s=gsl: e.activation(out=s,
                      in_=b[:, 0:512], func=AF.Silu), reads=[kp1],
                      writes=[gslk])
                em.op("dve", lambda e, b=p3, s=gsl: e.tensor_mul(s, s,
                      b[:, 0:512]), reads=[kp3, gslk], writes=[gslk])
                em.op("dve", lambda e, s=gsl: e.tensor_mul(s, s, wrep[:]),
                      reads=["wrep", gslk], writes=[gslk])
            for dq in range(4):
                ybanks = [next_bank() for _ in range(4)]
                for ic in range(11):
                    wb = w13_b[ic % 2]
                    em.op(("sp", "act")[ic % 2],
                          lambda e, e0=ex, i0=ic, q0=dq, w=wb:
                          e.dma_start(out=w[0:128, 0:512],
                          in_=w2t[e0, i0 * 128:(i0 + 1) * 128,
                                  q0 * 512:(q0 + 1) * 512]),
                          writes=[f"w13_{ic % 2}"], dma=f"w13l{ic % 2}")
                    gsl, gslk = gexp_slots[ic]
                    for tq in range(4):
                        py, kpy = ybanks[tq]
                        em.op("pe", lambda e, t0=tq, b=py, w=wb, s=gsl,
                              i0=ic: e.matmul(
                              b[:, 0:512],
                              s[0:128, t0 * 128:(t0 + 1) * 128],
                              w[0:128, 0:512],
                              start=(i0 == 0), stop=(i0 == 10)),
                              reads=[f"w13_{ic % 2}", gslk], writes=[kpy])
                for tq in range(4):
                    py, kpy = ybanks[tq]
                    y_add(tq, dq, py, kpy, ex == 0)
        em.barrier()
        for ic in range(NSH_CH):
            p1, kp1 = next_bank()
            p3, kp3 = next_bank()
            for dc in range(DCH):
                wb = w13_b[dc % 2]
                wb3 = w13_b[2 + dc % 2]
                em.op(("sp", "act")[dc % 2], lambda e, d0=dc, i0=ic, w=wb:
                      e.dma_start(out=w[0:128, 0:128],
                      in_=ws1t[d0 * 128:(d0 + 1) * 128, i0 * 128:(i0 + 1) * 128]),
                      writes=[f"w13_{dc % 2}"], dma=f"w13l{dc % 2}")
                em.op(("pool", "sp")[dc % 2], lambda e, d0=dc, i0=ic, w=wb3:
                      e.dma_start(out=w[0:128, 0:128],
                      in_=ws3t[d0 * 128:(d0 + 1) * 128, i0 * 128:(i0 + 1) * 128]),
                      writes=[f"w33_{dc % 2}"], dma=f"w33l{dc % 2}")
                em.op("pe", lambda e, d0=dc, b=p1, w=wb: e.matmul(
                      b[:, 0:512], w[0:128, 0:128],
                      xn_own[0:128, d0 * 512:(d0 + 1) * 512],
                      start=(d0 == 0), stop=(d0 == DCH - 1)),
                      reads=[f"w13_{dc % 2}", "xn_own0", "xn_own1"], writes=[kp1])
                em.op("pe", lambda e, d0=dc, b=p3, w=wb3: e.matmul(
                      b[:, 0:512], w[0:128, 0:128],
                      xn_own[0:128, d0 * 512:(d0 + 1) * 512],
                      start=(d0 == 0), stop=(d0 == DCH - 1)),
                      reads=[f"w33_{dc % 2}", "xn_own0", "xn_own1"], writes=[kp3])
            gs, gk = gsh_slots[ic]
            em.op("act", lambda e, b=p1, s=gs: e.activation(out=s,
                  in_=b[:, 0:512], func=AF.Silu), reads=[kp1], writes=[gk])
            em.op("dve", lambda e, b=p3, s=gs: e.tensor_mul(s, s, b[:, 0:512]),
                  reads=[kp3, gk], writes=[gk])
        for dq in range(4):
            ybanks = [next_bank() for _ in range(4)]
            for ic in range(NSH_CH):
                wb = w13_b[ic % 2]
                gs, gk = gsh_slots[ic]
                em.op(("sp", "act")[ic % 2], lambda e, i0=ic, q0=dq, w=wb:
                      e.dma_start(out=w[0:128, 0:512],
                      in_=ws2t[i0 * 128:(i0 + 1) * 128, q0 * 512:(q0 + 1) * 512]),
                      writes=[f"w13_{ic % 2}"], dma=f"w13l{ic % 2}")
                for tq in range(4):
                    py, kpy = ybanks[tq]
                    em.op("pe", lambda e, i0=ic, t0=tq, b=py, w=wb, s=gs:
                          e.matmul(b[:, 0:512],
                          s[0:128, t0 * 128:(t0 + 1) * 128], w[0:128, 0:512],
                          start=(i0 == 0), stop=(i0 == NSH_CH - 1)),
                          reads=[f"w13_{ic % 2}", gk], writes=[kpy])
            for tq in range(4):
                py, kpy = ybanks[tq]
                y_add(tq, dq, py, kpy, False)
        em.barrier()
        fin_a = xn_own[0:128, 0:D]
        for ti in range(4):
            it = half * 4 + ti
            em.op("sp", lambda e, i0=it: e.dma_start(
                out=fin_a[:], in_=hidden[t_pre + i0 * 128:t_pre + (i0 + 1) * 128, :]),
                writes=["fin_a"], dma="finx")
            for g in range(NG):
                for dq in range(4):
                    em.op("sp", lambda e, i0=it, g0=g, q0=dq: e.dma_start(
                        out=fin_b[:],
                        in_=attn_part[g0, i0 * 128:(i0 + 1) * 128,
                                      q0 * 512:(q0 + 1) * 512]),
                        reads=["attn_part"], writes=["fin_b"], dma="finp")
                    em.op("dve", lambda e, q0=dq: e.tensor_add(
                        fin_a[:, q0 * 512:(q0 + 1) * 512],
                        fin_a[:, q0 * 512:(q0 + 1) * 512], fin_b[:]),
                        reads=["fin_a", "fin_b"], writes=["fin_a"])
            em.op("dve", lambda e, t0=ti: e.tensor_add(fin_a[:], fin_a[:],
                  yslot[0:128, t0 * 2048:(t0 + 1) * 2048]),
                  reads=["fin_a", "yslot"], writes=["fin_a"])
            em.op("pool", lambda e, i0=it: e.dma_start(
                out=out[i0 * 128:(i0 + 1) * 128, :], in_=fin_a[:]),
                reads=["fin_a"], dma="outw")

    if debug_out:
        dbg_b = sbuf("dbg_b", (128, 512))
        for name, shape in debug_out:
            if name == "xn_hbm":
                for tt in range(NT):
                    for half in range(4):
                        em.op("pool", lambda e, t0=tt, h0=half: e.dma_start(
                            out=dbg_b[:, 0:512],
                            in_=xn_hbm[t0, :, h0 * 512:(h0 + 1) * 512]),
                            reads=["xn_hbm"], writes=["dbg_b"], dma="dbg")
                        em.op("pool", lambda e, t0=tt, h0=half: e.dma_start(
                            out=dbg["xn_hbm"][t0, :, h0 * 512:(h0 + 1) * 512],
                            in_=dbg_b[:, 0:512]),
                            reads=["dbg_b"], dma="dbg")

    sems = {}
    for e in ENGINES:
        h = nc.semaphore("sem_" + e)
        keep.append(h)
        sems["sem_" + e] = h.__enter__()
    for s in em.dma_cnt:
        h = nc.semaphore(s)
        keep.append(h)
        sems[s] = h.__enter__()

    em.build(sems)
    return nc


def prep_inputs(kw, t_pre=None, t_own=None):
    t_pre = T_PRE if t_pre is None else t_pre
    t_own = T_OWN if t_own is None else t_own
    hid = np.asarray(kw["L0_attn_hidden"], dtype=np.float32)
    shared = {
        "w_qkv": np.ascontiguousarray(kw["L0_attn_W_qkv"], dtype=np.float32),
        "w_z": np.ascontiguousarray(kw["L0_attn_W_z"], dtype=np.float32),
        "w_b": np.ascontiguousarray(kw["L0_attn_W_b"], dtype=np.float32),
        "w_a": np.ascontiguousarray(kw["L0_attn_W_a"], dtype=np.float32),
        "conv_w": np.ascontiguousarray(kw["L0_attn_conv_weight"], dtype=np.float32),
        "dt_bias": np.ascontiguousarray(kw["L0_attn_dt_bias"], dtype=np.float32),
        "a_log": np.ascontiguousarray(kw["L0_attn_A_log"], dtype=np.float32),
        "norm_w": np.ascontiguousarray(kw["L0_attn_norm_w"], dtype=np.float32),
        "w_out": np.ascontiguousarray(kw["L0_attn_W_out"], dtype=np.float32),
        "gate_wt": np.ascontiguousarray(np.asarray(kw["L0_ffn_gate_w"]).T, dtype=np.float32),
        "w1t": np.ascontiguousarray(np.transpose(np.asarray(kw["L0_ffn_W1"]), (0, 2, 1)), dtype=np.float32),
        "w2t": np.ascontiguousarray(np.transpose(np.asarray(kw["L0_ffn_W2"]), (0, 2, 1)), dtype=np.float32),
        "w3t": np.ascontiguousarray(np.transpose(np.asarray(kw["L0_ffn_W3"]), (0, 2, 1)), dtype=np.float32),
        "ws1t": np.ascontiguousarray(np.asarray(kw["L0_ffn_Ws1"]).T, dtype=np.float32),
        "ws2t": np.ascontiguousarray(np.asarray(kw["L0_ffn_Ws2"]).T, dtype=np.float32),
        "ws3t": np.ascontiguousarray(np.asarray(kw["L0_ffn_Ws3"]).T, dtype=np.float32),
    }
    in_maps = []
    n_q = T // t_own
    for c in range(N_CORES):
        b, q = divmod(c, n_q)
        start = q * t_own
        pre_real = hid[b, max(0, start - t_pre):start, :]
        pad = np.zeros((t_pre - pre_real.shape[0], D), dtype=np.float32)
        own = hid[b, start:start + t_own, :]
        m = dict(shared)
        m["hidden"] = np.ascontiguousarray(
            np.concatenate([pad, pre_real, own], axis=0))
        in_maps.append(m)
    return in_maps


def kernel(**kw):
    in_maps = prep_inputs(kw)
    nc = build_program()
    res = run_bass_kernel_spmd(nc, in_maps, core_ids=list(range(N_CORES)))
    parts = [np.asarray(r["out"]) for r in res.results]
    full = np.concatenate(parts, axis=0).reshape(B, T, D)
    return np.ascontiguousarray(full, dtype=np.float32)


# version 13 (final)
# speedup vs baseline: 1.1166x; 1.0726x over previous
import numpy as np
import concourse.bass as bass
import concourse.mybir as mybir
from concourse.bass_utils import run_bass_kernel_spmd

F32 = mybir.dt.float32
BF16 = mybir.dt.bfloat16
I16 = mybir.dt.int16
AF = mybir.ActivationFunctionType
ALU = mybir.AluOpType

B, T, D = 2, 4096, 2048
HK, HV, AK, UV = 16, 32, 128, 128
KEY_DIM = HK * AK
VALUE_DIM = HV * UV
CONV_DIM = 2 * KEY_DIM + VALUE_DIM
KCONV = 4
CHUNK = 64
EPS_L2 = 1e-6
EPS_LN = 1e-5
EPS_RMS = 1e-6
NE, NI = 16, 1408
NSH = 2816

T_PRE = 3072
T_OWN = 1024
N_CORES = 8

ENGINES = ("pe", "act", "dve", "sp", "pool")


class Em:

    def __init__(self, nc):
        self.nc = nc
        self.q = {e: [] for e in ENGINES}
        self.cnt = {e: 0 for e in ENGINES}
        self.dma_cnt = {}
        self.last_w = {}
        self.readers = {}
        self.hwm = {e: {} for e in ENGINES}
        self.barrier_req = {}
        self.n_ops = 0

    def barrier(self):
        snap = {("sem_" + e): self.cnt[e] for e in ENGINES if self.cnt[e]}
        snap.update(dict(self.dma_cnt))
        self.barrier_req = snap
        self.last_w = {}
        self.readers = {}

    def op(self, eng, fn, reads=(), writes=(), dma=None):
        need = dict(self.barrier_req)
        for k in reads:
            w = self.last_w.get(k)
            if w:
                need[w[0]] = max(need.get(w[0], 0), w[1])
        for k in writes:
            w = self.last_w.get(k)
            if w:
                need[w[0]] = max(need.get(w[0], 0), w[1])
            for s, v in self.readers.get(k, {}).items():
                need[s] = max(need.get(s, 0), v)
        hw = self.hwm[eng]
        waits = []
        for s, v in sorted(need.items()):
            if hw.get(s, -1) < v:
                waits.append((s, v))
                hw[s] = v
        if dma:
            sem_name = "dma_" + dma
            need.pop(sem_name, None)
            self.dma_cnt[sem_name] = self.dma_cnt.get(sem_name, 0) + 16
            val = self.dma_cnt[sem_name]
        else:
            self.cnt[eng] += 1
            sem_name, val = "sem_" + eng, self.cnt[eng]
        self.q[eng].append((waits, fn, sem_name))
        for k in writes:
            self.last_w[k] = (sem_name, val)
            self.readers[k] = {}
        for k in reads:
            self.readers.setdefault(k, {})
            r = self.readers[k]
            r[sem_name] = max(r.get(sem_name, 0), val)
        self.n_ops += 1

    def build(self, sems):
        def run(engine_obj, name):
            for waits, fn, sem_name in self.q[name]:
                for s, v in waits:
                    engine_obj.wait_ge(sems[s], v)
                last = fn(engine_obj)
                inc = 16 if sem_name.startswith("dma_") else 1
                last.then_inc(sems[sem_name], inc)
            if name == "pool":
                for s2, v2 in sorted(self.dma_cnt.items()):
                    engine_obj.wait_ge(sems[s2], v2)
                for e2 in ENGINES:
                    if self.cnt[e2] and e2 != "pool":
                        engine_obj.wait_ge(sems["sem_" + e2], self.cnt[e2])

        with self.nc.Block() as block:
            @block.tensor
            def _(e):
                run(e, "pe")

            @block.scalar
            def _(e):
                run(e, "act")

            @block.vector
            def _(e):
                run(e, "dve")

            @block.sync
            def _(e):
                run(e, "sp")

            @block.gpsimd
            def _(e):
                run(e, "pool")


def ceil_div(a, b):
    return (a + b - 1) // b


def build_program(t_pre=None, t_own=None, debug_out=()):
    t_pre = T_PRE if t_pre is None else t_pre
    t_own = T_OWN if t_own is None else t_own
    t_all = t_pre + t_own
    assert t_all % 128 == 0 and t_own % 128 == 0 and t_pre % CHUNK == 0
    NT = t_all // 128
    DCH = D // 128

    nc = bass.Bass("TRN2", target_bir_lowering=False)
    em = Em(nc)
    keep = []
    dr = {}

    def din(name, shape, dtype=F32):
        dr[name] = nc.dram_tensor(name, list(shape), dtype, kind="ExternalInput").ap()
        return dr[name]

    def dout_(name, shape, dtype=F32):
        dr[name] = nc.dram_tensor(name, list(shape), dtype, kind="ExternalOutput").ap()
        return dr[name]

    def dscratch(name, shape, dtype=F32):
        dr[name] = nc.dram_tensor(name, list(shape), dtype, kind="Internal").ap()
        return dr[name]

    hidden = din("hidden", (t_all, D))
    w_qkv = din("w_qkv", (D, CONV_DIM))
    w_z = din("w_z", (D, VALUE_DIM))
    w_b = din("w_b", (D, HV))
    w_a = din("w_a", (D, HV))
    conv_w = din("conv_w", (CONV_DIM, KCONV))
    dt_bias = din("dt_bias", (HV,))
    a_log = din("a_log", (HV,))
    norm_w = din("norm_w", (UV,))
    w_out = din("w_out", (VALUE_DIM, D))
    gate_wt = din("gate_wt", (D, NE))
    w1t = din("w1t", (NE, D, NI))
    w2t = din("w2t", (NE, NI, D))
    w3t = din("w3t", (NE, D, NI))
    ws1t = din("ws1t", (D, NSH))
    ws2t = din("ws2t", (NSH, D))
    ws3t = din("ws3t", (D, NSH))

    out = dout_("out", (t_own, D))
    xn_hbm = dscratch("xn_hbm", (NT, 128, D))
    attn_part = dscratch("attn_part", (8, t_own, D))

    dbg = {}
    for name, shape in debug_out:
        dbg[name] = dout_("dbg_" + name, shape)

    sb = {}

    def sbuf(name, shape, dtype=F32):
        h = nc.sbuf_tensor(name, list(shape), dtype)
        keep.append(h)
        t = h.__enter__()
        sb[name] = t
        return t

    ps = []
    for i in range(8):
        h = nc.psum_tensor(f"psum{i}", [128, 512], F32)
        keep.append(h)
        ps.append(h.__enter__())
    ps_rot = [0]

    def next_bank():
        ps_rot[0] = (ps_rot[0] + 1) % 8
        return ps[ps_rot[0]], f"psum{ps_rot[0]}"

    ones_col = sbuf("ones_col", (128, 1))
    ones128 = sbuf("ones128", (128, 128))
    ident128 = sbuf("ident128", (128, 128))

    cscal = {}

    def const_col(val):
        if val not in cscal:
            idx = len(cscal)
            t = sbuf(f"cconst{idx}", (128, 1))
            em.op("pool", lambda e, a=t, v=val: e.memset(a[:], v),
                  writes=[f"cconst{idx}"])
            cscal[val] = (t, f"cconst{idx}")
        t, k = cscal[val]
        return t[:, 0:1], k

    em.op("pool", lambda e: e.memset(ones_col[:], 1.0), writes=["ones_col"])
    em.op("pool", lambda e: e.memset(ones128[:], 1.0), writes=["ones128"])
    em.op("pool", lambda e: e.affine_select(
        ident128[:], ones128[:], [[-1, 128]], ALU.is_equal, 0.0,
        channel_multiplier=1), reads=["ones128"], writes=["ident128"])

    wkv_sb = sbuf("wkv_sb", (128, 16384))
    wq_sb = sbuf("wq_sb", (128, 2048))
    xn_g = sbuf("xn_g", (128, 8 * 512))
    kvf = sbuf("kvf", (128, 6 * 512))

    x_tm = [wkv_sb[0:128, i * D:(i + 1) * D] for i in range(2)]
    ln_scr = wkv_sb[0:128, 2 * D:3 * D]
    ln_s = [sbuf(f"ln_s{i}", (128, 4)) for i in range(2)]
    xn_st = [wkv_sb[0:128, (3 + i) * D:(4 + i) * D] for i in range(2)]

    for it in range(NT):
        bi = it % 2
        xt, kx = x_tm[bi], f"x_tm{bi}"
        st, kst = xn_st[bi], f"xn_st{bi}"
        lt, kl = ln_s[bi], f"ln_s{bi}"
        em.op("sp", lambda e, a=xt, i0=it: e.dma_start(
            out=a[:], in_=hidden[i0 * 128:(i0 + 1) * 128, :]),
            writes=[kx], dma=f"xld{bi}")
        em.op("act", lambda e, a=xt, l=lt: e.activation(
            out=ln_scr[:], in_=a[:], func=AF.Copy, accum_out=l[:, 0:1]),
            reads=[kx], writes=["ln_scr", kl])
        em.op("act", lambda e, a=xt, l=lt: e.activation(
            out=ln_scr[:], in_=a[:], func=AF.Square, accum_out=l[:, 1:2]),
            reads=[kx], writes=["ln_scr", kl])
        em.op("dve", lambda e, l=lt: e.tensor_scalar_mul(
            l[:, 2:3], l[:, 0:1], 1.0 / D), reads=[kl], writes=[kl])
        em.op("dve", lambda e, l=lt: e.scalar_tensor_tensor(
            l[:, 3:4], l[:, 2:3], -1.0, l[:, 2:3], ALU.mult, ALU.mult),
            reads=[kl], writes=[kl])
        em.op("dve", lambda e, l=lt: e.scalar_tensor_tensor(
            l[:, 3:4], l[:, 1:2], 1.0 / D, l[:, 3:4], ALU.mult, ALU.add),
            reads=[kl], writes=[kl])
        eps_ap, eps_k = const_col(EPS_LN)
        em.op("act", lambda e, l=lt, b_=eps_ap: e.activation(
            out=l[:, 3:4], in_=l[:, 3:4], func=AF.Sqrt, bias=b_),
            reads=[kl, eps_k], writes=[kl])
        em.op("dve", lambda e, l=lt: e.reciprocal(l[:, 3:4], l[:, 3:4]),
              reads=[kl], writes=[kl])
        em.op("dve", lambda e, a=xt, l=lt: e.tensor_scalar(
            a[:], a[:], l[:, 2:3], l[:, 3:4], ALU.subtract, ALU.mult),
            reads=[kx, kl], writes=[kx])
        for dc in range(DCH):
            bank, bk = next_bank()
            em.op("pe", lambda e, a=xt, d0=dc, b=bank: e.transpose(
                b[:, 0:128], a[:, d0 * 128:(d0 + 1) * 128], ident128[:]),
                reads=[kx, "ident128"], writes=[bk])
            em.op("dve", lambda e, d0=dc, b=bank, s=st: e.tensor_copy(
                s[:, d0 * 128:(d0 + 1) * 128], b[:, 0:128]),
                reads=[bk], writes=[kst])
        em.op("sp", lambda e, s=st, i0=it: e.dma_start(
            out=xn_hbm[i0], in_=s[:]),
            reads=[kst], writes=["xn_hbm"], dma="xnsp")

    em.barrier()
    TG = 512
    assert t_pre % TG == 0 and t_own % TG == 0
    NTG = t_all // TG
    NCH = t_all // CHUNK
    CPG = TG // CHUNK
    TPG = TG // 128
    NG = 8
    LN2_AK = float(np.log(AK) * 0.5)

    zeros64 = sbuf("zeros64", (64, 64))
    em.op("pool", lambda e: e.memset(zeros64[:], 0.0), writes=["zeros64"])
    wba = wq_sb[0:128, 0:DCH * 64]
    for dc in range(DCH):
        em.op("sp", lambda e, d0=dc: e.dma_start(
            out=wba[:, d0 * 64:d0 * 64 + 32],
            in_=w_b[d0 * 128:(d0 + 1) * 128, :]), writes=["wba"], dma="wba")
        em.op("sp", lambda e, d0=dc: e.dma_start(
            out=wba[:, d0 * 64 + 32:d0 * 64 + 64],
            in_=w_a[d0 * 128:(d0 + 1) * 128, :]), writes=["wba"], dma="wba")
    dtb = sbuf("dtb", (32, 4))
    em.op("sp", lambda e: e.dma_start(out=dtb[0:32, 0:1], in_=dt_bias),
          writes=["dtb"], dma="dtb")
    em.op("sp", lambda e: e.dma_start(out=dtb[0:32, 1:2], in_=a_log),
          writes=["dtb"], dma="dtb")
    em.op("act", lambda e: e.activation(out=dtb[0:32, 2:3], in_=dtb[0:32, 1:2],
          func=AF.Exp), reads=["dtb"], writes=["dtb"])
    em.op("dve", lambda e: e.tensor_scalar_mul(dtb[0:32, 2:3], dtb[0:32, 2:3],
          -1.0), reads=["dtb"], writes=["dtb"])

    conv_sb = sbuf("conv_sb", (128, 64 * 4))
    for cc in range(64):
        em.op("sp", lambda e, c0=cc: e.dma_start(
            out=conv_sb[:, c0 * 4:(c0 + 1) * 4],
            in_=conv_w[c0 * 128:(c0 + 1) * 128, :]), writes=["conv_sb"], dma="conv")

    nw_row = sbuf("nw_row", (1, UV))
    nw_tile = sbuf("nw_tile", (128, UV))
    em.op("sp", lambda e: e.dma_start(out=nw_row[0:1, :], in_=norm_w),
          writes=["nw_row"], dma="nwr")
    bank, bk = next_bank()
    em.op("pe", lambda e, b=bank: e.matmul(b[0:128, 0:UV], ones128[0:1, :],
          nw_row[0:1, :], start=True, stop=True),
          reads=["nw_row", "ones128"], writes=[bk])
    em.op("dve", lambda e, b=bank: e.tensor_copy(nw_tile[:], b[0:128, 0:UV]),
          reads=[bk], writes=["nw_tile"])

    gj_hm = sbuf("gj_hm", (32, t_all))
    p1_hm = sbuf("p1_hm", (32, t_all))
    lnsk4 = sbuf("lnsk4", (4, 512))
    lnsq4 = sbuf("lnsq4", (4, 512))
    egl_hm = sbuf("egl_hm", (32, NCH))
    gj_tm = sbuf("gj_tm", (128, 32 * 4))
    p1_tm = sbuf("p1_tm", (128, 32 * 4))
    p3_tm = sbuf("p3_tm", (128, 32 * 4))
    b_tm = sbuf("b_tm", (128, 32 * 4))
    lnsk_tm = sbuf("lnsk_tm", (128, 4 * 4))
    p3s_hm = sbuf("p3s_hm", (32, 512))
    em.op("pool", lambda e: e.memset(lnsk4[:], 0.0), writes=["lnsk4"])
    em.op("pool", lambda e: e.memset(lnsq4[:], 0.0), writes=["lnsq4"])
    em.op("pool", lambda e: e.memset(lnsk_tm[:], 0.0), writes=["lnsk_tm"])

    fam = kvf[0:32, 0:6 * TG // 1][0:32, 0:3072]
    fam = kvf[0:32, 0:3072]
    one_ap, one_k = const_col(1.0)
    epsl2_ap, epsl2_k = const_col(EPS_L2)
    epsrms_ap, epsrms_k = const_col(EPS_RMS)
    negln_ap, negln_k = const_col(-LN2_AK)

    def load_xn_half(r, tg0, n_tiles=TPG, key="xn_g"):
        for dcl in range(8):
            dc = 8 * r + dcl
            for ti in range(n_tiles):
                tile = tg0 // 128 + ti
                em.op(("sp", "act")[dcl % 2],
                    lambda e, d0=dc, dl=dcl, t0=tile, i0=ti: e.dma_start(
                    out=xn_g[:, dl * TG + i0 * 128:dl * TG + i0 * 128 + 128],
                    in_=xn_hbm[t0, :, d0 * 128:(d0 + 1) * 128]),
                    reads=["xn_hbm"], writes=[key + str(dcl % 2)], dma="xng" + str(dcl % 2))

    for tgi in range(NTG):
        tg0 = tgi * TG
        pb, kpb = next_bank()
        pa, kpa = next_bank()
        for r in range(2):
            load_xn_half(r, tg0)
            for dcl in range(8):
                dc = 8 * r + dcl
                em.op("pe", lambda e, b=pb, d0=dc, dl=dcl, st=(r == 0 and dcl == 0),
                      sp=(r == 1 and dcl == 7): e.matmul(
                      b[0:32, 0:TG], wba[:, d0 * 64:d0 * 64 + 32],
                      xn_g[:, dl * TG:(dl + 1) * TG], start=st, stop=sp),
                      reads=["wba", "xn_g0", "xn_g1"], writes=[kpb])
                em.op("pe", lambda e, b=pa, d0=dc, dl=dcl, st=(r == 0 and dcl == 0),
                      sp=(r == 1 and dcl == 7): e.matmul(
                      b[0:32, 0:TG], wba[:, d0 * 64 + 32:d0 * 64 + 64],
                      xn_g[:, dl * TG:(dl + 1) * TG], start=st, stop=sp),
                      reads=["wba", "xn_g0", "xn_g1"], writes=[kpa])
        b_s, a_s = fam[0:32, 0:TG], fam[0:32, TG:2 * TG]
        bet = fam[0:32, 2 * TG:3 * TG]
        u_s = fam[0:32, 3 * TG:4 * TG]
        l_s = fam[0:32, 4 * TG:5 * TG]
        g_s = fam[0:32, 5 * TG:6 * TG]
        em.op("dve", lambda e, b=pb: e.tensor_copy(b_s, b[0:32, 0:TG]),
              reads=[kpb], writes=["famb"])
        em.op("dve", lambda e, b=pa: e.tensor_copy(a_s, b[0:32, 0:TG]),
              reads=[kpa], writes=["fama"])
        em.op("act", lambda e: e.activation(out=bet, in_=b_s, func=AF.Sigmoid),
              reads=["famb"], writes=["fambet"])
        em.op("act", lambda e: e.activation(out=b_s, in_=bet, func=AF.Ln),
              reads=["fambet"], writes=["famb"])
        em.op("dve", lambda e: e.tensor_single_scalar(u_s, a_s,
              dtb[0:32, 0:1], ALU.add), reads=["fama", "dtb"], writes=["famu"])
        em.op("act", lambda e: e.activation(out=a_s, in_=u_s, func=AF.Abs),
              reads=["famu"], writes=["fama"])
        em.op("act", lambda e: e.activation(out=l_s, in_=a_s, func=AF.Exp,
              scale=-1.0), reads=["fama"], writes=["faml"])
        em.op("act", lambda e, b_=one_ap: e.activation(out=l_s, in_=l_s,
              func=AF.Ln, bias=b_[0:32, :]), reads=["faml", one_k], writes=["faml"])
        em.op("dve", lambda e: e.tensor_scalar_max(u_s, u_s, 0.0),
              reads=["famu"], writes=["famu"])
        em.op("dve", lambda e: e.tensor_add(u_s, u_s, l_s),
              reads=["famu", "faml"], writes=["famu"])
        em.op("dve", lambda e: e.tensor_single_scalar(g_s, u_s, dtb[0:32, 2:3],
              ALU.mult), reads=["famu", "dtb"], writes=["famg"])
        for c in range(CPG):
            em.op("dve", lambda e, c0=c, t0=tg0: e.tensor_tensor_scan(
                gj_hm[:, t0 + c0 * 64:t0 + (c0 + 1) * 64],
                fam[0:32, 5 * TG + c0 * 64:5 * TG + (c0 + 1) * 64],
                zeros64[0:32, :], 0.0, ALU.add, ALU.add),
                reads=["famg", "zeros64"], writes=["gj_hm"])
        em.op("dve", lambda e, t0=tg0: e.tensor_add(p1_hm[:, t0:t0 + TG],
              gj_hm[:, t0:t0 + TG], b_s), reads=["gj_hm", "famb"],
              writes=["p1_hm"])
        em.op("act", lambda e, t0=tg0: e.activation(
            out=egl_hm[:, t0 // 64:t0 // 64 + CPG],
            in_=gj_hm[:, t0 + 63:t0 + TG:64], func=AF.Exp),
            reads=["gj_hm"], writes=["egl_hm"])

    em.barrier()
    masks = sbuf("masks", (128, 6 * 64))
    negones = sbuf("negones", (128, 128))
    em.op("pool", lambda e: e.memset(negones[:], -1.0), writes=["negones"])
    mask_defs = [
        (0, ALU.is_gt, -1, 1, False),
        (1, ALU.is_gt, -1, 1, True),
        (2, ALU.is_gt, 1, -1, False),
        (3, ALU.is_gt, 1, -1, True),
        (4, ALU.is_ge, 1, -1, False),
        (5, ALU.is_ge, 1, -1, True),
    ]
    for slot, cmp, stp, mlt, neg in mask_defs:
        src_t = negones if neg else ones128
        src_k = "negones" if neg else "ones128"
        for hh in range(2):
            em.op("pool", lambda e, s0=slot, c0=cmp, sa=src_t, st0=stp,
                m0=mlt, h0=hh: e.affine_select(
                masks[h0 * 64:(h0 + 1) * 64, s0 * 64:(s0 + 1) * 64],
                sa[0:64, 0:64], [[st0, 64]], c0,
                0.0, channel_multiplier=m0), reads=[src_k], writes=["masks"])
    m01_B, mneg_B = masks[:, 0:64], masks[:, 64:128]
    m01_Bt, mneg_Bt = masks[:, 128:192], masks[:, 192:256]
    m01_iT, mneg_iT = masks[:, 256:320], masks[:, 320:384]

    sel_sb = sbuf("sel_sb", (32, 4 * 128))
    selg_sb = sbuf("selg_sb", (4, 4 * 128))
    selgn_sb = sbuf("selgn_sb", (4, 4 * 128))
    kmask = sbuf("kmask", (128, 2 * 32))
    ext = sbuf("ext", (128, 8 * (TG + 4)))
    EXTW = TG + 4
    qf = sbuf("qf", (128, 2 * TG))
    z_sb = sbuf("z_sb", (128, 4 * 512))
    wz_buf = [sbuf(f"wz_buf{i}", (128, 512)) for i in range(2)]
    wout_buf = sbuf("wout_buf", (128, 512))
    S_sb = sbuf("S_sb", (128, 4 * 128))
    eglrep = sbuf("eglrep", (128, max(4 * NCH, 96)))
    dtile = sbuf("dtile", (128, 64))
    Bpr = [sbuf(f"BprT{i}", (128, 128)) for i in range(2)]
    Btpr = [sbuf(f"BtprT{i}", (128, 128)) for i in range(2)]
    Rpr = sbuf("RprT", (128, 128))
    intraT = sbuf("intraTT", (128, 64))
    kts = sbuf("ktsT", (128, 128))
    u1 = sbuf("u1T", (128, 128))
    Pt = sbuf("PtT", (128, 128))
    vnew = sbuf("vnewT", (128, 128))
    qs = sbuf("qsT", (128, 64))
    qsc = sbuf("qscT", (128, 64))
    e1b = sbuf("e1b", (128, 4))
    e5b = sbuf("e5b", (128, 4))
    xg = sbuf("xg", (128, 128))
    rmsc = sbuf("rmsc", (128, 2))
    normed_sb = sbuf("normed_sb", (128, 512))
    part_sb = sbuf("part_sb", (128, 512))
    k2s = part_sb[0:128, 0:512]
    normT = sbuf("normT", (128, 512))
    zsl = sbuf("zsl", (128, 128))

    def emit_group(g):
        sfx = f"_g{g}"
        kv_cols = ([KEY_DIM + (2 * g + j) * 128 for j in range(2)] +
                   [2 * KEY_DIM + (4 * g + h) * 128 for h in range(4)])
        q_cols = [(2 * g + j) * 128 for j in range(2)]
        for dc in range(DCH):
            for ct, col in enumerate(kv_cols):
                qi = (dc * 6 + ct) % 4
                em.op(("sp", "act", "pool", "sp")[qi],
                    lambda e, d0=dc, c0=col, t0=ct: e.dma_start(
                    out=wkv_sb[:, (d0 * 6 + t0) * 128:(d0 * 6 + t0 + 1) * 128],
                    in_=w_qkv[d0 * 128:(d0 + 1) * 128, c0:c0 + 128]),
                    writes=[f"wkv_sb{qi}"], dma=f"wkv{qi}")
        for hl in range(4):
            em.op("pool", lambda e, h0=hl: e.affine_select(
                sel_sb[:, h0 * 128:(h0 + 1) * 128], ones128[0:32, :],
                [[0, 128]], ALU.is_equal, 0.0, base=-(4 * g + h0),
                channel_multiplier=1), reads=["ones128"], writes=["sel_sb"])
            em.op("pool", lambda e, h0=hl: e.affine_select(
                selgn_sb[0:4, h0 * 128:(h0 + 1) * 128], negones[0:4, :],
                [[0, 128]], ALU.is_equal, 0.0, base=-h0,
                channel_multiplier=1), reads=["negones"], writes=["selgn_sb"])
        for hl in range(4):
            em.op("pool", lambda e, h0=hl: e.affine_select(
                selg_sb[0:4, h0 * 128:(h0 + 1) * 128], ones128[0:4, :],
                [[0, 128]], ALU.is_equal, 0.0, base=-h0,
                channel_multiplier=1), reads=["ones128"], writes=["selg_sb"])
        for j in range(2):
            c0 = 2 * j
            em.op("pool", lambda e, cc=c0, j0=j: e.affine_select(
                kmask[:, j0 * 32:(j0 + 1) * 32], ones128[:, 0:32], [[1, 32]],
                ALU.is_ge, 0.0, base=-cc, channel_multiplier=0),
                reads=["ones128"], writes=["kmask"])
            em.op("pool", lambda e, cc=c0, j0=j: e.affine_select(
                kmask[:, j0 * 32:(j0 + 1) * 32], kmask[:, j0 * 32:(j0 + 1) * 32],
                [[-1, 32]], ALU.is_ge, 0.0, base=cc + 1, channel_multiplier=0),
                reads=["kmask"], writes=["kmask"])
        for hl in range(4):
            bank, bk = next_bank()
            em.op("pe", lambda e, h0=hl, b=bank: e.matmul(
                b[0:128, 0:NCH], sel_sb[:, h0 * 128:(h0 + 1) * 128],
                egl_hm[0:32, :], start=True, stop=True),
                reads=["sel_sb", "egl_hm"], writes=[bk])
            em.op("dve", lambda e, h0=hl, b=bank: e.tensor_copy(
                eglrep[:, h0 * NCH:(h0 + 1) * NCH], b[0:128, 0:NCH]),
                reads=[bk], writes=["eglrep"])
        em.op("pool", lambda e: e.memset(S_sb[:], 0.0), writes=["S_sb"])
        em.op("pool", lambda e: e.memset(Bpr[0][:], 0.0), writes=["Bpr0"])
        em.op("pool", lambda e: e.memset(Btpr[0][:], 0.0), writes=["Btpr0"])
        em.op("pool", lambda e: e.memset(ext[:], 0.0), writes=["ext"])

        NPRE_TG = t_pre // TG

        def conv_one(ct_slot, dst_ap, dst_k, ch_glob, psum_ap, psum_k):
            e0 = ext[:, ct_slot * EXTW:(ct_slot + 1) * EXTW]
            em.op("act", lambda e: e.activation(out=e0[:, 3:3 + TG],
                  in_=psum_ap, func=AF.Copy), reads=[psum_k], writes=["ext"])
            em.op("act", lambda e: e.activation(out=dst_ap, in_=e0[:, 0:TG],
                  func=AF.Copy, scale=conv_sb[:, ch_glob * 4:ch_glob * 4 + 1]),
                  reads=["ext", "conv_sb"], writes=[dst_k])
            for j in range(1, KCONV):
                em.op("dve", lambda e, j0=j: e.scalar_tensor_tensor(
                    dst_ap, e0[:, j0:j0 + TG],
                    conv_sb[:, ch_glob * 4 + j0:ch_glob * 4 + j0 + 1], dst_ap,
                    ALU.mult, ALU.add), reads=["ext", "conv_sb", dst_k],
                    writes=[dst_k])
            em.op("act", lambda e: e.activation(out=dst_ap, in_=dst_ap,
                  func=AF.Silu), reads=[dst_k], writes=[dst_k])
            em.op("dve", lambda e: e.tensor_copy(e0[:, 0:3], e0[:, TG:TG + 3]),
                  reads=["ext"], writes=["ext"])

        for tgi in range(NTG):
            tg0 = tgi * TG
            own_tg = tg0 >= t_pre
            warm_tg = (t_pre > 0 and tgi == NPRE_TG - 1)
            banks = []
            nct = 6 + (2 if own_tg else 0)
            for _ in range(nct + (2 if warm_tg else 0)):
                banks.append(next_bank())
            for r in range(2):
                load_xn_half(r, tg0)
                if own_tg or warm_tg:
                    for dcl in range(8):
                        for tq, col in enumerate(q_cols):
                            em.op("sp", lambda e, d0=8 * r + dcl, dl=dcl,
                                  c0=col, t0=tq: e.dma_start(
                                  out=wq_sb[:, (dl * 2 + t0) * 128:(dl * 2 + t0 + 1) * 128],
                                  in_=w_qkv[d0 * 128:(d0 + 1) * 128, c0:c0 + 128]),
                                  writes=["wq_sb"], dma="wq")
                for dcl in range(8):
                    dc = 8 * r + dcl
                    st = (r == 0 and dcl == 0)
                    sp = (r == 1 and dcl == 7)
                    for ct in range(6):
                        b0, bk0 = banks[ct]
                        em.op("pe", lambda e, d0=dc, dl=dcl, t0=ct, b=b0,
                              s1=st, s2=sp: e.matmul(
                              b[:, 0:TG], wkv_sb[:, (d0 * 6 + t0) * 128:(d0 * 6 + t0 + 1) * 128],
                              xn_g[:, dl * TG:(dl + 1) * TG], start=s1, stop=s2),
                              reads=["wkv_sb0", "wkv_sb1", "wkv_sb2", "wkv_sb3", "xn_g0", "xn_g1"], writes=[bk0])
                    if own_tg:
                        for tq in range(2):
                            b0, bk0 = banks[6 + tq]
                            em.op("pe", lambda e, dl=dcl, t0=tq, b=b0,
                                  s1=st, s2=sp: e.matmul(
                                  b[:, 0:TG], wq_sb[:, (dl * 2 + t0) * 128:(dl * 2 + t0 + 1) * 128],
                                  xn_g[:, dl * TG:(dl + 1) * TG], start=s1, stop=s2),
                                  reads=["wq_sb", "xn_g0", "xn_g1"], writes=[bk0])
                    if warm_tg:
                        for tq in range(2):
                            b0, bk0 = banks[6 + tq]
                            em.op("pe", lambda e, dl=dcl, t0=tq, b=b0,
                                  s1=st, s2=sp: e.matmul(
                                  b[:, 0:128], wq_sb[:, (dl * 2 + t0) * 128:(dl * 2 + t0 + 1) * 128],
                                  xn_g[:, dl * TG + 3 * 128:dl * TG + 4 * 128],
                                  start=s1, stop=s2),
                                  reads=["wq_sb", "xn_g0", "xn_g1"], writes=[bk0])
            for ct in range(6):
                ch_glob = ((HK + 2 * g + ct) if ct < 2
                           else (2 * HK + 4 * g + ct - 2))
                b0, bk0 = banks[ct]
                conv_one(ct, kvf[:, ct * TG:(ct + 1) * TG], "kvf", ch_glob,
                         b0[:, 0:TG], bk0)
            if own_tg:
                for tq in range(2):
                    b0, bk0 = banks[6 + tq]
                    conv_one(6 + tq, qf[:, tq * TG:(tq + 1) * TG], "qf",
                             2 * g + tq, b0[:, 0:TG], bk0)
            if warm_tg:
                for tq in range(2):
                    b0, bk0 = banks[6 + tq]
                    slot = 6 + tq
                    e0 = ext[:, slot * EXTW:(slot + 1) * EXTW]
                    em.op("act", lambda e, b=b0, e_=e0: e.activation(
                        out=e_[:, 3:3 + 128], in_=b[:, 0:128], func=AF.Copy),
                        reads=[bk0], writes=["ext"])
                    em.op("dve", lambda e, e_=e0: e.tensor_copy(
                        e_[:, 0:3], e_[:, 128:131]), reads=["ext"], writes=["ext"])
            pks, kpks = next_bank()
            for j in range(2):
                em.op("act", lambda e, j0=j: e.activation(
                    out=k2s[:], in_=kvf[:, j0 * TG:(j0 + 1) * TG],
                    func=AF.Square), reads=["kvf"], writes=["part_sb"])
                em.op("pe", lambda e, j0=j, b=pks: e.matmul(
                    b[0:32, 0:TG], kmask[:, j0 * 32:(j0 + 1) * 32],
                    k2s[:], start=(j0 == 0), stop=(j0 == 1)),
                    reads=["kmask", "part_sb"], writes=[kpks])
            em.op("act", lambda e, b=pks: e.activation(
                out=lnsk4[0:4, 0:TG],
                in_=b[0:4, 0:TG], func=AF.Ln,
                bias=epsl2_ap[0:4, :]),
                reads=[kpks, epsl2_k], writes=["lnsk4"])
            em.op("dve", lambda e: e.tensor_scalar_mul(
                lnsk4[0:4, 0:TG],
                lnsk4[0:4, 0:TG], -0.5),
                reads=["lnsk4"], writes=["lnsk4"])
            if own_tg:
                pqs, kpqs = next_bank()
                for j in range(2):
                    em.op("act", lambda e, j0=j: e.activation(
                        out=k2s[:], in_=qf[:, j0 * TG:(j0 + 1) * TG],
                        func=AF.Square), reads=["qf"], writes=["part_sb"])
                    em.op("pe", lambda e, j0=j, b=pqs: e.matmul(
                        b[0:32, 0:TG], kmask[:, j0 * 32:(j0 + 1) * 32],
                        k2s[:], start=(j0 == 0), stop=(j0 == 1)),
                        reads=["kmask", "part_sb"], writes=[kpqs])
                em.op("act", lambda e, b=pqs: e.activation(
                    out=lnsq4[0:4, 0:TG],
                    in_=b[0:4, 0:TG], func=AF.Ln,
                    bias=epsl2_ap[0:4, :]),
                    reads=[kpqs, epsl2_k], writes=["lnsq4"])
                em.op("dve", lambda e: e.tensor_scalar_mul(
                    lnsq4[0:4, 0:TG],
                    lnsq4[0:4, 0:TG], -0.5),
                    reads=["lnsq4"], writes=["lnsq4"])
            if own_tg:
                zbanks = [next_bank() for _ in range(4)]
                for r in range(2):
                    load_xn_half(r, tg0, key="xn_g")
                    for dcl in range(8):
                        dc = 8 * r + dcl
                        wb_i = dc % 2
                        em.op("sp", lambda e, d0=dc, w0=wb_i: e.dma_start(
                            out=wz_buf[w0][:],
                            in_=w_z[d0 * 128:(d0 + 1) * 128,
                                    4 * g * 128:(4 * g + 4) * 128]),
                            writes=[f"wz{wb_i}"], dma=f"wz{wb_i}")
                        for ti in range(4):
                            zb, zbk = zbanks[ti]
                            em.op("pe", lambda e, d0=dc, dl=dcl, i0=ti, b=zb,
                                  w0=wb_i, s1=(r == 0 and dcl == 0),
                                  s2=(r == 1 and dcl == 7): e.matmul(
                                  b[:, 0:512],
                                  xn_g[:, dl * TG + i0 * 128:dl * TG + (i0 + 1) * 128],
                                  wz_buf[w0][:], start=s1, stop=s2),
                                  reads=["xn_g0", "xn_g1", f"wz{wb_i}"], writes=[zbk])
                for ti in range(4):
                    zb, zbk = zbanks[ti]
                    em.op("dve", lambda e, i0=ti, b=zb: e.tensor_copy(
                        z_sb[:, i0 * 512:(i0 + 1) * 512], b[:, 0:512]),
                        reads=[zbk], writes=["z_sb"])
            for c in range(CPG):
                em.op("dve", lambda e, c0=c, t0=tg0: e.tensor_scalar(
                    p3s_hm[:, c0 * 64:(c0 + 1) * 64],
                    gj_hm[:, t0 + c0 * 64:t0 + (c0 + 1) * 64], -1.0,
                    gj_hm[:, t0 + c0 * 64 + 63:t0 + c0 * 64 + 64],
                    ALU.mult, ALU.add), reads=["gj_hm"], writes=["p3s_hm"])
            for cp in range(CPG // 2):
                for src_ap, src_k, dst, dst_k in (
                    (gj_hm[:, tg0 + cp * 128:tg0 + (cp + 1) * 128], "gj_hm", gj_tm, "gj_tm"),
                    (p1_hm[:, tg0 + cp * 128:tg0 + (cp + 1) * 128], "p1_hm", p1_tm, "p1_tm"),
                    (p3s_hm[:, cp * 128:(cp + 1) * 128], "p3s_hm", p3_tm, "p3_tm"),
                ):
                    bank, bk = next_bank()
                    em.op("pe", lambda e, a=src_ap, b=bank: e.transpose(
                        b[0:128, 0:32], a, ident128[0:32, 0:32]),
                        reads=[src_k, "ident128"], writes=[bk])
                    em.op("dve", lambda e, b=bank, d=dst, p0=cp,
                        dk=dst_k: e.tensor_copy(
                        d[:, p0 * 32:(p0 + 1) * 32], b[0:128, 0:32]),
                        reads=[bk], writes=[dst_k])
                bank, bk = next_bank()
                em.op("pe", lambda e, p0=cp, b=bank: e.transpose(
                    b[0:128, 0:4], lnsk4[0:4, p0 * 128:(p0 + 1) * 128],
                    ident128[0:4, 0:4]), reads=["lnsk4", "ident128"], writes=[bk])
                em.op("dve", lambda e, b=bank, p0=cp: e.tensor_copy(
                    lnsk_tm[:, p0 * 4:(p0 + 1) * 4], b[0:128, 0:4]),
                    reads=[bk], writes=["lnsk_tm"])
                em.op("dve", lambda e, p0=cp: e.tensor_sub(
                    b_tm[:, p0 * 32:(p0 + 1) * 32],
                    p1_tm[:, p0 * 32:(p0 + 1) * 32],
                    gj_tm[:, p0 * 32:(p0 + 1) * 32]),
                    reads=["p1_tm", "gj_tm"], writes=["b_tm"])
                em.op("act", lambda e, p0=cp: e.activation(
                    out=b_tm[:, p0 * 32:(p0 + 1) * 32],
                    in_=b_tm[:, p0 * 32:(p0 + 1) * 32], func=AF.Exp),
                    reads=["b_tm"], writes=["b_tm"])
            for cp in range(CPG // 2):
                own_ch = (tg0 + cp * 128) >= t_pre
                em.op("dve", lambda e, p0=cp: e.tensor_add(e1b[:, 0:4],
                      p1_tm[:, p0 * 32 + 4 * g:p0 * 32 + 4 * g + 4],
                      lnsk_tm[:, p0 * 4:(p0 + 1) * 4]),
                      reads=["p1_tm", "lnsk_tm"], writes=["e1b"])
                em.op("act", lambda e: e.activation(out=e1b[:, 0:4],
                      in_=e1b[:, 0:4], func=AF.Exp), reads=["e1b"],
                      writes=["e1b"])
                em.op("dve", lambda e, p0=cp: e.tensor_add(e5b[:, 0:4],
                      p3_tm[:, p0 * 32 + 4 * g:p0 * 32 + 4 * g + 4],
                      lnsk_tm[:, p0 * 4:(p0 + 1) * 4]),
                      reads=["p3_tm", "lnsk_tm"], writes=["e5b"])
                em.op("act", lambda e: e.activation(out=e5b[:, 0:4],
                      in_=e5b[:, 0:4], func=AF.Exp), reads=["e5b"],
                      writes=["e5b"])
                for hl in range(4):
                    emit_pair(g, hl, cp, tg0, own_ch)
                if own_ch:
                    flush_tile(g, tg0, cp)

    def emit_pair(g, hl, cp, tg0, own_ch):
        h = 4 * g + hl
        j = hl // 2
        S_h = S_sb[:, hl * 128:(hl + 1) * 128]
        sk = f"S{hl}"
        selp = sel_sb[:, hl * 128:hl * 128 + 64]
        selp128 = sel_sb[:, hl * 128:(hl + 1) * 128]
        selg = selg_sb[0:4, hl * 128:hl * 128 + 64]
        selg128 = selg_sb[0:4, hl * 128:(hl + 1) * 128]
        selgn = selgn_sb[0:4, hl * 128:hl * 128 + 64]
        pgs = []
        for c2 in range(2):
            c = 2 * cp + c2
            cg = tg0 // 64 + c
            o = 64 * c2
            ksl = kvf[:, j * TG + c * 64:j * TG + (c + 1) * 64]
            gslice = gj_hm[:, cg * 64:(cg + 1) * 64]
            p1slice = p1_hm[:, cg * 64:(cg + 1) * 64]
            lkslice = lnsk4[0:4, c * 64:(c + 1) * 64]
            gjcol = gj_tm[o:o + 64, cp * 32 + h:cp * 32 + h + 1]
            p1col = p1_tm[o:o + 64, cp * 32 + h:cp * 32 + h + 1]
            lkcol = lnsk_tm[o:o + 64, cp * 4 + hl:cp * 4 + hl + 1]
            dtq = dtile[o:o + 64, 0:64]
            pg, kpg = next_bank()
            pgs.append((pg, kpg, o))
            em.op("pe", lambda e, b=pg, q=o, a=ksl: e.matmul(
                  b[q:q + 64, 0:64], a, a, start=True, stop=True),
                  reads=["kvf"], writes=[kpg])
            pB, kpB = next_bank()
            em.op("pe", lambda e, b=pB, q=o, s1=gslice, s2=lkslice: (
                e.matmul(b[q:q + 64, 0:64], selp, s1, start=True, stop=False),
                e.matmul(b[q:q + 64, 0:64], selgn, s2, start=False, stop=True))[-1],
                reads=["sel_sb", "selgn_sb", "gj_hm", "lnsk4"], writes=[kpB])
            em.op("dve", lambda e, b=pB, q=o, d=dtq, s1=p1col, s2=lkcol:
                  e.tensor_scalar(d, b[q:q + 64, 0:64], s1, s2,
                  ALU.subtract, ALU.subtract),
                  reads=[kpB, "p1_tm", "lnsk_tm"], writes=["dtile"])
            em.op("dve", lambda e, d=dtq, q=o: e.tensor_mul(d, d,
                  mneg_B[q:q + 64, :]), reads=["dtile", "masks"], writes=["dtile"])
            em.op("act", lambda e, d=dtq: e.activation(out=d, in_=d,
                  func=AF.Exp), reads=["dtile"], writes=["dtile"])
            em.op("dve", lambda e, d=dtq, q=o: e.tensor_mul(d, d,
                  mneg_B[q:q + 64, :]), reads=["dtile", "masks"], writes=["dtile"])
            em.op("dve", lambda e, b=pg, q=o, d=dtq: e.tensor_mul(
                  Bpr[0][q:q + 64, q:q + 64], b[q:q + 64, 0:64], d),
                  reads=[kpg, "dtile"], writes=["Bpr0"])
            pBt, kpBt = next_bank()
            em.op("pe", lambda e, b=pBt, q=o, s1=p1slice, s2=lkslice: (
                e.matmul(b[q:q + 64, 0:64], selp, s1, start=True, stop=False),
                e.matmul(b[q:q + 64, 0:64], selg, s2, start=False, stop=True))[-1],
                reads=["sel_sb", "selg_sb", "p1_hm", "lnsk4"], writes=[kpBt])
            em.op("dve", lambda e, b=pBt, q=o, d=dtq, s1=gjcol, s2=lkcol:
                  e.tensor_scalar(d, b[q:q + 64, 0:64], s1, s2,
                  ALU.subtract, ALU.add),
                  reads=[kpBt, "gj_tm", "lnsk_tm"], writes=["dtile"])
            em.op("dve", lambda e, d=dtq, q=o: e.tensor_mul(d, d,
                  m01_Bt[q:q + 64, :]), reads=["dtile", "masks"], writes=["dtile"])
            em.op("act", lambda e, d=dtq: e.activation(out=d, in_=d,
                  func=AF.Exp), reads=["dtile"], writes=["dtile"])
            em.op("dve", lambda e, d=dtq, q=o: e.tensor_mul(d, d,
                  mneg_Bt[q:q + 64, :]), reads=["dtile", "masks"], writes=["dtile"])
            em.op("dve", lambda e, b=pg, q=o, d=dtq: e.tensor_mul(
                  Btpr[0][q:q + 64, q:q + 64], b[q:q + 64, 0:64], d),
                  reads=[kpg, "dtile"], writes=["Btpr0"])
        em.op("dve", lambda e: e.tensor_add(Rpr[:], Btpr[0][:], ident128[:]),
              reads=["Btpr0", "ident128"], writes=["Rpr"])
        cur = 0
        for step in range(5):
            nxt = 1 - cur
            pB2, kpB2 = next_bank()
            em.op("pe", lambda e, b=pB2, c0=cur: e.matmul(b[:, 0:128],
                  Btpr[c0][:], Bpr[c0][:], start=True, stop=True),
                  reads=[f"Btpr{cur}", f"Bpr{cur}"], writes=[kpB2])
            pBt2, kpBt2 = next_bank()
            em.op("pe", lambda e, b=pBt2, c0=cur: e.matmul(b[:, 0:128],
                  Bpr[c0][:], Btpr[c0][:], start=True, stop=True),
                  reads=[f"Bpr{cur}", f"Btpr{cur}"], writes=[kpBt2])
            em.op("dve", lambda e, b=pB2, c0=nxt: e.tensor_copy(Bpr[c0][:],
                  b[:, 0:128]), reads=[kpB2], writes=[f"Bpr{nxt}"])
            em.op("dve", lambda e, b=pBt2, c0=nxt: e.tensor_copy(Btpr[c0][:],
                  b[:, 0:128]), reads=[kpBt2], writes=[f"Btpr{nxt}"])
            pRu, kpRu = next_bank()
            em.op("pe", lambda e, b=pRu, c0=nxt: e.matmul(b[:, 0:128],
                  Bpr[c0][:], Rpr[:], start=True, stop=True),
                  reads=[f"Bpr{nxt}", "Rpr"], writes=[kpRu])
            em.op("dve", lambda e, b=pRu: e.tensor_add(Rpr[:], Rpr[:],
                  b[:, 0:128]), reads=[kpRu, "Rpr"], writes=["Rpr"])
            cur = nxt
        kslp = kvf[:, j * TG + cp * 128:j * TG + (cp + 1) * 128]
        vslp = kvf[:, (2 + hl) * TG + cp * 128:(2 + hl) * TG + (cp + 1) * 128]
        pkt, kpkt = next_bank()
        em.op("pe", lambda e, b=pkt, a=kslp: e.transpose(
              b[0:128, 0:128], a, ident128[:]),
              reads=["kvf", "ident128"], writes=[kpkt])
        em.op("dve", lambda e, b=pkt: e.tensor_scalar(
              kts[:], b[0:128, 0:128], e5b[0:128, hl:hl + 1], None, ALU.mult),
              reads=[kpkt, "e5b"], writes=["kts"])
        pvt, kpvt = next_bank()
        em.op("pe", lambda e, b=pvt, a=vslp: e.transpose(
              b[0:128, 0:128], a, ident128[:]),
              reads=["kvf", "ident128"], writes=[kpvt])
        em.op("dve", lambda e, b=pvt: e.tensor_scalar(
              Pt[:], b[0:128, 0:128],
              b_tm[0:128, cp * 32 + h:cp * 32 + h + 1], None, ALU.mult),
              reads=[kpvt, "b_tm"], writes=["Pt"])
        for c2 in range(2):
            c = 2 * cp + c2
            cg = tg0 // 64 + c
            o = 64 * c2
            ksl = kvf[:, j * TG + c * 64:j * TG + (c + 1) * 64]
            e1col = e1b[o:o + 64, hl:hl + 1]
            eglcol = eglrep[:, hl * NCH + cg:hl * NCH + cg + 1]
            pP, kpP = next_bank()
            em.op("pe", lambda e, b=pP, q=o, a=ksl: e.matmul(
                  b[q:q + 64, 0:128], a, S_h, start=True, stop=True),
                  reads=["kvf", sk], writes=[kpP])
            em.op("dve", lambda e, b=pP, q=o, s1=e1col: e.tensor_scalar(
                  u1[q:q + 64, :], b[q:q + 64, 0:128], s1, None, ALU.mult),
                  reads=[kpP, "e1b"], writes=["u1"])
            em.op("dve", lambda e, q=o: e.tensor_sub(
                  Pt[q:q + 64, :], Pt[q:q + 64, :], u1[q:q + 64, :]),
                  reads=["Pt", "u1"], writes=["Pt"])
            pvn, kpvn = next_bank()
            em.op("pe", lambda e, b=pvn, q=o: e.matmul(
                  b[q:q + 64, 0:128], Rpr[q:q + 64, q:q + 64],
                  Pt[q:q + 64, :], start=True, stop=True),
                  reads=["Rpr", "Pt"], writes=[kpvn])
            em.op("dve", lambda e, b=pvn, q=o: e.tensor_copy(
                  vnew[q:q + 64, :], b[q:q + 64, 0:128]),
                  reads=[kpvn], writes=["vnew"])
            if own_ch:
                emit_own(g, hl, cg, c, cp, c2, tg0, ksl, S_h, sk)
            pSu, kpSu = next_bank()
            em.op("pe", lambda e, b=pSu, q=o: e.matmul(b[0:128, 0:128],
                  kts[q:q + 64, :], vnew[q:q + 64, :], start=True, stop=True),
                  reads=["kts", "vnew"], writes=[kpSu])
            em.op("dve", lambda e, b=pSu, s1=eglcol: e.scalar_tensor_tensor(
                  S_h, S_h, s1, b[0:128, 0:128], ALU.mult, ALU.add),
                  reads=[sk, "eglrep", kpSu], writes=[sk])
        if "Spost" in dbg and g == 0 and hl == 0 and cp == 0:
            em.op("pool", lambda e: e.dma_start(out=dbg["Spost"][:], in_=S_h),
                  reads=[sk], dma="dbg_Spost")

    def emit_own(g, hl, cg, c, ti, co2, tg0, ksl, S_h, sk):
        h = 4 * g + hl
        jq = hl // 2
        o = 64 * co2
        qsl = qf[:, jq * TG + c * 64:jq * TG + (c + 1) * 64]
        gslice = gj_hm[:, cg * 64:(cg + 1) * 64]
        lqslice = lnsq4[0:4, c * 64:(c + 1) * 64]
        gjcol = gj_tm[o:o + 64, ti * 32 + h:ti * 32 + h + 1]
        lkcol = lnsk_tm[o:o + 64, ti * 4 + hl:ti * 4 + hl + 1]
        selp = sel_sb[:, hl * 128:hl * 128 + 64]
        selp128 = sel_sb[:, hl * 128:(hl + 1) * 128]
        selg = selg_sb[0:4, hl * 128:hl * 128 + 64]
        selg128 = selg_sb[0:4, hl * 128:(hl + 1) * 128]
        dtq = dtile[o:o + 64, 0:64]
        pq, kpq = next_bank()
        em.op("pe", lambda e, b=pq, q=o, a=ksl, a2=qsl: e.matmul(
              b[q:q + 64, 0:64], a, a2, start=True, stop=True),
              reads=["kvf", "qf"], writes=[kpq])
        pDi, kpDi = next_bank()
        em.op("pe", lambda e, b=pDi, q=o, s1=gslice, s2=lqslice: (
            e.matmul(b[q:q + 64, 0:64], selp, s1, start=True, stop=False),
            e.matmul(b[q:q + 64, 0:64], selg, s2, start=False, stop=True))[-1],
            reads=["sel_sb", "selg_sb", "gj_hm", "lnsq4"], writes=[kpDi])
        em.op("dve", lambda e, b=pDi, q=o, d=dtq, s1=gjcol, s2=lkcol:
              e.tensor_scalar(d, b[q:q + 64, 0:64], s1, s2,
              ALU.subtract, ALU.add),
              reads=[kpDi, "gj_tm", "lnsk_tm"], writes=["dtile"])
        em.op("dve", lambda e, d=dtq, q=o: e.tensor_mul(d, d,
              m01_iT[q:q + 64, :]), reads=["dtile", "masks"], writes=["dtile"])
        em.op("act", lambda e, d=dtq, q=o, b_=negln_ap: e.activation(out=d,
              in_=d, func=AF.Exp, bias=b_[q:q + 64, :]),
              reads=["dtile", negln_k], writes=["dtile"])
        em.op("dve", lambda e, d=dtq, q=o: e.tensor_mul(d, d,
              m01_iT[q:q + 64, :]), reads=["dtile", "masks"], writes=["dtile"])
        em.op("dve", lambda e, b=pq, q=o, d=dtq: e.tensor_mul(
              intraT[q:q + 64, :], b[q:q + 64, 0:64], d),
              reads=[kpq, "dtile"], writes=["intraT"])
        pQ, kpQ = next_bank()
        em.op("pe", lambda e, b=pQ, s1=gslice, s2=lqslice: (
            e.matmul(b[0:128, 0:64], selp128, s1, start=True, stop=False),
            e.matmul(b[0:128, 0:64], selg128, s2, start=False, stop=True))[-1],
            reads=["sel_sb", "selg_sb", "gj_hm", "lnsq4"], writes=[kpQ])
        em.op("act", lambda e, b=pQ, b_=negln_ap: e.activation(out=qsc[:],
              in_=b[0:128, 0:64], func=AF.Exp, bias=b_),
              reads=[kpQ, negln_k], writes=["qsc"])
        em.op("dve", lambda e, a=qsl: e.tensor_mul(qs[:], a, qsc[:]),
              reads=["qf", "qsc"], writes=["qs"])
        pO, kpO = next_bank()
        em.op("pe", lambda e, b=pO, q=o: (
            e.matmul(b[q:q + 64, 0:128], qs[:], S_h, start=True, stop=False),
            e.matmul(b[q:q + 64, 0:128], intraT[q:q + 64, :],
                     vnew[q:q + 64, :], start=False, stop=True))[-1],
            reads=["qs", sk, "intraT", "vnew"], writes=[kpO])
        em.op("act", lambda e, q=o, i0=ti, h0=hl: e.activation(
            out=zsl[q:q + 64, 0:128],
            in_=z_sb[q:q + 64, i0 * 512 + h0 * 128:i0 * 512 + (h0 + 1) * 128],
            func=AF.Silu), reads=["z_sb"], writes=["zsl"])
        em.op("dve", lambda e, b=pO, q=o: e.tensor_mul(
            xg[q:q + 64, :], b[q:q + 64, 0:128], zsl[q:q + 64, 0:128]),
            reads=[kpO, "zsl"], writes=["xg"])
        em.op("act", lambda e, q=o: e.activation(
            out=zsl[q:q + 64, 0:128], in_=xg[q:q + 64, :], func=AF.Square,
            accum_out=rmsc[q:q + 64, 0:1]),
            reads=["xg"], writes=["zsl", "rmsc"])
        em.op("act", lambda e, q=o, b_=epsrms_ap: e.activation(
            out=rmsc[q:q + 64, 1:2], in_=rmsc[q:q + 64, 0:1], func=AF.Sqrt,
            scale=1.0 / UV, bias=b_[q:q + 64, :]),
            reads=["rmsc", epsrms_k], writes=["rmsc"])
        em.op("dve", lambda e, q=o: e.reciprocal(
            rmsc[q:q + 64, 1:2], rmsc[q:q + 64, 1:2]),
            reads=["rmsc"], writes=["rmsc"])
        em.op("dve", lambda e, q=o: e.tensor_scalar(
            xg[q:q + 64, :], xg[q:q + 64, :], rmsc[q:q + 64, 1:2], None,
            ALU.mult), reads=["xg", "rmsc"], writes=["xg"])
        em.op("dve", lambda e, q=o, h0=hl: e.tensor_mul(
            normed_sb[q:q + 64, h0 * 128:(h0 + 1) * 128],
            xg[q:q + 64, :], nw_tile[q:q + 64, :]),
            reads=["xg", "nw_tile"], writes=["normed_sb"])
        if "qs" in dbg and g == 0 and hl == 0 and c == 1:
            for nm, ap, key in [
                ("qs", qs[:], "qs"), ("qsc", qsc[:], "qsc"),
                ("dtile", dtile[:], "dtile"),
                ("intraT", intraT[:], "intraT"), ("vnew", vnew[:], "vnew"),
                ("normed", normed_sb[:, 0:128], "normed_sb"),
                ("Pt", Pt[:], "Pt"), ("Rpr", Rpr[:], "Rpr"),
                ("kts", kts[:], "kts"),
            ]:
                em.op("pool", lambda e, a=ap, n=nm: e.dma_start(
                    out=dbg[n][:], in_=a), reads=[key], dma="dbg_" + nm)

    def flush_tile(g, tg0, ti):
        trow0 = tg0 + ti * 128 - t_pre
        for hl in range(4):
            bank, bk = next_bank()
            em.op("pe", lambda e, h0=hl, b=bank: e.transpose(
                b[0:128, 0:128], normed_sb[:, h0 * 128:(h0 + 1) * 128],
                ident128[:]), reads=["normed_sb", "ident128"], writes=[bk])
            em.op("dve", lambda e, h0=hl, b=bank: e.tensor_copy(
                normT[:, h0 * 128:(h0 + 1) * 128], b[0:128, 0:128]),
                reads=[bk], writes=["normT"])
        obanks = [next_bank() for _ in range(4)]
        for hl in range(4):
            for dq in range(4):
                em.op("sp", lambda e, h0=hl, q0=dq: e.dma_start(
                    out=wout_buf[:],
                    in_=w_out[(4 * g + h0) * 128:(4 * g + h0 + 1) * 128,
                              q0 * 512:(q0 + 1) * 512]),
                    writes=["wout_buf"], dma="wol")
                ob, obk = obanks[dq]
                em.op("pe", lambda e, h0=hl, b=ob: e.matmul(
                    b[:, 0:512], normT[:, h0 * 128:(h0 + 1) * 128],
                    wout_buf[:, 0:512],
                    start=(h0 == 0), stop=(h0 == 3)),
                    reads=["normT", "wout_buf"], writes=[obk])
        for dq in range(4):
            ob, obk = obanks[dq]
            em.op("dve", lambda e, b=ob: e.tensor_copy(part_sb[:], b[:, 0:512]),
                  reads=[obk], writes=["part_sb"])
            em.op("pool", lambda e, q0=dq, t0=trow0: e.dma_start(
                out=attn_part[g, t0:t0 + 128, q0 * 512:(q0 + 1) * 512],
                in_=part_sb[:]), reads=["part_sb"], writes=["attn_part"],
                dma="partw")

    for g in range(NG):
        emit_group(g)
        em.barrier()

    em.barrier()
    NTO = t_own // 128
    TP0 = t_pre // 128

    fin_b = wout_buf[0:128, 0:512]
    gwt_sb = conv_sb[0:128, 0:DCH * 16]
    for dc in range(DCH):
        em.op("sp", lambda e, d0=dc: e.dma_start(
            out=gwt_sb[:, d0 * 16:(d0 + 1) * 16],
            in_=gate_wt[d0 * 128:(d0 + 1) * 128, :]), writes=["gwt_sb"],
            dma="gwt")
    comb_sb = normed_sb[0:128, 0:128]
    combT_h = [S_sb[0:16, 0:512], normT[0:16, 0:512]]
    rt_s = eglrep[0:128, max(4 * NCH, 96) - 96:max(4 * NCH, 96) - 16]
    rt_b = wz_buf[0][0:128, 0:256]
    rt_c = wz_buf[1][0:128, 0:256]
    iota_g = eglrep[0:128, max(4 * NCH, 96) - 16:max(4 * NCH, 96)]
    iota_e = normed_sb[0:128, 128:384]
    em.op("pool", lambda e: e.affine_select(
        iota_g[:], ones128[:, 0:16], [[1, 4], [-1, 4]], ALU.is_gt, 0.0,
        channel_multiplier=0), reads=["ones128"], writes=["iota_g"])
    em.op("pool", lambda e: e.memset(iota_e[:], 1.0), writes=["iota_e"])
    em.op("pool", lambda e: e.affine_select(
        iota_e[:], iota_e[:], [[1, 16], [-1, 16]],
        ALU.is_gt, 0.0, channel_multiplier=0),
        reads=["iota_e"], writes=["iota_e"])
    xn_b = z_sb[0:128, 0:D]
    assert NTO <= 8

    AX = mybir.AxisListType.X
    for it in range(NTO):
        em.op("sp", lambda e, i0=it: e.dma_start(
            out=xn_b[:], in_=xn_hbm[TP0 + i0]),
            reads=["xn_hbm"], writes=["xn_b"], dma="xnb")
        psc, kpsc = next_bank()
        for dc in range(DCH):
            em.op("pe", lambda e, d0=dc, b=psc: e.matmul(
                b[:, 0:16], xn_b[:, d0 * 128:(d0 + 1) * 128],
                gwt_sb[:, d0 * 16:(d0 + 1) * 16],
                start=(d0 == 0), stop=(d0 == DCH - 1)),
                reads=["xn_b", "gwt_sb"], writes=[kpsc])
        sc = rt_s[0:128, 0:16]
        gmx = rt_s[0:128, 16:20]
        cnt4 = rt_s[0:128, 20:24]
        msc = rt_s[0:128, 32:48]
        cnt16 = rt_s[0:128, 48:64]
        den = rt_s[0:128, 64:65]
        em.op("act", lambda e, b=psc: e.activation(out=sc, in_=b[:, 0:16],
              func=AF.Sigmoid), reads=[kpsc], writes=["rt_s"])
        em.op("dve", lambda e: e.reduce_max(gmx, sc.rearrange(
              "p (a b) -> p a b", b=4), axis=AX), reads=["rt_s"], writes=["rt_s"])
        em.op("dve", lambda e: e.tensor_tensor(
              rt_b[:, 0:16].rearrange("p (i j) -> p i j", j=4),
              gmx.unsqueeze(1).broadcast_to((128, 4, 4)),
              gmx.unsqueeze(2).broadcast_to((128, 4, 4)), ALU.is_gt),
              reads=["rt_s"], writes=["rt_b"])
        em.op("dve", lambda e: e.tensor_tensor(
              rt_b[:, 16:32].rearrange("p (i j) -> p i j", j=4),
              gmx.unsqueeze(1).broadcast_to((128, 4, 4)),
              gmx.unsqueeze(2).broadcast_to((128, 4, 4)), ALU.is_equal),
              reads=["rt_s"], writes=["rt_b"])
        em.op("dve", lambda e: e.tensor_mul(rt_b[:, 16:32], rt_b[:, 16:32],
              iota_g[:]), reads=["rt_b", "iota_g"], writes=["rt_b"])
        em.op("dve", lambda e: e.tensor_add(rt_b[:, 0:16], rt_b[:, 0:16],
              rt_b[:, 16:32]), reads=["rt_b"], writes=["rt_b"])
        em.op("dve", lambda e: e.reduce_sum(cnt4, rt_b[:, 0:16].rearrange(
              "p (i j) -> p i j", j=4), axis=AX), reads=["rt_b"], writes=["rt_s"])
        em.op("dve", lambda e: e.tensor_single_scalar(cnt4, cnt4, 1.5,
              ALU.is_lt), reads=["rt_s"], writes=["rt_s"])
        em.op("dve", lambda e: e.tensor_tensor(
              msc.rearrange("p (a b) -> p a b", b=4),
              sc.rearrange("p (a b) -> p a b", b=4),
              cnt4.unsqueeze(2).broadcast_to((128, 4, 4)), ALU.mult),
              reads=["rt_s"], writes=["rt_s"])
        em.op("dve", lambda e: e.tensor_tensor(
              rt_b[:, 0:256].rearrange("p (i j) -> p i j", j=16),
              msc.unsqueeze(1).broadcast_to((128, 16, 16)),
              msc.unsqueeze(2).broadcast_to((128, 16, 16)), ALU.is_gt),
              reads=["rt_s"], writes=["rt_b"])
        eqb = rt_c[0:128, 0:256]
        em.op("dve", lambda e, s=eqb: e.tensor_tensor(
              s.rearrange("p (i j) -> p i j", j=16),
              msc.unsqueeze(1).broadcast_to((128, 16, 16)),
              msc.unsqueeze(2).broadcast_to((128, 16, 16)), ALU.is_equal),
              reads=["rt_s"], writes=["rt_c"])
        em.op("dve", lambda e, s=eqb: e.tensor_mul(s, s, iota_e[:]),
              reads=["rt_c", "iota_e"], writes=["rt_c"])
        em.op("dve", lambda e, s=eqb: e.tensor_add(rt_b[:, 0:256],
              rt_b[:, 0:256], s), reads=["rt_b", "rt_c"], writes=["rt_b"])
        em.op("dve", lambda e: e.reduce_sum(cnt16, rt_b[:, 0:256].rearrange(
              "p (i j) -> p i j", j=16), axis=AX), reads=["rt_b"], writes=["rt_s"])
        em.op("dve", lambda e: e.tensor_single_scalar(cnt16, cnt16, 3.5,
              ALU.is_lt), reads=["rt_s"], writes=["rt_s"])
        em.op("dve", lambda e, i0=it: e.tensor_mul(
              comb_sb[:, i0 * 16:(i0 + 1) * 16], sc, cnt16),
              reads=["rt_s"], writes=["comb_sb"])
        em.op("dve", lambda e, i0=it: e.reduce_sum(den,
              comb_sb[:, i0 * 16:(i0 + 1) * 16], axis=AX),
              reads=["comb_sb"], writes=["rt_s"])
        em.op("dve", lambda e: e.reciprocal(den, den),
              reads=["rt_s"], writes=["rt_s"])
        em.op("dve", lambda e, i0=it: e.tensor_scalar(
              comb_sb[:, i0 * 16:(i0 + 1) * 16],
              comb_sb[:, i0 * 16:(i0 + 1) * 16], den, None, ALU.mult),
              reads=["comb_sb", "rt_s"], writes=["comb_sb"])
        bank, bk = next_bank()
        em.op("pe", lambda e, i0=it, b=bank: e.transpose(
            b[0:16, 0:128], comb_sb[:, i0 * 16:(i0 + 1) * 16],
            ident128[:]), reads=["comb_sb", "ident128"], writes=[bk])
        em.op("dve", lambda e, i0=it, b=bank: e.tensor_copy(
            combT_h[i0 // 4][0:16, (i0 % 4) * 128:(i0 % 4 + 1) * 128],
            b[0:16, 0:128]),
            reads=[bk], writes=["combT"])

    em.barrier()
    xn_own = wkv_sb[0:128, 0:8192]
    yslot = wkv_sb[0:128, 8192:16384]
    w13_b = [wq_sb[0:128, i * 512:(i + 1) * 512] for i in range(4)]
    wrep = part_sb[0:128, 0:512]
    sel16 = S_sb[0:16, 512:512] if False else None
    sel16 = xg[0:16, 0:128]
    gexp_slots = []
    for nm, tnsr, ncols in (("xng", xn_g, 4096), ("kvf", kvf, 3072)):
        for s0 in range(0, ncols - 511, 512):
            if len(gexp_slots) < 11:
                gexp_slots.append((tnsr[0:128, s0:s0 + 512], f"gx_{nm}_{s0}"))
    gsh_slots = []
    for nm, tnsr, ncols in (("qf", qf, 1024), ("zb", z_sb, 2048),
                            ("ext", ext, 4128), ("xng", xn_g, 4096),
                            ("kvf", kvf, 3072)):
        for s0 in range(0, ncols - 511, 512):
            if len(gsh_slots) < NSH // 128:
                gsh_slots.append((tnsr[0:128, s0:s0 + 512], f"gsh_{nm}_{s0}"))
    NSH_CH = NSH // 128

    def y_add(tq, dq, pbank, pkey, first):
        dst = yslot[0:128, tq * 2048 + dq * 512:tq * 2048 + (dq + 1) * 512]
        if first:
            em.op("dve", lambda e, b=pbank: e.tensor_copy(
                dst, b[:, 0:512]), reads=[pkey], writes=["yslot"])
        else:
            em.op("dve", lambda e, b=pbank: e.tensor_add(
                dst, dst, b[:, 0:512]), reads=[pkey, "yslot"], writes=["yslot"])

    for half in range(t_own // 512):
        em.barrier()
        th0 = half * 512
        for dc in range(DCH):
            for ti in range(4):
                em.op(("sp", "act")[dc % 2],
                    lambda e, d0=dc, i0=ti, h0=half: e.dma_start(
                    out=xn_own[0:128, d0 * 512 + i0 * 128:d0 * 512 + (i0 + 1) * 128],
                    in_=xn_hbm[TP0 + h0 * 4 + i0, :, d0 * 128:(d0 + 1) * 128]),
                    reads=["xn_hbm"], writes=["xn_own" + str(dc % 2)],
                    dma="xnown" + str(dc % 2))
        for ex in range(NE):
            em.op("pool", lambda e, e0=ex: e.affine_select(
                sel16[0:16, :], ones128[0:16, :], [[0, 128]], ALU.is_equal,
                0.0, base=-e0, channel_multiplier=1),
                reads=["ones128"], writes=["sel16"])
            pwr, kpwr = next_bank()
            em.op("pe", lambda e, b=pwr, h0=half: e.matmul(b[0:128, 0:512],
                  sel16[0:16, :], combT_h[h0][0:16, 0:512],
                  start=True, stop=True),
                  reads=["sel16", "combT"], writes=[kpwr])
            em.op("dve", lambda e, b=pwr: e.tensor_copy(wrep[:], b[0:128, 0:512]),
                  reads=[kpwr], writes=["wrep"])
            for ic in range(11):
                p1, kp1 = next_bank()
                p3, kp3 = next_bank()
                for dc in range(DCH):
                    wb = w13_b[dc % 2]
                    wb3 = w13_b[2 + dc % 2]
                    em.op(("sp", "act")[dc % 2],
                          lambda e, e0=ex, d0=dc, i0=ic, w=wb:
                          e.dma_start(out=w[0:128, 0:128],
                          in_=w1t[e0, d0 * 128:(d0 + 1) * 128,
                                  i0 * 128:(i0 + 1) * 128]),
                          writes=[f"w13_{dc % 2}"], dma=f"w13l{dc % 2}")
                    em.op(("pool", "sp")[dc % 2],
                          lambda e, e0=ex, d0=dc, i0=ic, w=wb3:
                          e.dma_start(out=w[0:128, 0:128],
                          in_=w3t[e0, d0 * 128:(d0 + 1) * 128,
                                  i0 * 128:(i0 + 1) * 128]),
                          writes=[f"w33_{dc % 2}"], dma=f"w33l{dc % 2}")
                    em.op("pe", lambda e, d0=dc, b=p1, w=wb: e.matmul(
                          b[:, 0:512], w[0:128, 0:128],
                          xn_own[0:128, d0 * 512:(d0 + 1) * 512],
                          start=(d0 == 0), stop=(d0 == DCH - 1)),
                          reads=[f"w13_{dc % 2}", "xn_own0", "xn_own1"], writes=[kp1])
                    em.op("pe", lambda e, d0=dc, b=p3, w=wb3: e.matmul(
                          b[:, 0:512], w[0:128, 0:128],
                          xn_own[0:128, d0 * 512:(d0 + 1) * 512],
                          start=(d0 == 0), stop=(d0 == DCH - 1)),
                          reads=[f"w33_{dc % 2}", "xn_own0", "xn_own1"], writes=[kp3])
                gsl, gslk = gexp_slots[ic]
                em.op("act", lambda e, b=p1, s=gsl: e.activation(out=s,
                      in_=b[:, 0:512], func=AF.Silu), reads=[kp1],
                      writes=[gslk])
                em.op("dve", lambda e, b=p3, s=gsl: e.tensor_mul(s, s,
                      b[:, 0:512]), reads=[kp3, gslk], writes=[gslk])
                em.op("dve", lambda e, s=gsl: e.tensor_mul(s, s, wrep[:]),
                      reads=["wrep", gslk], writes=[gslk])
            for dq in range(4):
                ybanks = [next_bank() for _ in range(4)]
                for ic in range(11):
                    wb = w13_b[ic % 2]
                    em.op(("sp", "act")[ic % 2],
                          lambda e, e0=ex, i0=ic, q0=dq, w=wb:
                          e.dma_start(out=w[0:128, 0:512],
                          in_=w2t[e0, i0 * 128:(i0 + 1) * 128,
                                  q0 * 512:(q0 + 1) * 512]),
                          writes=[f"w13_{ic % 2}"], dma=f"w13l{ic % 2}")
                    gsl, gslk = gexp_slots[ic]
                    for tq in range(4):
                        py, kpy = ybanks[tq]
                        em.op("pe", lambda e, t0=tq, b=py, w=wb, s=gsl,
                              i0=ic: e.matmul(
                              b[:, 0:512],
                              s[0:128, t0 * 128:(t0 + 1) * 128],
                              w[0:128, 0:512],
                              start=(i0 == 0), stop=(i0 == 10)),
                              reads=[f"w13_{ic % 2}", gslk], writes=[kpy])
                for tq in range(4):
                    py, kpy = ybanks[tq]
                    y_add(tq, dq, py, kpy, ex == 0)
        em.barrier()
        for ic in range(NSH_CH):
            p1, kp1 = next_bank()
            p3, kp3 = next_bank()
            for dc in range(DCH):
                wb = w13_b[dc % 2]
                wb3 = w13_b[2 + dc % 2]
                em.op(("sp", "act")[dc % 2], lambda e, d0=dc, i0=ic, w=wb:
                      e.dma_start(out=w[0:128, 0:128],
                      in_=ws1t[d0 * 128:(d0 + 1) * 128, i0 * 128:(i0 + 1) * 128]),
                      writes=[f"w13_{dc % 2}"], dma=f"w13l{dc % 2}")
                em.op(("pool", "sp")[dc % 2], lambda e, d0=dc, i0=ic, w=wb3:
                      e.dma_start(out=w[0:128, 0:128],
                      in_=ws3t[d0 * 128:(d0 + 1) * 128, i0 * 128:(i0 + 1) * 128]),
                      writes=[f"w33_{dc % 2}"], dma=f"w33l{dc % 2}")
                em.op("pe", lambda e, d0=dc, b=p1, w=wb: e.matmul(
                      b[:, 0:512], w[0:128, 0:128],
                      xn_own[0:128, d0 * 512:(d0 + 1) * 512],
                      start=(d0 == 0), stop=(d0 == DCH - 1)),
                      reads=[f"w13_{dc % 2}", "xn_own0", "xn_own1"], writes=[kp1])
                em.op("pe", lambda e, d0=dc, b=p3, w=wb3: e.matmul(
                      b[:, 0:512], w[0:128, 0:128],
                      xn_own[0:128, d0 * 512:(d0 + 1) * 512],
                      start=(d0 == 0), stop=(d0 == DCH - 1)),
                      reads=[f"w33_{dc % 2}", "xn_own0", "xn_own1"], writes=[kp3])
            gs, gk = gsh_slots[ic]
            em.op("act", lambda e, b=p1, s=gs: e.activation(out=s,
                  in_=b[:, 0:512], func=AF.Silu), reads=[kp1], writes=[gk])
            em.op("dve", lambda e, b=p3, s=gs: e.tensor_mul(s, s, b[:, 0:512]),
                  reads=[kp3, gk], writes=[gk])
        for dq in range(4):
            ybanks = [next_bank() for _ in range(4)]
            for ic in range(NSH_CH):
                wb = w13_b[ic % 2]
                gs, gk = gsh_slots[ic]
                em.op(("sp", "act")[ic % 2], lambda e, i0=ic, q0=dq, w=wb:
                      e.dma_start(out=w[0:128, 0:512],
                      in_=ws2t[i0 * 128:(i0 + 1) * 128, q0 * 512:(q0 + 1) * 512]),
                      writes=[f"w13_{ic % 2}"], dma=f"w13l{ic % 2}")
                for tq in range(4):
                    py, kpy = ybanks[tq]
                    em.op("pe", lambda e, i0=ic, t0=tq, b=py, w=wb, s=gs:
                          e.matmul(b[:, 0:512],
                          s[0:128, t0 * 128:(t0 + 1) * 128], w[0:128, 0:512],
                          start=(i0 == 0), stop=(i0 == NSH_CH - 1)),
                          reads=[f"w13_{ic % 2}", gk], writes=[kpy])
            for tq in range(4):
                py, kpy = ybanks[tq]
                y_add(tq, dq, py, kpy, False)
        em.barrier()
        fin_a = xn_own[0:128, 0:D]
        for ti in range(4):
            it = half * 4 + ti
            em.op("sp", lambda e, i0=it: e.dma_start(
                out=fin_a[:], in_=hidden[t_pre + i0 * 128:t_pre + (i0 + 1) * 128, :]),
                writes=["fin_a"], dma="finx")
            for g in range(NG):
                for dq in range(4):
                    em.op("sp", lambda e, i0=it, g0=g, q0=dq: e.dma_start(
                        out=fin_b[:],
                        in_=attn_part[g0, i0 * 128:(i0 + 1) * 128,
                                      q0 * 512:(q0 + 1) * 512]),
                        reads=["attn_part"], writes=["fin_b"], dma="finp")
                    em.op("dve", lambda e, q0=dq: e.tensor_add(
                        fin_a[:, q0 * 512:(q0 + 1) * 512],
                        fin_a[:, q0 * 512:(q0 + 1) * 512], fin_b[:]),
                        reads=["fin_a", "fin_b"], writes=["fin_a"])
            em.op("dve", lambda e, t0=ti: e.tensor_add(fin_a[:], fin_a[:],
                  yslot[0:128, t0 * 2048:(t0 + 1) * 2048]),
                  reads=["fin_a", "yslot"], writes=["fin_a"])
            em.op("pool", lambda e, i0=it: e.dma_start(
                out=out[i0 * 128:(i0 + 1) * 128, :], in_=fin_a[:]),
                reads=["fin_a"], dma="outw")

    if debug_out:
        dbg_b = sbuf("dbg_b", (128, 512))
        for name, shape in debug_out:
            if name == "xn_hbm":
                for tt in range(NT):
                    for half in range(4):
                        em.op("pool", lambda e, t0=tt, h0=half: e.dma_start(
                            out=dbg_b[:, 0:512],
                            in_=xn_hbm[t0, :, h0 * 512:(h0 + 1) * 512]),
                            reads=["xn_hbm"], writes=["dbg_b"], dma="dbg")
                        em.op("pool", lambda e, t0=tt, h0=half: e.dma_start(
                            out=dbg["xn_hbm"][t0, :, h0 * 512:(h0 + 1) * 512],
                            in_=dbg_b[:, 0:512]),
                            reads=["dbg_b"], dma="dbg")

    sems = {}
    for e in ENGINES:
        h = nc.semaphore("sem_" + e)
        keep.append(h)
        sems["sem_" + e] = h.__enter__()
    for s in em.dma_cnt:
        h = nc.semaphore(s)
        keep.append(h)
        sems[s] = h.__enter__()

    em.build(sems)
    return nc


def prep_inputs(kw, t_pre=None, t_own=None):
    t_pre = T_PRE if t_pre is None else t_pre
    t_own = T_OWN if t_own is None else t_own
    hid = np.asarray(kw["L0_attn_hidden"], dtype=np.float32)
    shared = {
        "w_qkv": np.ascontiguousarray(kw["L0_attn_W_qkv"], dtype=np.float32),
        "w_z": np.ascontiguousarray(kw["L0_attn_W_z"], dtype=np.float32),
        "w_b": np.ascontiguousarray(kw["L0_attn_W_b"], dtype=np.float32),
        "w_a": np.ascontiguousarray(kw["L0_attn_W_a"], dtype=np.float32),
        "conv_w": np.ascontiguousarray(kw["L0_attn_conv_weight"], dtype=np.float32),
        "dt_bias": np.ascontiguousarray(kw["L0_attn_dt_bias"], dtype=np.float32),
        "a_log": np.ascontiguousarray(kw["L0_attn_A_log"], dtype=np.float32),
        "norm_w": np.ascontiguousarray(kw["L0_attn_norm_w"], dtype=np.float32),
        "w_out": np.ascontiguousarray(kw["L0_attn_W_out"], dtype=np.float32),
        "gate_wt": np.ascontiguousarray(np.asarray(kw["L0_ffn_gate_w"]).T, dtype=np.float32),
        "w1t": np.ascontiguousarray(np.transpose(np.asarray(kw["L0_ffn_W1"]), (0, 2, 1)), dtype=np.float32),
        "w2t": np.ascontiguousarray(np.transpose(np.asarray(kw["L0_ffn_W2"]), (0, 2, 1)), dtype=np.float32),
        "w3t": np.ascontiguousarray(np.transpose(np.asarray(kw["L0_ffn_W3"]), (0, 2, 1)), dtype=np.float32),
        "ws1t": np.ascontiguousarray(np.asarray(kw["L0_ffn_Ws1"]).T, dtype=np.float32),
        "ws2t": np.ascontiguousarray(np.asarray(kw["L0_ffn_Ws2"]).T, dtype=np.float32),
        "ws3t": np.ascontiguousarray(np.asarray(kw["L0_ffn_Ws3"]).T, dtype=np.float32),
    }
    in_maps = []
    n_q = T // t_own
    for c in range(N_CORES):
        b, q = divmod(c, n_q)
        start = q * t_own
        pre_real = hid[b, max(0, start - t_pre):start, :]
        pad = np.zeros((t_pre - pre_real.shape[0], D), dtype=np.float32)
        own = hid[b, start:start + t_own, :]
        m = dict(shared)
        m["hidden"] = np.ascontiguousarray(
            np.concatenate([pad, pre_real, own], axis=0))
        in_maps.append(m)
    return in_maps


def kernel(**kw):
    in_maps = prep_inputs(kw)
    nc = build_program()
    res = run_bass_kernel_spmd(nc, in_maps, core_ids=list(range(N_CORES)))
    parts = [np.asarray(r["out"]) for r in res.results]
    full = np.concatenate(parts, axis=0).reshape(B, T, D)
    return np.ascontiguousarray(full, dtype=np.float32)
